# version 21
# baseline (speedup 1.0000x reference)
import numpy as np
from contextlib import ExitStack

import concourse.bass as bass
import concourse.mybir as mybir
from concourse.bass_utils import run_bass_kernel_spmd

F32 = mybir.dt.float32
I32 = mybir.dt.int32
BF16 = mybir.dt.bfloat16
FP16 = mybir.dt.float16
AF = mybir.ActivationFunctionType
ALU = mybir.AluOpType
AX = mybir.AxisListType

B = 8
NTOK = 8192
D = 1024
DI = 3072
H = 16
NCH = 16
CHT = 512
NDC = D // 128
NJC = DI // 128
DEB = 32768.0

C_RB, C_ONE, C_65536, C_INV1, C_SC3, C_NB3, C_F3, C_F2, C_F4, C_F5, C_SC5, C_RBW = range(12)


AUTO_FLUSH = [True]

DBG = set()


def build_nc():
    nc = bass.Bass("TRN2", target_bir_lowering=False, debug=False)

    def scratch(name, shape, dt):
        kind = "ExternalOutput" if name in DBG else "Internal"
        return nc.dram_tensor(name, shape, dt, kind=kind).ap()

    xT = nc.dram_tensor("xT", [128, NCH * NDC * CHT], F32, kind="ExternalInput").ap()
    wqkv = nc.dram_tensor("wqkv", [D, 3 * D], F32, kind="ExternalInput").ap()
    wout = nc.dram_tensor("wout", [D, D], F32, kind="ExternalInput").ap()
    bout = nc.dram_tensor("bout", [D], F32, kind="ExternalInput").ap()
    wgate = nc.dram_tensor("wgate", [DI, D], F32, kind="ExternalInput").ap()
    wup = nc.dram_tensor("wup", [DI, D], F32, kind="ExternalInput").ap()
    wdown = nc.dram_tensor("wdown", [D, DI], F32, kind="ExternalInput").ap()
    outT = nc.dram_tensor("outT", [128, NCH * NDC * CHT], F32,
                          kind="ExternalOutput").ap()

    q_s = scratch("q_s", [128, NCH * NDC * CHT], BF16)
    xl2_s = scratch("xl2_s", [128, NCH * NDC * CHT], FP16)
    x1_s = scratch("x1_s", [128, NCH * NDC * CHT], F32)
    up_s = scratch("up_s", [128, NCH * NJC * CHT], FP16)
    gate_s = scratch("gate_s", [128, NCH * NJC * CHT], FP16)
    g2_s = scratch("g2_s", [128, NCH * NJC * CHT], BF16)
    o_s = scratch("o_s", [128, NCH * NJC * CHT], FP16)
    wupT_s = scratch("wupT_s", [128, NDC * DI], BF16)
    wgateT_s = scratch("wgateT_s", [128, NDC * DI], BF16)
    wdownT_s = scratch("wdownT_s", [128, NJC * D], BF16)
    cc_in = nc.dram_tensor("cc_in", [128, 4], F32)
    cc_out = nc.dram_tensor("cc_out", [128, 4], F32)

    C = {}

    top = ExitStack()
    with top:
        S = {}
        for n in ("ld", "st", "pe", "vec", "sc", "gp", "cc", "te"):
            S[n] = top.enter_context(nc.semaphore(n))
            C[n] = 0

        def I(instr, sem=None, n=1):
            if sem is not None:
                instr.then_inc(S[sem], n)
                C[sem] += n
            return instr

        def VW(v):
            v.wait_ge(S["vec"], C["vec"])

        def SW(s):
            s.wait_ge(S["sc"], C["sc"])

        class _InstrRec:
            def __init__(self, rec, eng, op, args, kwargs):
                self.rec, self.eng, self.op = rec, eng, op
                self.args, self.kwargs = args, kwargs
                self.incs = []

            def then_inc(self, sem, n=1):
                self.incs.append((sem, n))
                return self

            def emit(self, e):
                ins = getattr(e, self.op)(*self.args, **self.kwargs)
                for sem, n in self.incs:
                    ins.then_inc(sem, n)
                return ins

        class _EngProxy:
            def __init__(self, rec, name):
                self._rec, self._name = rec, name

            def wait_ge(self, sem, val):
                r = _InstrRec(self._rec, self._name, "wait_ge", (sem, val), {})
                self._rec.q[self._name].append(r)
                return r

            def __getattr__(self, op):
                def call(*a, **k):
                    r = _InstrRec(self._rec, self._name, op, a, k)
                    self._rec.q[self._name].append(r)
                    return r
                return call

        class Rec:
            ENGS = ("sync", "gpsimd", "scalar", "tensor", "vector")
            AUTO = AUTO_FLUSH

            def __init__(self, blk=None):
                self.q = {e: [] for e in self.ENGS}
                self.blk = blk

            def __getattr__(self, eng):
                if eng not in self.q:
                    raise AttributeError(eng)
                def run(fn):
                    fn(_EngProxy(self, eng))
                    if self.blk is not None and (Rec.AUTO[0] or eng != "scalar"):
                        self.flush_one(self.blk, eng)
                return run

            def flush_one(self, blk, eng):
                ths = self.q[eng]
                if not ths:
                    return
                self.q[eng] = []
                getattr(blk, eng)(lambda e, ths=ths: [t.emit(e) for t in ths])

            def flush(self, blk):
                for eng in self.ENGS:
                    self.flush_one(blk, eng)

        cst = top.enter_context(nc.sbuf_tensor("cst", [128, 16], F32))
        bout_sb = top.enter_context(nc.sbuf_tensor("bout_sb", [128, NDC], F32))
        ones_m = top.enter_context(nc.sbuf_tensor("ones_m", [1, 128], F32))
        ones_mb = top.enter_context(nc.sbuf_tensor("ones_mb", [1, 128], BF16))
        onesk = top.enter_context(nc.sbuf_tensor("onesk", [128, 1], F32))
        oneskb = top.enter_context(nc.sbuf_tensor("oneskb", [128, 1], BF16))
        idn = top.enter_context(nc.sbuf_tensor("idn", [128, 128], BF16))
        iot = top.enter_context(nc.sbuf_tensor("iot", [128, 128], F32))
        prow = top.enter_context(nc.sbuf_tensor("prow", [1, 8], F32))
        pri = top.enter_context(nc.sbuf_tensor("pri", [1, 4], I32))
        accs = {k: top.enter_context(nc.sbuf_tensor(f"acc{k}", [128, NCH], F32))
                for k in (1, 2, 3, 4, 5)}
        jtmp = top.enter_context(nc.sbuf_tensor("jtmp", [128, NJC], F32))
        fold = top.enter_context(nc.sbuf_tensor("fold", [128, 4], F32))
        brow = top.enter_context(nc.sbuf_tensor("brow", [1, 512], F32))
        srow = top.enter_context(nc.sbuf_tensor("srow", [1, 16], F32))
        kv_sb = top.enter_context(nc.sbuf_tensor("kv_sb", [128, (H // 2) * 128], BF16))
        kv_ks = top.enter_context(nc.sbuf_tensor("kv_ks", [128, H // 2], BF16))
        mrow = top.enter_context(nc.sbuf_tensor("mrow", [1, CHT], F32))
        m2row = top.enter_context(nc.sbuf_tensor("m2row", [1, CHT], F32))
        trow = top.enter_context(nc.sbuf_tensor("trow", [1, CHT], F32))
        vrowt = top.enter_context(nc.sbuf_tensor("vrowt", [1, CHT], F32))
        rrow = top.enter_context(nc.sbuf_tensor("rrow", [1, CHT], F32))
        zrow = top.enter_context(nc.sbuf_tensor("zrow", [1, CHT], F32))
        zrow2 = top.enter_context(nc.sbuf_tensor("zrow2", [1, CHT], F32))

        with ExitStack() as p0, nc.Block() as blk:
            R = Rec(blk)
            wraw = p0.enter_context(nc.sbuf_tensor("wraw", [128, DI], F32))
            tmpa = p0.enter_context(nc.sbuf_tensor("tmpa", [128, DI], F32))
            tree = p0.enter_context(nc.sbuf_tensor("tree", [128, DI], F32))
            rep1 = p0.enter_context(nc.sbuf_tensor("rep1", [128, DI], F32))
            rep2 = p0.enter_context(nc.sbuf_tensor("rep2", [128, DI], F32))
            sca = p0.enter_context(nc.sbuf_tensor("sca", [128, DI // 32], F32))
            inva = p0.enter_context(nc.sbuf_tensor("inva", [128, DI // 32], F32))
            ti32 = p0.enter_context(nc.sbuf_tensor("ti32", [128, DI], I32))
            wqn = p0.enter_context(nc.sbuf_tensor("wqn", [128, DI], BF16))
            wTf = p0.enter_context(nc.sbuf_tensor("wTf", [128, NDC * DI], BF16))
            pstr = p0.enter_context(nc.psum_tensor("pstr", [128, 512], BF16))
            psb0 = p0.enter_context(nc.psum_tensor("psb0", [128, 8], F32))

            def g_init(g):
                I(g.memset(cst[:, C_ONE:C_ONE + 1], 1.0), "gp")
                I(g.memset(cst[:, C_65536:C_65536 + 1], 65536.0), "gp")
                I(g.memset(ones_m[:, :], 1.0), "gp")
                I(g.memset(onesk[:, :], 1.0), "gp")
                I(g.memset(oneskb[:, :], 1.0), "gp")
                I(g.memset(prow[:, 0:8], 0.7), "gp")
                for k in accs:
                    I(g.memset(accs[k][:, :], 0.0), "gp")
                I(g.memset(fold[:, :], 0.0), "gp")
                I(g.iota(iot[:, :], [[1, 128]], channel_multiplier=-1,
                         allow_small_or_imprecise_dtypes=True), "gp")
            R.gpsimd(g_init)
            gp0 = C["gp"]

            def v_idn_f(v):
                v.wait_ge(S["gp"], gp0)
                I(v.tensor_scalar(out=idn[:, :], in0=iot[:, :], scalar1=0.0,
                                  scalar2=None, op0=ALU.is_equal), "vec")
            R.vector(v_idn_f)
            v_idn = C["vec"]

            def s_probe(s):
                s.wait_ge(S["gp"], gp0)
                I(s.activation(out=pri[:, :], in_=prow[0:1, 0:4], func=AF.Relu,
                               bias=fold[0:1, 0:1],
                               scale=cst[0:1, C_ONE:C_ONE + 1]), "sc")
            R.scalar(s_probe)
            sc_probe = C["sc"]

            def v_rb(v):
                v.wait_ge(S["sc"], sc_probe)
                I(v.tensor_scalar(out=prow[:, 1:2], in0=pri[0:1, 0:1],
                                  scalar1=-0.5, scalar2=32768.5,
                                  op0=ALU.mult, op1=ALU.add), "vec")
                I(v.tensor_scalar(out=prow[:, 2:3], in0=pri[0:1, 0:1],
                                  scalar1=-0.5, scalar2=8.5,
                                  op0=ALU.mult, op1=ALU.add), "vec")
            R.vector(v_rb)
            v_rbc = C["vec"]

            def t_rb(t):
                t.wait_ge(S["vec"], v_rbc)
                I(t.matmul(psb0[:, 0:2], ones_m[:, :], prow[:, 1:3],
                           start=True, stop=True), "pe")
            R.tensor(t_rb)
            pe_rb = C["pe"]

            def s_rb(s):
                s.wait_ge(S["pe"], pe_rb)
                I(s.copy(out=cst[:, C_RB:C_RB + 1], in_=psb0[:, 0:1]), "sc")
                I(s.copy(out=cst[:, C_RBW:C_RBW + 1], in_=psb0[:, 1:2]), "sc")
            R.scalar(s_rb)
            sc_rb = C["sc"]

            def sy_bout(sy):
                I(sy.dma_start(out=bout_sb[:, :],
                               in_=bout.rearrange("(a p) -> p a", p=128),
                               allow_slow_non_contiguous=True), "ld", 16)
            R.sync(sy_bout)

            wr_rd = [0]
            tm_rd = [0]
            wq_rd = [0]
            ev_done = [0]
            for wdram, O, Iw, wdst in ((wup, DI, D, wupT_s),
                                       (wgate, DI, D, wgateT_s),
                                       (wdown, D, DI, wdownT_s)):
                nic = Iw // 128
                nb = Iw // 32
                for ot in range(O // 128):
                    R.flush(blk)
                    def sy_w(sy, wdram=wdram, Iw=Iw, ot=ot, wrv=wr_rd[0]):
                        sy.wait_ge(S["vec"], wrv)
                        I(sy.dma_start(
                            out=wraw[:, 0:Iw],
                            in_=wdram.rearrange("(a p) c -> a p c", p=128)[ot]),
                          "ld", 16)
                    R.sync(sy_w)
                    ld_w = C["ld"]

                    def v_abs(v, Iw=Iw, ld_w=ld_w, tmv=tm_rd[0]):
                        v.wait_ge(S["ld"], ld_w)
                        v.wait_ge(S["sc"], tmv)
                        I(v.scalar_tensor_tensor(out=tmpa[:, 0:Iw],
                                                 in0=wraw[:, 0:Iw], scalar=-1.0,
                                                 in1=wraw[:, 0:Iw],
                                                 op0=ALU.mult, op1=ALU.max), "vec")
                    R.vector(v_abs)
                    v_absc = C["vec"]

                    def v_dq(v, Iw=Iw, nb=nb, sc_abs=v_absc):
                        v.wait_ge(S["vec"], sc_abs)
                        VW(v)
                        src = tmpa[:, 0:Iw].rearrange("p (n two) -> p n two", two=2)
                        w = Iw // 2
                        I(v.tensor_tensor(out=tree[:, 0:w], in0=src[:, :, 0],
                                          in1=src[:, :, 1], op=ALU.max), "vec")
                        off = 0
                        while w > nb:
                            VW(v)
                            s2 = tree[:, off:off + w].rearrange(
                                "p (n two) -> p n two", two=2)
                            I(v.tensor_tensor(out=tree[:, off + w:off + w + w // 2],
                                              in0=s2[:, :, 0], in1=s2[:, :, 1],
                                              op=ALU.max), "vec")
                            off += w
                            w //= 2
                        VW(v)
                        I(v.tensor_scalar(out=sca[:, 0:nb], in0=tree[:, off:off + nb],
                                          scalar1=1.0 / 7.0, scalar2=1e-12,
                                          op0=ALU.mult, op1=ALU.max), "vec")
                        VW(v)
                        I(v.reciprocal(out=inva[:, 0:nb], in_=sca[:, 0:nb]), "vec")
                        VW(v)
                        for dst, srct in ((rep1, inva), (rep2, sca)):
                            dv = dst[:, 0:Iw].rearrange("p (n k) -> p n k", k=32)
                            I(v.tensor_copy(
                                out=dv[:, :, 0:1],
                                in_=srct[:, 0:nb].rearrange("p (n o) -> p n o", o=1)), "vec")
                            wd = 1
                            while wd < 32:
                                take = min(wd, 32 - wd)
                                VW(v)
                                I(v.tensor_copy(out=dv[:, :, wd:wd + take],
                                                in_=dv[:, :, 0:take]), "vec")
                                wd += take
                            VW(v)
                        I(v.tensor_tensor(out=tmpa[:, 0:Iw], in0=wraw[:, 0:Iw],
                                          in1=rep1[:, 0:Iw], op=ALU.mult), "vec")
                    R.vector(v_dq)
                    wr_rd[0] = C["vec"]
                    v_t = C["vec"]

                    def s_conv(s, Iw=Iw, v_t=v_t):
                        s.wait_ge(S["vec"], v_t)
                        s.wait_ge(S["sc"], sc_rb)
                        I(s.activation(out=ti32[:, 0:Iw], in_=tmpa[:, 0:Iw],
                                       func=AF.Relu, bias=cst[:, C_RBW:C_RBW + 1],
                                       scale=cst[:, C_ONE:C_ONE + 1]), "sc")
                    R.scalar(s_conv)
                    sc_conv = C["sc"]
                    tm_rd[0] = C["sc"]

                    def v_wq(v, Iw=Iw, sc_conv=sc_conv, wqv=wq_rd[0]):
                        v.wait_ge(S["sc"], sc_conv)
                        v.wait_ge(S["pe"], wqv)
                        I(v.scalar_tensor_tensor(out=wqn[:, 0:Iw], in0=ti32[:, 0:Iw],
                                                 scalar=8.0, in1=rep2[:, 0:Iw],
                                                 op0=ALU.subtract, op1=ALU.mult),
                          "vec")
                    R.vector(v_wq)
                    v_wqc = C["vec"]

                    for g0 in range(0, nic, 4):
                        gk = min(4, nic - g0)

                        def t_tr(t, g0=g0, gk=gk, v_wqc=v_wqc, evd=ev_done[0]):
                            t.wait_ge(S["vec"], max(v_wqc, v_idn))
                            t.wait_ge(S["sc"], evd)
                            for j in range(gk):
                                ic = g0 + j
                                I(t.transpose(pstr[:, j * 128:j * 128 + 128],
                                              wqn[:, ic * 128:ic * 128 + 128],
                                              idn[:, :]), "pe")
                        R.tensor(t_tr)
                        pe_tr = C["pe"]
                        wq_rd[0] = C["pe"]

                        def s_ev(s, g0=g0, gk=gk, ot=ot, O=O, pe_tr=pe_tr):
                            s.wait_ge(S["pe"], pe_tr)
                            dst = wTf.ap().rearrange("p (a j) -> p a j", j=O)[
                                :, g0:g0 + gk, ot * 128:ot * 128 + 128]
                            I(s.copy(out=dst,
                                     in_=pstr[:, 0:gk * 128].rearrange(
                                         "p (a j) -> p a j", j=128)), "sc")
                        R.scalar(s_ev)
                        ev_done[0] = C["sc"]

                def g_st(g, wdst=wdst, nic=nic, O=O, scv=ev_done[0]):
                    g.wait_ge(S["sc"], scv)
                    I(g.dma_start(out=wdst[:, 0:nic * O], in_=wTf[:, 0:nic * O]),
                      "st", 16)
                R.gpsimd(g_st)
                st_w = C["st"]

                def s_gate(s, st_w=st_w):
                    s.wait_ge(S["st"], st_w)
                R.scalar(s_gate)

            R.flush(blk)

        stack_wq = ExitStack()
        wqkv_sb = stack_wq.enter_context(
            nc.sbuf_tensor("wqkv_sb", [128, NDC * 3 * D], BF16))
        stack_wo = ExitStack()
        wout_sb = stack_wo.enter_context(
            nc.sbuf_tensor("wout_sb", [128, NDC * D], BF16))
        with ExitStack() as p0b, nc.Block() as blk:
            R = Rec(blk)
            wcst = [p0b.enter_context(nc.sbuf_tensor(f"wcst{j}", [128, 3 * D], F32))
                    for j in range(2)]
            casts = []
            for idx in range(2 * NDC):
                R.flush(blk)
                is_o = idx >= NDC
                ic = idx - NDC if is_o else idx
                w = D if is_o else 3 * D
                src = (wout if is_o else wqkv).rearrange("(a p) c -> a p c", p=128)[ic]

                def sy_c(sy, idx=idx, src=src, w=w):
                    if idx >= 2:
                        sy.wait_ge(S["vec"], casts[idx - 2])
                    I(sy.dma_start(out=wcst[idx % 2][:, 0:w], in_=src), "ld", 16)
                R.sync(sy_c)
                ld_c = C["ld"]

                def v_c(v, idx=idx, is_o=is_o, ic=ic, w=w, ld_c=ld_c):
                    v.wait_ge(S["ld"], ld_c)
                    dst = (wout_sb[:, ic * D:(ic + 1) * D] if is_o
                           else wqkv_sb[:, ic * 3 * D:(ic + 1) * 3 * D])
                    I(v.tensor_copy(out=dst, in_=wcst[idx % 2][:, 0:w]), "vec")
                    casts.append(C["vec"])
                R.vector(v_c)

            R.flush(blk)

        with ExitStack() as pa, nc.Block() as blk:
            R = Rec(blk)
            xblk = [pa.enter_context(nc.sbuf_tensor(f"xblk{j}", [128, NDC * CHT], F32))
                    for j in range(2)]
            sqblk = pa.enter_context(nc.sbuf_tensor("sqblk", [128, NDC * CHT], F32))
            tblk = sqblk
            xl1 = pa.enter_context(nc.sbuf_tensor("xl1", [128, NDC * CHT], BF16))
            k_sb = pa.enter_context(nc.sbuf_tensor("k_sb", [128, 4 * D], BF16))
            v_sb = pa.enter_context(nc.sbuf_tensor("v_sb", [128, 4 * D], BF16))
            qev = [pa.enter_context(nc.sbuf_tensor(f"qev{j}", [128, NDC * CHT], BF16))
                   for j in range(2)]
            mBs = pa.enter_context(nc.sbuf_tensor("mBs", [128, CHT], F32))
            rBs = pa.enter_context(nc.sbuf_tensor("rBs", [128, CHT], F32))
            psA = pa.enter_context(nc.psum_tensor("psA", [128, CHT], F32))
            psB = pa.enter_context(nc.psum_tensor("psB", [128, CHT], F32))
            psP = [pa.enter_context(nc.psum_tensor(f"psP{j}", [128, CHT], F32))
                   for j in range(2)]
            kvP = [pa.enter_context(nc.psum_tensor(f"kvP{j}", [128, 512], F32))
                   for j in range(2)]
            kvS2 = pa.enter_context(nc.psum_tensor("kvS2", [128, 8], F32))

            sn = []
            for i in range(NCH):
                bi = i % 2
                s_now = {}

                def sy_x(sy, i=i, bi=bi):
                    if i >= 2:
                        sy.wait_ge(S["vec"], sn[i - 2]["x_rd_v"])
                        sy.wait_ge(S["pe"], sn[i - 2]["x_rd_pe"])
                    I(sy.dma_start(out=xblk[bi][:, :],
                                   in_=xT[:, i * NDC * CHT:(i + 1) * NDC * CHT]),
                      "ld", 16)
                R.sync(sy_x)
                ld_x = C["ld"]

                def s_sq(s, i=i, bi=bi, ld_x=ld_x):
                    s.wait_ge(S["ld"], ld_x)
                    if i >= 1:
                        s.wait_ge(S["pe"], sn[i - 1]["stats"])
                        s.wait_ge(S["vec"], sn[i - 1]["ln"])
                    I(s.activation(out=sqblk[:, :], in_=xblk[bi][:, :],
                                   func=AF.Square, bias=0.0, scale=1.0), "sc")
                R.scalar(s_sq)
                sc_sq = C["sc"]

                def t_st(t, i=i, bi=bi, sc_sq=sc_sq):
                    t.wait_ge(S["sc"], sc_sq)
                    if i >= 1:
                        t.wait_ge(S["vec"], sn[i - 1]["bce"])
                    for a in range(NDC):
                        I(t.matmul(psA[0:1, :], onesk[:, :],
                                   xblk[bi][:, a * CHT:(a + 1) * CHT],
                                   start=(a == 0), stop=(a == NDC - 1)), "pe")
                    for a in range(NDC):
                        I(t.matmul(psB[0:1, :], onesk[:, :],
                                   sqblk[:, a * CHT:(a + 1) * CHT],
                                   start=(a == 0), stop=(a == NDC - 1)), "pe")
                R.tensor(t_st)
                s_now["stats"] = C["pe"]
                pe_stats = C["pe"]

                def v_rm(v, pe_stats=pe_stats):
                    v.wait_ge(S["pe"], pe_stats)
                    I(v.tensor_scalar(out=mrow[:, :], in0=psA[0:1, :],
                                      scalar1=1.0 / D, scalar2=None,
                                      op0=ALU.mult), "vec")
                    I(v.tensor_scalar(out=m2row[:, :], in0=psB[0:1, :],
                                      scalar1=1.0 / D, scalar2=None,
                                      op0=ALU.mult), "vec")
                R.vector(v_rm)
                s_now["rm1"] = C["vec"]

                def v_rm2(v):
                    VW(v)
                    I(v.tensor_tensor(out=trow[:, :], in0=mrow[:, :],
                                      in1=mrow[:, :], op=ALU.mult), "vec")
                    VW(v)
                    I(v.tensor_tensor(out=vrowt[:, :], in0=m2row[:, :],
                                      in1=trow[:, :], op=ALU.subtract), "vec")
                    VW(v)
                    I(v.tensor_scalar(out=trow[:, :], in0=vrowt[:, :],
                                      scalar1=1e-5, scalar2=None, op0=ALU.add),
                      "vec")
                    VW(v)
                    I(v.reciprocal(out=vrowt[:, :], in_=trow[:, :]), "vec")
                R.vector(v_rm2)
                v_rm2c = C["vec"]

                def s_rstd(s, v_rm2c=v_rm2c):
                    s.wait_ge(S["vec"], v_rm2c)
                    I(s.activation(out=rrow[:, :], in_=vrowt[:, :], func=AF.Sqrt,
                                   bias=0.0, scale=1.0), "sc")
                R.scalar(s_rstd)
                sc_rstd = C["sc"]

                def t_bc(t, i=i, sc_rstd=sc_rstd):
                    t.wait_ge(S["sc"], sc_rstd)
                    if i >= 1:
                        t.wait_ge(S["vec"], sn[i - 1]["bce"])
                    I(t.matmul(psA[:, :], ones_m[:, :], mrow[:, :],
                               start=True, stop=True), "pe")
                    I(t.matmul(psB[:, :], ones_m[:, :], rrow[:, :],
                               start=True, stop=True), "pe")
                R.tensor(t_bc)
                pe_bc = C["pe"]

                def v_bce(v, pe_bc=pe_bc):
                    v.wait_ge(S["pe"], pe_bc)
                    I(v.tensor_copy(out=mBs[:, :], in_=psA[:, :]), "vec")
                    I(v.tensor_copy(out=rBs[:, :], in_=psB[:, :]), "vec")
                R.vector(v_bce)
                s_now["bce"] = C["vec"]

                def v_ln(v, i=i, bi=bi):
                    if i >= 1:
                        v.wait_ge(S["pe"], sn[i - 1]["qkv"])
                    VW(v)
                    for a in range(NDC):
                        sl = slice(a * CHT, (a + 1) * CHT)
                        I(v.tensor_tensor(out=tblk[:, sl], in0=xblk[bi][:, sl],
                                          in1=mBs[:, :], op=ALU.subtract), "vec")
                        VW(v)
                        I(v.tensor_tensor(out=xl1[:, sl], in0=tblk[:, sl],
                                          in1=rBs[:, :], op=ALU.mult), "vec")
                R.vector(v_ln)
                s_now["ln"] = C["vec"]
                s_now["x_rd_v"] = C["vec"]
                v_lnc = C["vec"]

                if i == 0:
                    qe_prev = {0: ("sc", 0), 1: ("sc", 0)}
                q_ev = []
                for dot in range(NDC):
                    def t_q(t, i=i, dot=dot, v_lnc=v_lnc, qp=None):
                        t.wait_ge(S["vec"], v_lnc)
                        if i >= 1 or dot >= 2:
                            t.wait_ge(S[qe_prev[dot % 2][0]], qe_prev[dot % 2][1])
                        for ic in range(NDC):
                            I(t.matmul(psP[dot % 2][:, :],
                                       wqkv_sb[:, ic * 3 * D + dot * 128:
                                               ic * 3 * D + dot * 128 + 128],
                                       xl1[:, ic * CHT:(ic + 1) * CHT],
                                       start=(ic == 0), stop=(ic == NDC - 1)),
                              "pe")
                    R.tensor(t_q)
                    pe_q = C["pe"]

                    def v_qe(v, bi=bi, dot=dot, pe_q=pe_q, i=i):
                        v.wait_ge(S["pe"], pe_q)
                        if i >= 1:
                            v.wait_ge(S["st"], sn[i - 1]["qst"])
                        I(v.tensor_scalar(out=qev[bi][:, dot * CHT:(dot + 1) * CHT],
                                          in0=psP[dot % 2][:, :], scalar1=0.0,
                                          scalar2=None, op0=ALU.max), "vec")
                    R.vector(v_qe)
                    q_ev.append(("vec", C["vec"]))
                    qe_prev[dot % 2] = ("vec", C["vec"])

                for g in range(4):
                    for ct in range(4):
                        def t_kv(t, i=i, g=g, ct=ct, v_lnc=v_lnc):
                            t.wait_ge(S["vec"], v_lnc)
                            t.wait_ge(S[qe_prev[ct % 2][0]], qe_prev[ct % 2][1])
                            for ic in range(NDC):
                                I(t.matmul(
                                    psP[ct % 2][:, :],
                                    xl1[:, ic * CHT + g * 128:
                                        ic * CHT + g * 128 + 128],
                                    wqkv_sb[:, ic * 3 * D + D + ct * CHT:
                                            ic * 3 * D + D + (ct + 1) * CHT],
                                    start=(ic == 0), stop=(ic == NDC - 1)), "pe")
                        R.tensor(t_kv)
                        pe_kv = C["pe"]

                        def v_kve(v, g=g, ct=ct, pe_kv=pe_kv):
                            v.wait_ge(S["pe"], pe_kv)
                            if ct < 2:
                                I(v.tensor_scalar(
                                    out=k_sb[:, g * D + ct * CHT:
                                             g * D + (ct + 1) * CHT],
                                    in0=psP[ct % 2][:, :], scalar1=0.0,
                                    scalar2=None, op0=ALU.max), "vec")
                            else:
                                I(v.tensor_copy(
                                    out=v_sb[:, g * D + (ct - 2) * CHT:
                                             g * D + (ct - 1) * CHT],
                                    in_=psP[ct % 2][:, :]), "vec")
                        R.vector(v_kve)
                        qe_prev[ct % 2] = ("vec", C["vec"])
                    sc_kve = C["vec"]

                    def t_acc(t, i=i, g=g, sc_kve=sc_kve):
                        t.wait_ge(S["vec"], sc_kve)
                        first = (i == 0 and g == 0)
                        last = (i == NCH - 1 and g == 3)
                        for a in range(NDC):
                            I(t.matmul(kvP[a // 4][:, (a % 4) * 128:
                                                   (a % 4) * 128 + 128],
                                       k_sb[:, g * D + a * 128:g * D + (a + 1) * 128],
                                       v_sb[:, g * D + a * 128:g * D + (a + 1) * 128],
                                       start=first, stop=last), "pe")
                            I(t.matmul(kvS2[:, a:a + 1],
                                       k_sb[:, g * D + a * 128:g * D + (a + 1) * 128],
                                       oneskb[:, :],
                                       start=first, stop=last), "pe")
                    R.tensor(t_acc)
                    kv_rd = [C["pe"]]
                s_now["qkv"] = C["pe"]
                s_now["x_rd_pe"] = s_now["stats"]

                def g_qst(g, i=i, bi=bi, qlast=q_ev[-1]):
                    g.wait_ge(S[qlast[0]], qlast[1])
                    I(g.dma_start(out=q_s[:, i * NDC * CHT:(i + 1) * NDC * CHT],
                                  in_=qev[bi][:, :]), "st", 16)
                R.gpsimd(g_qst)
                s_now["qst"] = C["st"]
                sn.append(s_now)
                R.flush(blk)

            pe_all = C["pe"]

            def g_kvz(g, pe_all=pe_all):
                g.wait_ge(S["pe"], pe_all)
                I(g.memset(kv_sb[:, :], 0.0), "gp")
            R.gpsimd(g_kvz)
            gp_kvz = C["gp"]

            def s_kvout(s, pe_all=pe_all, gp_kvz=gp_kvz):
                s.wait_ge(S["gp"], gp_kvz)
                I(s.copy(out=kv_sb.ap().rearrange("p (h c) -> p h c", c=128)[0:64, 0:4, 0:64],
                         in_=bass.AP(kvP[0].ap().tensor, 0,
                                     [[512, 64], [128, 4], [1, 64]])), "sc")
                I(s.copy(out=kv_sb.ap().rearrange("p (h c) -> p h c", c=128)[0:64, 4:8, 0:64],
                         in_=bass.AP(kvP[1].ap().tensor, 0,
                                     [[512, 64], [128, 4], [1, 64]])), "sc")
                I(s.copy(out=kv_sb.ap().rearrange("p (h c) -> p h c", c=128)[64:128, 0:4, 64:128],
                         in_=bass.AP(kvP[0].ap().tensor, 64 * 512 + 64,
                                     [[512, 64], [128, 4], [1, 64]])), "sc")
                I(s.copy(out=kv_sb.ap().rearrange("p (h c) -> p h c", c=128)[64:128, 4:8, 64:128],
                         in_=bass.AP(kvP[1].ap().tensor, 64 * 512 + 64,
                                     [[512, 64], [128, 4], [1, 64]])), "sc")
                I(s.activation(
                    out=kv_ks.ap().rearrange("p (h o) -> p h o", o=1),
                    in_=kvS2[:, :].rearrange("p (h o) -> p h o", o=1),
                    func=AF.Copy, bias=1e-6, scale=1.0), "sc")
            R.scalar(s_kvout)
            a1_done = dict(C)
            R.flush(blk)

        with ExitStack() as pa2, nc.Block() as blk:
            R = Rec(blk)
            qld = [pa2.enter_context(nc.sbuf_tensor(f"qld{j}", [128, NDC * CHT], BF16))
                   for j in range(2)]
            xblk = [pa2.enter_context(nc.sbuf_tensor(f"x2blk{j}", [128, NDC * CHT], F32))
                    for j in range(2)]
            oaT = pa2.enter_context(nc.sbuf_tensor("oaT", [128, NDC * CHT], BF16))
            x1blk = pa2.enter_context(nc.sbuf_tensor("x1blk", [128, NDC * CHT], F32))
            sqblk = pa2.enter_context(nc.sbuf_tensor("sq2blk", [128, NDC * CHT], F32))
            tblk = sqblk
            xl2 = [pa2.enter_context(nc.sbuf_tensor(f"xl2b{j}", [128, NDC * CHT], FP16))
                   for j in range(2)]
            psO = [pa2.enter_context(nc.psum_tensor(f"psO{j}", [128, CHT], F32))
                   for j in range(2)]
            psZd = pa2.enter_context(nc.psum_tensor("psZd", [16, CHT], F32))
            psZB = pa2.enter_context(nc.psum_tensor("psZB", [128, CHT], F32))
            zdr = pa2.enter_context(nc.sbuf_tensor("zdr", [16, CHT], F32))
            zdr2 = pa2.enter_context(nc.sbuf_tensor("zdr2", [16, CHT], F32))
            zrS = pa2.enter_context(nc.sbuf_tensor("zrS", [16, CHT], BF16))
            zbsv = pa2.enter_context(nc.sbuf_tensor("zbsv", [128, CHT], F32))
            Z16 = pa2.enter_context(nc.sbuf_tensor("Z16", [128, 128], BF16))
            Sall = pa2.enter_context(nc.sbuf_tensor("Sall", [16, 8 * 128], BF16))
            psY = [pa2.enter_context(nc.psum_tensor(f"psY{j}", [128, CHT], F32))
                   for j in range(2)]
            psA = pa2.enter_context(nc.psum_tensor("ps2A", [128, CHT], F32))
            psB = pa2.enter_context(nc.psum_tensor("ps2B", [128, CHT], F32))
            mBs = pa2.enter_context(nc.sbuf_tensor("mBs2", [128, CHT], F32))
            rBs = pa2.enter_context(nc.sbuf_tensor("rBs2", [128, CHT], F32))

            Sit = pa2.enter_context(nc.sbuf_tensor("Sit", [16, 8 * 128], F32))

            def g_sel(g):
                I(g.memset(Z16[:, :], 0.0), "gp")
                I(g.iota(Sit.ap().rearrange("k (c w) -> k c w", w=64),
                         [[1, 16], [0, 64]], channel_multiplier=-1,
                         allow_small_or_imprecise_dtypes=True), "gp")
            R.gpsimd(g_sel)
            gp_sel = C["gp"]

            def v_sel(v):
                v.wait_ge(S["gp"], gp_sel)
                I(v.tensor_scalar(out=Sall[:, :], in0=Sit[:, :], scalar1=0.0,
                                  scalar2=None, op0=ALU.is_equal), "vec")
            R.vector(v_sel)

            def s_z16(s):
                s.wait_ge(S["gp"], gp_sel)
                s.wait_ge(S["sc"], a1_done["sc"])
                for h in range(H):
                    a, pb = h // 2, (h % 2) * 64
                    I(s.copy(out=Z16[pb:pb + 64, a * 16 + h:a * 16 + h + 1],
                             in_=kv_ks[pb:pb + 64, a:a + 1]), "sc")
            R.scalar(s_z16)
            sc_z16 = C["sc"]

            sn = []
            zev_prev = [0]
            zbc_last = [0]
            for i in range(NCH):
                bi = i % 2
                s_now = {}

                def sy_q(sy, i=i, bi=bi):
                    if i >= 2:
                        sy.wait_ge(S["pe"], sn[i - 2]["q_rd"])
                        sy.wait_ge(S["vec"], sn[i - 2]["x_rd"])
                    else:
                        sy.wait_ge(S["sc"], a1_done["sc"])
                        sy.wait_ge(S["st"], a1_done["st"])
                        sy.wait_ge(S["vec"], a1_done["vec"])
                    I(sy.dma_start(out=qld[bi][:, :],
                                   in_=q_s[:, i * NDC * CHT:(i + 1) * NDC * CHT]),
                      "ld", 16)
                    I(sy.dma_start(out=xblk[bi][:, :],
                                   in_=xT[:, i * NDC * CHT:(i + 1) * NDC * CHT]),
                      "ld", 16)
                R.sync(sy_q)
                ld_q = C["ld"]

                def t_zd(t, i=i, bi=bi, ld_q=ld_q, zp=zev_prev[0]):
                    t.wait_ge(S["ld"], ld_q)
                    t.wait_ge(S["sc"], sc_z16)
                    if i >= 1:
                        t.wait_ge(S["vec"], zp)
                    for a in range(NDC):
                        I(t.matmul(psZd[:, :], Z16[:, a * 16:(a + 1) * 16],
                                   qld[bi][:, a * CHT:(a + 1) * CHT],
                                   start=(a == 0), stop=(a == NDC - 1)), "pe")
                R.tensor(t_zd)
                pe_zd = C["pe"]

                def v_zd(v, pe_zd=pe_zd, zb=zbc_last[0], i=i):
                    v.wait_ge(S["pe"], pe_zd)
                    if i >= 1:
                        v.wait_ge(S["pe"], zb)
                    I(v.tensor_scalar(out=zdr[:, :], in0=psZd[:, :],
                                      scalar1=1e-6, scalar2=None,
                                      op0=ALU.add), "vec")
                    VW(v)
                    I(v.reciprocal(out=zdr2[:, :], in_=zdr[:, :]), "vec")
                    VW(v)
                    I(v.tensor_copy(out=zrS[:, :], in_=zdr2[:, :]), "vec")
                R.vector(v_zd)
                zev_prev[0] = C["vec"]
                v_zdc = C["vec"]

                oa_done = []
                vzc_prev = [0]
                for a in range(NDC):
                    def t_o(t, i=i, bi=bi, a=a, ld_q=ld_q):
                        t.wait_ge(S["ld"], ld_q)
                        if a >= 2:
                            t.wait_ge(S["vec"], oa_done[a - 2])
                        elif i >= 1:
                            t.wait_ge(S["vec"], sn[i - 1]["oa_last2"][a % 2])
                        I(t.matmul(psO[a % 2][:, :],
                                   kv_sb[:, a * 128:(a + 1) * 128],
                                   qld[bi][:, a * CHT:(a + 1) * CHT],
                                   start=True, stop=True), "pe")
                    R.tensor(t_o)
                    pe_o = C["pe"]

                    def t_zbc(t, a=a, i=i, v_zdc=v_zdc, vz=vzc_prev[0]):
                        t.wait_ge(S["vec"], v_zdc)
                        if a >= 1 or i >= 1:
                            t.wait_ge(S["vec"], vz)
                        I(t.matmul(psZB[:, :], Sall[:, a * 128:(a + 1) * 128],
                                   zrS[:, :], start=True, stop=True), "pe")
                    R.tensor(t_zbc)
                    pe_zbc = C["pe"]
                    zbc_last[0] = pe_zbc

                    def v_zc(v, pe_zbc=pe_zbc):
                        v.wait_ge(S["pe"], pe_zbc)
                        I(v.tensor_copy(out=zbsv[:, :], in_=psZB[:, :]), "vec")
                    R.vector(v_zc)
                    vzc_prev[0] = C["vec"]

                    def v_oa(v, a=a, i=i, pe_o=pe_o):
                        v.wait_ge(S["pe"], pe_o)
                        VW(v)
                        if i >= 1 and a < 2:
                            v.wait_ge(S["pe"], sn[i - 1]["wout"])
                        I(v.tensor_tensor(
                            out=oaT[:, a * CHT:(a + 1) * CHT],
                            in0=psO[a % 2][:, :], in1=zbsv[:, :],
                            op=ALU.mult), "vec")
                    R.vector(v_oa)
                    oa_done.append(C["vec"])
                s_now["q_rd"] = C["pe"]
                s_now["oa_last2"] = {0: oa_done[-2], 1: oa_done[-1]}
                v_oal = C["vec"]

                x1_parts = []
                for dot in range(NDC):
                    def t_y(t, dot=dot, v_oal=v_oal, i=i):
                        t.wait_ge(S["vec"], v_oal)
                        if dot >= 2:
                            t.wait_ge(S["vec"], x1_parts[dot - 2])
                        elif i >= 1:
                            t.wait_ge(S["vec"], sn[i - 1]["x1_last2"][dot])
                        for ic in range(NDC):
                            I(t.matmul(psY[dot % 2][:, :],
                                       wout_sb[:, ic * D + dot * 128:
                                               ic * D + dot * 128 + 128],
                                       oaT[:, ic * CHT:(ic + 1) * CHT],
                                       start=(ic == 0), stop=(ic == NDC - 1)),
                              "pe")
                    R.tensor(t_y)
                    pe_y = C["pe"]

                    def v_x1(v, i=i, dot=dot, bi=bi, pe_y=pe_y):
                        v.wait_ge(S["pe"], pe_y)
                        if dot == 0 and i >= 1:
                            v.wait_ge(S["st"], sn[i - 1]["xl2st"])
                        I(v.scalar_tensor_tensor(
                            out=x1blk[:, dot * CHT:(dot + 1) * CHT],
                            in0=psY[dot % 2][:, :],
                            scalar=bout_sb[:, dot:dot + 1],
                            in1=xblk[bi][:, dot * CHT:(dot + 1) * CHT],
                            op0=ALU.add, op1=ALU.add), "vec")
                    R.vector(v_x1)
                    x1_parts.append(C["vec"])
                s_now["wout"] = C["pe"]
                s_now["x1_last2"] = {0: x1_parts[-2], 1: x1_parts[-1]}
                s_now["x_rd"] = C["vec"]
                v_x1c = C["vec"]

                def s_sq2(s, v_x1c=v_x1c, i=i):
                    s.wait_ge(S["vec"], v_x1c)
                    if i >= 1:
                        s.wait_ge(S["pe"], sn[i - 1]["stats2"])
                        s.wait_ge(S["vec"], sn[i - 1]["ln2"])
                    I(s.activation(out=sqblk[:, :], in_=x1blk[:, :],
                                   func=AF.Square, bias=0.0, scale=1.0), "sc")
                R.scalar(s_sq2)
                sc_sq2 = C["sc"]

                def t_st2(t, sc_sq2=sc_sq2, i=i):
                    t.wait_ge(S["sc"], sc_sq2)
                    if i >= 1:
                        t.wait_ge(S["vec"], sn[i - 1]["bce2"])
                    for a in range(NDC):
                        I(t.matmul(psA[0:1, :], onesk[:, :],
                                   x1blk[:, a * CHT:(a + 1) * CHT],
                                   start=(a == 0), stop=(a == NDC - 1)), "pe")
                    for a in range(NDC):
                        I(t.matmul(psB[0:1, :], onesk[:, :],
                                   sqblk[:, a * CHT:(a + 1) * CHT],
                                   start=(a == 0), stop=(a == NDC - 1)), "pe")
                R.tensor(t_st2)
                s_now["stats2"] = C["pe"]
                pe_st2 = C["pe"]

                def v_rm(v, pe_st2=pe_st2):
                    v.wait_ge(S["pe"], pe_st2)
                    I(v.tensor_scalar(out=mrow[:, :], in0=psA[0:1, :],
                                      scalar1=1.0 / D, scalar2=None,
                                      op0=ALU.mult), "vec")
                    I(v.tensor_scalar(out=m2row[:, :], in0=psB[0:1, :],
                                      scalar1=1.0 / D, scalar2=None,
                                      op0=ALU.mult), "vec")
                R.vector(v_rm)
                s_now["rm2"] = C["vec"]

                def v_rm2(v):
                    VW(v)
                    I(v.tensor_tensor(out=trow[:, :], in0=mrow[:, :],
                                      in1=mrow[:, :], op=ALU.mult), "vec")
                    VW(v)
                    I(v.tensor_tensor(out=vrowt[:, :], in0=m2row[:, :],
                                      in1=trow[:, :], op=ALU.subtract), "vec")
                    VW(v)
                    I(v.tensor_scalar(out=trow[:, :], in0=vrowt[:, :],
                                      scalar1=1e-5, scalar2=None, op0=ALU.add),
                      "vec")
                    VW(v)
                    I(v.reciprocal(out=vrowt[:, :], in_=trow[:, :]), "vec")
                R.vector(v_rm2)
                v_rm2c = C["vec"]

                def s_rstd(s, v_rm2c=v_rm2c):
                    s.wait_ge(S["vec"], v_rm2c)
                    I(s.activation(out=rrow[:, :], in_=vrowt[:, :], func=AF.Sqrt,
                                   bias=0.0, scale=1.0), "sc")
                R.scalar(s_rstd)
                sc_rstd = C["sc"]

                def t_bc2(t, sc_rstd=sc_rstd, i=i):
                    t.wait_ge(S["sc"], sc_rstd)
                    if i >= 1:
                        t.wait_ge(S["vec"], sn[i - 1]["bce2"])
                    I(t.matmul(psA[:, :], ones_m[:, :], mrow[:, :],
                               start=True, stop=True), "pe")
                    I(t.matmul(psB[:, :], ones_m[:, :], rrow[:, :],
                               start=True, stop=True), "pe")
                R.tensor(t_bc2)
                pe_bc2 = C["pe"]

                def v_bce2(v, pe_bc2=pe_bc2):
                    v.wait_ge(S["pe"], pe_bc2)
                    I(v.tensor_copy(out=mBs[:, :], in_=psA[:, :]), "vec")
                    I(v.tensor_copy(out=rBs[:, :], in_=psB[:, :]), "vec")
                R.vector(v_bce2)
                s_now["bce2"] = C["vec"]

                def v_ln2(v, i=i, bi=bi):
                    if i >= 2:
                        v.wait_ge(S["st"], sn[i - 2]["xl2st"])
                    VW(v)
                    for a in range(NDC):
                        sl = slice(a * CHT, (a + 1) * CHT)
                        I(v.tensor_tensor(out=tblk[:, sl], in0=x1blk[:, sl],
                                          in1=mBs[:, :], op=ALU.subtract), "vec")
                        VW(v)
                        I(v.tensor_tensor(out=xl2[bi][:, sl], in0=tblk[:, sl],
                                          in1=rBs[:, :], op=ALU.mult), "vec")
                    VW(v)
                    I(v.reduce_max(out=accs[1][:, i:i + 1], in_=xl2[bi][:, :],
                                   axis=AX.X, apply_absolute_value=True), "vec")
                R.vector(v_ln2)
                s_now["ln2"] = C["vec"]
                v_ln2c = C["vec"]

                def g_st2(g, i=i, bi=bi, v_x1c=v_x1c, v_ln2c=v_ln2c):
                    g.wait_ge(S["vec"], v_ln2c)
                    I(g.dma_start(out=x1_s[:, i * NDC * CHT:(i + 1) * NDC * CHT],
                                  in_=x1blk[:, :]), "st", 16)
                    I(g.dma_start(out=xl2_s[:, i * NDC * CHT:(i + 1) * NDC * CHT],
                                  in_=xl2[bi][:, :]), "st", 16)
                R.gpsimd(g_st2)
                s_now["xl2st"] = C["st"]
                sn.append(s_now)
                R.flush(blk)

            def v_fold1(v):
                VW(v)
                I(v.reduce_max(out=fold[:, 0:1], in_=accs[1][:, :], axis=AX.X),
                  "vec")
            R.vector(v_fold1)
            v_f1 = C["vec"]

            def g_cc1(g, v_f1=v_f1):
                g.wait_ge(S["vec"], v_f1)
                I(g.dma_start(out=cc_in.ap(), in_=fold[:, :]), "st", 16)
                g.wait_ge(S["st"], C["st"])
                I(g.collective_compute(
                    "AllReduce", ALU.max, replica_groups=[list(range(B))],
                    ins=[cc_in.ap()], outs=[cc_out.ap()]), "cc")
                g.wait_ge(S["cc"], C["cc"])
                I(g.dma_start(
                    out=brow[:, :],
                    in_=cc_out.ap().rearrange("p f -> (p f)")), "st", 16)
            R.gpsimd(g_cc1)
            st_cc1 = C["st"]

            def v_s1(v, st_cc1=st_cc1):
                v.wait_ge(S["st"], st_cc1)
                VW(v)
                I(v.reduce_max(out=srow[:, 0:4],
                               in_=brow.ap().rearrange("o (p f) -> o f p", f=4),
                               axis=AX.X), "vec")
                VW(v)
                I(v.tensor_scalar(out=srow[:, 4:5], in0=srow[0:1, 0:1],
                                  scalar1=1.0 / 32767.0, scalar2=1e-12,
                                  op0=ALU.mult, op1=ALU.max), "vec")
                VW(v)
                I(v.reciprocal(out=srow[:, 5:6], in_=srow[0:1, 4:5]), "vec")
            R.vector(v_s1)
            v_s1c = C["vec"]

            def t_bcs1b(t, v_s1c=v_s1c):
                t.wait_ge(S["vec"], v_s1c)
                I(t.matmul(psY[0][:, 0:1], ones_m[:, :], srow[:, 5:6],
                           start=True, stop=True), "pe")
            R.tensor(t_bcs1b)
            pe_b1 = C["pe"]

            def s_cst1(s, pe_b1=pe_b1):
                s.wait_ge(S["pe"], pe_b1)
                I(s.copy(out=cst[:, C_INV1:C_INV1 + 1], in_=psY[0][:, 0:1]), "sc")
            R.scalar(s_cst1)
            a2_done = dict(C)
            R.flush(blk)

        stack_wo.close()
        stack_wq.close()

        stack_wu = ExitStack()
        wuT = stack_wu.enter_context(nc.sbuf_tensor("wuT", [128, NDC * DI], BF16))
        stack_wg = ExitStack()
        wgT = stack_wg.enter_context(nc.sbuf_tensor("wgT", [128, NDC * DI], BF16))

        def emit_up_or_gate(R, buf, is_gate, acc_idx, scr_out, wT, f_scale_slot,
                            start_waits, hook=None):
            sn = []
            nxl = len(buf["xl2ld"])
            NG = NJC // 4
            for i in range(NCH):
                bi = i % nxl
                s_now = {}

                def sy_l(sy, i=i, bi=bi, nxl=nxl):
                    if i >= nxl:
                        sy.wait_ge(S["sc"], sn[i - nxl]["xq_sc"])
                    else:
                        for k, vv in start_waits.items():
                            sy.wait_ge(S[k], vv)
                    I(sy.dma_start(
                        out=buf["xl2ld"][bi][:, :],
                        in_=xl2_s[:, i * NDC * CHT:(i + 1) * NDC * CHT]), "ld", 16)
                R.sync(sy_l)
                ld_l = C["ld"]

                def s_xq(s, i=i, bi=bi, ld_l=ld_l):
                    s.wait_ge(S["ld"], ld_l)
                    if i >= 1:
                        s.wait_ge(S["vec"], sn[i - 1]["xqt_v"])
                    I(s.activation(out=buf["xqi"][:, :], in_=buf["xl2ld"][bi][:, :],
                                   func=AF.Relu, bias=cst[:, C_RB:C_RB + 1],
                                   scale=cst[:, C_INV1:C_INV1 + 1]), "sc")
                R.scalar(s_xq)
                s_now["xq_sc"] = C["sc"]
                sc_xq = C["sc"]

                def v_xq(v, sc_xq=sc_xq, i=i):
                    v.wait_ge(S["sc"], sc_xq)
                    if i >= 1:
                        v.wait_ge(S["pe"], sn[i - 1]["mm"])
                    I(v.tensor_scalar(out=buf["xqt"][:, :], in0=buf["xqi"][:, :],
                                      scalar1=DEB, scalar2=None,
                                      op0=ALU.subtract), "vec")
                R.vector(v_xq)
                s_now["xqt_v"] = C["vec"]
                v_xqc = C["vec"]

                ev_list = []
                rd_list = []
                for g4 in range(NG):
                    pg = buf["psF"][g4 % 2]

                    def t_mm(t, i=i, g4=g4, v_xqc=v_xqc, wT=wT, pg=pg):
                        t.wait_ge(S["vec"], v_xqc)
                        if g4 >= 2:
                            t.wait_ge(S["sc"], ev_list[g4 - 2])
                        elif i >= 1:
                            t.wait_ge(S["sc"], sn[i - 1]["ev_last2"][g4])
                        for jj in range(4):
                            jt = g4 * 4 + jj
                            for ic in range(NDC):
                                I(t.matmul(pg[:, jj * CHT:(jj + 1) * CHT],
                                           wT[:, ic * DI + jt * 128:
                                              ic * DI + jt * 128 + 128],
                                           buf["xqt"][:, ic * CHT:(ic + 1) * CHT],
                                           start=(ic == 0), stop=(ic == NDC - 1)),
                                  "pe")
                    R.tensor(t_mm)
                    pe_mm = C["pe"]

                    def s_ev(s, i=i, g4=g4, pe_mm=pe_mm, pg=pg):
                        s.wait_ge(S["pe"], pe_mm)
                        if i > 0 or g4 > 1:
                            s.wait_ge(S["st"], ev_st[g4 % 2][0])
                            if len(rd_list) >= 2:
                                s.wait_ge(S["vec"], rd_list[-2])
                            elif i > 0:
                                s.wait_ge(S["vec"], sn[i - 1]["rd_last2"][g4 % 2])
                        I(s.activation(
                            out=buf["evb"][g4 % 2][:, :],
                            in_=pg[:, :], func=AF.Copy,
                            bias=0.0, scale=1.0 / 16.0), "sc")
                    R.scalar(s_ev)
                    ev_list.append(C["sc"])

                    def v_red(v, g4=g4, ev=ev_list[-1]):
                        v.wait_ge(S["sc"], ev)
                        I(v.reduce_max(out=jtmp[:, g4:g4 + 1],
                                       in_=buf["evb"][g4 % 2][:, :], axis=AX.X,
                                       apply_absolute_value=True), "vec")
                    R.vector(v_red)
                    rd_list.append(C["vec"])

                    def g_stg(g, i=i, g4=g4, el=ev_list[-1], scr_out=scr_out):
                        g.wait_ge(S["sc"], el)
                        I(g.dma_start(
                            out=scr_out[:, (i * NJC + g4 * 4) * CHT:
                                        (i * NJC + (g4 + 1) * 4) * CHT],
                            in_=buf["evb"][g4 % 2][:, :]), "st", 16)
                    R.gpsimd(g_stg)
                    if i == 0 and g4 < 2:
                        ev_st = {0: [0], 1: [0]} if g4 == 0 else ev_st
                    ev_st[g4 % 2] = [C["st"]]
                    if hook is not None and i >= 1 and 1 <= g4 <= 4:
                        hook(i - 1, g4 - 1)
                s_now["mm"] = C["pe"]
                s_now["ev_last2"] = {0: ev_list[-2], 1: ev_list[-1]}
                s_now["rd_last2"] = {0: rd_list[-2], 1: rd_list[-1]}

                def v_fold(v, i=i, acc_idx=acc_idx, rl=rd_list[-1], NG=NG):
                    v.wait_ge(S["vec"], rl)
                    VW(v)
                    I(v.reduce_max(out=accs[acc_idx][:, i:i + 1],
                                   in_=jtmp[:, 0:NG], axis=AX.X), "vec")
                R.vector(v_fold)

                s_now["sto"] = C["st"]
                if hook is not None and i >= 1:
                    hook(i - 1, "end")
                sn.append(s_now)
                R.flush(blk)
            if hook is not None:
                for st_ in (0, 1, 2, 3, "end"):
                    hook(NCH - 1, st_)
                R.flush(blk)
            return sn

        with ExitStack() as pf1, nc.Block() as blk:
            R = Rec(blk)
            buf = {
                "xl2ld": [pf1.enter_context(
                    nc.sbuf_tensor(f"xl2ld{j}", [128, NDC * CHT], FP16))
                    for j in range(2)],
                "xqi": pf1.enter_context(
                    nc.sbuf_tensor("xqi", [128, NDC * CHT], I32)),
                "xqt": pf1.enter_context(
                    nc.sbuf_tensor("xqt", [128, NDC * CHT], BF16)),
                "evb": [pf1.enter_context(
                    nc.sbuf_tensor(f"gevb{j}", [128, 4 * CHT], FP16))
                    for j in range(2)],
                "psF": [pf1.enter_context(nc.psum_tensor(f"psF{j}", [128, 4 * CHT], F32))
                        for j in range(2)],
            }
            psb2 = buf["psF"][0]

            def sy_wg(sy):
                sy.wait_ge(S["sc"], a2_done["sc"])
                I(sy.dma_start(out=wgT[:, :], in_=wgateT_s[:, :]), "ld", 16)
                I(sy.dma_start(out=wuT[:, :], in_=wupT_s[:, :]), "ld", 16)
            R.sync(sy_wg)
            ld_wg = C["ld"]

            def t_wait(t, ld_wg=ld_wg):
                t.wait_ge(S["ld"], ld_wg)
            R.tensor(t_wait)

            sn_g = emit_up_or_gate(R, buf, True, 3, gate_s, wgT, None,
                                   {"sc": a2_done["sc"], "vec": a2_done["vec"],
                                    "st": a2_done["st"]})

            def v_fold3(v):
                VW(v)
                I(v.reduce_max(out=fold[:, 0:1], in_=accs[3][:, :], axis=AX.X),
                  "vec")
            R.vector(v_fold3)
            v_f3 = C["vec"]

            def g_cc3(g, v_f3=v_f3):
                g.wait_ge(S["vec"], v_f3)
                I(g.dma_start(out=cc_in.ap(), in_=fold[:, :]), "st", 16)
                g.wait_ge(S["st"], C["st"])
                I(g.collective_compute(
                    "AllReduce", ALU.max, replica_groups=[list(range(B))],
                    ins=[cc_in.ap()], outs=[cc_out.ap()]), "cc")
                g.wait_ge(S["cc"], C["cc"])
                I(g.dma_start(out=brow[:, :],
                              in_=cc_out.ap().rearrange("p f -> (p f)")),
                  "st", 16)
            R.gpsimd(g_cc3)
            st_cc3 = C["st"]

            def v_s3(v, st_cc3=st_cc3):
                v.wait_ge(S["st"], st_cc3)
                VW(v)
                I(v.reduce_max(out=srow[:, 0:4],
                               in_=brow.ap().rearrange("o (p f) -> o f p", f=4),
                               axis=AX.X), "vec")
                VW(v)
                I(v.tensor_scalar(out=srow[:, 10:11], in0=srow[0:1, 4:5],
                                  scalar1=16.0, scalar2=None, op0=ALU.mult), "vec")
                VW(v)
                I(v.tensor_tensor(out=srow[:, 6:7], in0=srow[0:1, 0:1],
                                  in1=srow[0:1, 10:11], op=ALU.mult), "vec")
                VW(v)
                I(v.tensor_scalar(out=srow[:, 7:8], in0=srow[0:1, 6:7],
                                  scalar1=1.0 / 32767.0, scalar2=1e-12,
                                  op0=ALU.mult, op1=ALU.max), "vec")
                VW(v)
                I(v.tensor_scalar(out=srow[:, 8:9], in0=srow[0:1, 7:8],
                                  scalar1=-DEB, scalar2=None, op0=ALU.mult), "vec")
                VW(v)
                I(v.reciprocal(out=srow[:, 9:10], in_=srow[0:1, 7:8]), "vec")
                VW(v)
                I(v.tensor_scalar(out=srow[:, 10:11], in0=srow[0:1, 4:5],
                                  scalar1=16.0, scalar2=None, op0=ALU.mult), "vec")
                VW(v)
                I(v.tensor_tensor(out=srow[:, 11:12], in0=srow[0:1, 10:11],
                                  in1=srow[0:1, 9:10], op=ALU.mult), "vec")
            R.vector(v_s3)
            v_s3c = C["vec"]

            def t_bc3(t, v_s3c=v_s3c, sc_all=C["sc"]):
                t.wait_ge(S["vec"], v_s3c)
                t.wait_ge(S["sc"], sc_all)
                I(t.matmul(psb2[:, 0:1], ones_m[:, :], srow[:, 7:8],
                           start=True, stop=True), "pe")
                I(t.matmul(psb2[:, 1:2], ones_m[:, :], srow[:, 8:9],
                           start=True, stop=True), "pe")
                I(t.matmul(psb2[:, 2:3], ones_m[:, :], srow[:, 11:12],
                           start=True, stop=True), "pe")
            R.tensor(t_bc3)
            pe_b3 = C["pe"]

            def s_cst3(s, pe_b3=pe_b3):
                s.wait_ge(S["pe"], pe_b3)
                I(s.copy(out=cst[:, C_SC3:C_SC3 + 1], in_=psb2[:, 0:1]), "sc")
                I(s.copy(out=cst[:, C_NB3:C_NB3 + 1], in_=psb2[:, 1:2]), "sc")
                I(s.copy(out=cst[:, C_F3:C_F3 + 1], in_=psb2[:, 2:3]), "sc")
            R.scalar(s_cst3)
            f1a_done = dict(C)
            R.flush(blk)
        stack_wg.close()

        with ExitStack() as pf2, nc.Block() as blk:
            R = Rec(blk)
            buf = {
                "xl2ld": [pf2.enter_context(
                    nc.sbuf_tensor("xl2le0", [128, NDC * CHT], FP16))],
                "xqi": pf2.enter_context(
                    nc.sbuf_tensor("xqi2", [128, NDC * CHT], I32)),
                "xqt": pf2.enter_context(
                    nc.sbuf_tensor("xqt2", [128, NDC * CHT], BF16)),
                "evb": [pf2.enter_context(
                    nc.sbuf_tensor(f"uevb{j}", [128, 4 * CHT], FP16))
                    for j in range(2)],
                "psF": [pf2.enter_context(nc.psum_tensor(f"psF2{j}", [128, 4 * CHT], F32))
                        for j in range(2)],
            }
            gld = [pf2.enter_context(nc.sbuf_tensor("gld0", [128, DI], FP16))]
            t3 = [pf2.enter_context(nc.sbuf_tensor(f"t3_{j}", [128, DI], I32))
                  for j in range(2)]
            sgt = pf2.enter_context(nc.sbuf_tensor("sgt", [128, DI], F32))
            u4 = [pf2.enter_context(nc.sbuf_tensor(f"u4_{j}", [128, DI], I32))
                  for j in range(2)]
            sqc = pf2.enter_context(nc.sbuf_tensor("sqc", [128, DI], F32))
            g2b = [pf2.enter_context(nc.sbuf_tensor(f"g2b{j}", [128, DI], BF16))
                   for j in range(2)]
            g4t = pf2.enter_context(nc.sbuf_tensor("g4t", [128, 4], F32))

            sn2 = []
            g2_stl = {0: [0], 1: [0]}
            g2_rd = {0: [0], 1: [0]}
            g_sc_rd = {0: [0]}

            f2_subv = {}

            def f2_hook(i, stage):
                if stage == "end":
                    sub_v = f2_subv[i]

                    def v_f4(v, i=i, lv=sub_v[-1]):
                        v.wait_ge(S["vec"], lv)
                        VW(v)
                        I(v.reduce_max(out=accs[4][:, i:i + 1], in_=g4t[:, :],
                                       axis=AX.X), "vec")
                    R.vector(v_f4)
                    sn2.append({"g2v": C["vec"], "g2st": C["st"]})
                    return
                sub_v = f2_subv.setdefault(i, [])
                if True:
                    sub = stage
                    sb = sub % 2

                    def sy_g(sy, i=i, sub=sub, sb=sb):
                        if i > 0 or sub > 0:
                            sy.wait_ge(S["sc"], g_sc_rd[0][0])
                        else:
                            sy.wait_ge(S["sc"], f1a_done["sc"])
                        I(sy.dma_start(
                            out=gld[0][:, :],
                            in_=gate_s[:, (i * NJC + sub * (NJC // 4)) * CHT:
                                       (i * NJC + (sub + 1) * (NJC // 4)) * CHT]),
                          "ld", 16)
                    R.sync(sy_g)
                    ld_g = C["ld"]

                    def s_ch(s, i=i, sub=sub, sb=sb, ld_g=ld_g):
                        s.wait_ge(S["ld"], ld_g)
                        if i > 0 or sub > 1:
                            s.wait_ge(S["vec"], g2_rd[sb][0])
                        I(s.activation(out=t3[sb][:, :], in_=gld[0][:, :],
                                       func=AF.Relu,
                                       bias=cst[:, C_RB:C_RB + 1],
                                       scale=cst[:, C_F3:C_F3 + 1]), "sc")
                        SW(s)
                        I(s.activation(out=sgt[:, :], in_=t3[sb][:, :],
                                       func=AF.Sigmoid,
                                       bias=cst[:, C_NB3:C_NB3 + 1],
                                       scale=cst[:, C_SC3:C_SC3 + 1]), "sc")
                        SW(s)
                        I(s.activation(out=u4[sb][:, :], in_=sgt[:, :],
                                       func=AF.Relu,
                                       bias=cst[:, C_RB:C_RB + 1],
                                       scale=cst[:, C_65536:C_65536 + 1]), "sc")
                    R.scalar(s_ch)
                    sc_ch = C["sc"]
                    g_sc_rd[0] = [C["sc"]]

                    def v_ch(v, i=i, sub=sub, sb=sb, sc_ch=sc_ch):
                        v.wait_ge(S["sc"], sc_ch)
                        if i > 0 or sub > 1:
                            v.wait_ge(S["st"], g2_stl[sb][0])
                        I(v.tensor_scalar(out=sqc[:, :], in0=u4[sb][:, :],
                                          scalar1=DEB + 65535.0, scalar2=DEB,
                                          op0=ALU.min, op1=ALU.subtract), "vec")
                        VW(v)
                        I(v.scalar_tensor_tensor(out=g2b[sb][:, :],
                                                 in0=t3[sb][:, :],
                                                 scalar=DEB, in1=sqc[:, :],
                                                 op0=ALU.subtract, op1=ALU.mult),
                          "vec")
                        VW(v)
                        I(v.reduce_max(out=g4t[:, sub:sub + 1],
                                       in_=g2b[sb][:, :],
                                       axis=AX.X, apply_absolute_value=True),
                          "vec")
                    R.vector(v_ch)
                    sub_v.append(C["vec"])
                    g2_rd[sb] = [C["vec"]]

                    def g_g2(g, i=i, sub=sub, sb=sb, vv=sub_v[-1]):
                        g.wait_ge(S["vec"], vv)
                        I(g.dma_start(
                            out=g2_s[:, (i * NJC + sub * (NJC // 4)) * CHT:
                                     (i * NJC + (sub + 1) * (NJC // 4)) * CHT],
                            in_=g2b[sb][:, :]), "st", 16)
                    R.gpsimd(g_g2)
                    g2_stl[sb] = [C["st"]]

            sn_u = emit_up_or_gate(R, buf, False, 2, up_s, wuT, None,
                                   {"sc": f1a_done["sc"], "vec": f1a_done["vec"],
                                    "st": f1a_done["st"]}, hook=f2_hook)

            def v_fold24(v, lg=sn2[-1]["g2v"]):
                v.wait_ge(S["vec"], lg)
                VW(v)
                I(v.reduce_max(out=fold[:, 0:1], in_=accs[2][:, :], axis=AX.X),
                  "vec")
                I(v.reduce_max(out=fold[:, 1:2], in_=accs[4][:, :], axis=AX.X),
                  "vec")
            R.vector(v_fold24)
            v_f24 = C["vec"]

            def g_cc24(g, v_f24=v_f24):
                g.wait_ge(S["vec"], v_f24)
                I(g.dma_start(out=cc_in.ap(), in_=fold[:, :]), "st", 16)
                g.wait_ge(S["st"], C["st"])
                I(g.collective_compute(
                    "AllReduce", ALU.max, replica_groups=[list(range(B))],
                    ins=[cc_in.ap()], outs=[cc_out.ap()]), "cc")
                g.wait_ge(S["cc"], C["cc"])
                I(g.dma_start(out=brow[:, :],
                              in_=cc_out.ap().rearrange("p f -> (p f)")),
                  "st", 16)
            R.gpsimd(g_cc24)
            st_cc24 = C["st"]

            def v_s24(v, st_cc24=st_cc24):
                v.wait_ge(S["st"], st_cc24)
                VW(v)
                I(v.reduce_max(out=srow[:, 0:4],
                               in_=brow.ap().rearrange("o (p f) -> o f p", f=4),
                               axis=AX.X), "vec")
                VW(v)
                I(v.tensor_scalar(out=srow[:, 10:11], in0=srow[0:1, 4:5],
                                  scalar1=16.0, scalar2=None, op0=ALU.mult), "vec")
                VW(v)
                I(v.tensor_tensor(out=srow[:, 6:7], in0=srow[0:1, 0:1],
                                  in1=srow[0:1, 10:11], op=ALU.mult), "vec")
                VW(v)
                I(v.tensor_scalar(out=srow[:, 12:13], in0=srow[0:1, 6:7],
                                  scalar1=1.0 / 32767.0, scalar2=1e-12,
                                  op0=ALU.mult, op1=ALU.max), "vec")
                VW(v)
                I(v.reciprocal(out=srow[:, 9:10], in_=srow[0:1, 12:13]), "vec")
                VW(v)
                I(v.tensor_scalar(out=srow[:, 10:11], in0=srow[0:1, 4:5],
                                  scalar1=16.0, scalar2=None, op0=ALU.mult), "vec")
                VW(v)
                I(v.tensor_tensor(out=srow[:, 11:12], in0=srow[0:1, 10:11],
                                  in1=srow[0:1, 9:10], op=ALU.mult), "vec")
                VW(v)
                I(v.tensor_scalar(out=srow[:, 13:14], in0=cst[0:1, C_SC3:C_SC3 + 1],
                                  scalar1=1.0 / 65536.0, scalar2=None,
                                  op0=ALU.mult), "vec")
                VW(v)
                I(v.tensor_tensor(out=srow[:, 6:7], in0=srow[0:1, 1:2],
                                  in1=srow[0:1, 13:14], op=ALU.mult), "vec")
                VW(v)
                I(v.tensor_scalar(out=srow[:, 14:15], in0=srow[0:1, 6:7],
                                  scalar1=1.0 / 32767.0, scalar2=1e-12,
                                  op0=ALU.mult, op1=ALU.max), "vec")
                VW(v)
                I(v.reciprocal(out=srow[:, 9:10], in_=srow[0:1, 14:15]), "vec")
                VW(v)
                I(v.tensor_tensor(out=srow[:, 15:16], in0=srow[0:1, 13:14],
                                  in1=srow[0:1, 9:10], op=ALU.mult), "vec")
            R.vector(v_s24)
            v_s24c = C["vec"]

            def t_bc24(t, v_s24c=v_s24c, sc_all=C["sc"]):
                t.wait_ge(S["vec"], v_s24c)
                t.wait_ge(S["sc"], sc_all)
                I(t.matmul(psb2[:, 0:1], ones_m[:, :], srow[:, 11:12],
                           start=True, stop=True), "pe")
                I(t.matmul(psb2[:, 1:2], ones_m[:, :], srow[:, 15:16],
                           start=True, stop=True), "pe")
            R.tensor(t_bc24)
            pe_b24 = C["pe"]

            def s_cst24(s, pe_b24=pe_b24):
                s.wait_ge(S["pe"], pe_b24)
                I(s.copy(out=cst[:, C_F2:C_F2 + 1], in_=psb2[:, 0:1]), "sc")
                I(s.copy(out=cst[:, C_F4:C_F4 + 1], in_=psb2[:, 1:2]), "sc")
            R.scalar(s_cst24)
            f2_done = dict(C)
            R.flush(blk)
        stack_wu.close()

        with ExitStack() as pf3, nc.Block() as blk:
            R = Rec(blk)
            uld = [pf3.enter_context(nc.sbuf_tensor(f"uld{j}", [128, DI], FP16))
                   for j in range(2)]
            g2ld = [pf3.enter_context(nc.sbuf_tensor(f"g2l{j}", [128, DI], BF16))
                    for j in range(2)]
            t2 = [pf3.enter_context(nc.sbuf_tensor(f"t2_{j}", [128, DI], I32))
                  for j in range(2)]
            upq = [pf3.enter_context(nc.sbuf_tensor(f"upq{j}", [128, DI], F32))
                   for j in range(2)]
            t4 = [pf3.enter_context(nc.sbuf_tensor(f"t4_{j}", [128, DI], I32))
                  for j in range(2)]
            ob = pf3.enter_context(nc.sbuf_tensor("ob", [128, NJC * CHT], FP16))
            o5t = pf3.enter_context(nc.sbuf_tensor("o5t", [128, 4], F32))
            psb3 = pf3.enter_context(nc.psum_tensor("psb3", [128, 8], F32))

            def s_g2i0(s):
                s.wait_ge(S["st"], f2_done["st"])
                I(s.dma_start(out=g2ld[0][:, :], in_=g2_s[:, 0:DI]), "te", 16)
            R.scalar(s_g2i0)

            sn3 = []
            for i in range(NCH):
                bi = i % 2
                s_now = {}

                sub_v = []
                for sub in range(4):
                    sl = slice(sub * DI, (sub + 1) * DI)
                    sb = sub % 2

                    def sy_u(sy, i=i, sub=sub, sb=sb):
                        if i > 0 or sub > 1:
                            sy.wait_ge(S["sc"], u_rd3[sb][0])
                        elif i == 0 and sub == 0:
                            sy.wait_ge(S["sc"], f2_done["sc"])
                            sy.wait_ge(S["st"], f2_done["st"])
                            sy.wait_ge(S["vec"], f2_done["vec"])
                        off = (i * NJC + sub * (NJC // 4)) * CHT
                        end = (i * NJC + (sub + 1) * (NJC // 4)) * CHT
                        I(sy.dma_start(out=uld[sb][:, :],
                                       in_=up_s[:, off:end]), "ld", 16)
                    R.sync(sy_u)
                    ld_u = C["ld"]

                    nxt = i * 4 + sub + 1
                    if nxt < NCH * 4:
                        ni, nsub, nsb = nxt // 4, nxt % 4, (nxt % 4) % 2

                        def s_g2i(s, ni=ni, nsub=nsub, nsb=nsb):
                            SW(s)
                            off = (ni * NJC + nsub * (NJC // 4)) * CHT
                            end = (ni * NJC + (nsub + 1) * (NJC // 4)) * CHT
                            I(s.dma_start(out=g2ld[nsb][:, :],
                                          in_=g2_s[:, off:end]), "te", 16)
                        R.scalar(s_g2i)
                    ld_g2 = 16 * (i * 4 + sub + 1)

                    def s_c2(s, i=i, sub=sub, sb=sb, sl=sl, ld_u=ld_u,
                             ld_g2=ld_g2, bi=bi):
                        s.wait_ge(S["ld"], ld_u)
                        if i > 0 or sub > 1:
                            s.wait_ge(S["vec"], ob_rd[sb][0])
                        I(s.activation(out=t2[sb][:, :], in_=uld[sb][:, :],
                                       func=AF.Relu, bias=cst[:, C_RB:C_RB + 1],
                                       scale=cst[:, C_F2:C_F2 + 1]), "sc")
                        SW(s)
                        I(s.activation(out=upq[sb][:, :], in_=t2[sb][:, :],
                                       func=AF.Copy, bias=-0.5,
                                       scale=1.0 / 65536.0), "sc")
                        s.wait_ge(S["te"], ld_g2)
                        I(s.activation(out=t4[sb][:, :], in_=g2ld[sb][:, :],
                                       func=AF.Relu, bias=cst[:, C_RB:C_RB + 1],
                                       scale=cst[:, C_F4:C_F4 + 1]), "sc")
                    R.scalar(s_c2)
                    sc_c2 = C["sc"]
                    if i == 0 and sub == 0:
                        u_rd3 = {0: [0], 1: [0]}
                    u_rd3[sb] = [C["sc"]]

                    def v_c2(v, i=i, sub=sub, sb=sb, sl=sl, sc_c2=sc_c2):
                        v.wait_ge(S["sc"], sc_c2)
                        if sub == 0 and i >= 1:
                            v.wait_ge(S["st"], sn3[i - 1]["obst"])
                        I(v.scalar_tensor_tensor(out=ob[:, sl], in0=t4[sb][:, :],
                                                 scalar=DEB, in1=upq[sb][:, :],
                                                 op0=ALU.subtract, op1=ALU.mult),
                          "vec")
                        VW(v)
                        I(v.reduce_max(out=o5t[:, sub:sub + 1], in_=ob[:, sl],
                                       axis=AX.X, apply_absolute_value=True),
                          "vec")
                    R.vector(v_c2)
                    sub_v.append(C["vec"])
                    if i == 0 and sub < 2:
                        ob_rd = {0: [0], 1: [0]} if sub == 0 else ob_rd
                    ob_rd[sb] = [C["vec"]]
                s_now["rd"] = C["sc"]

                def v_f5(v, i=i, lv=sub_v[-1]):
                    v.wait_ge(S["vec"], lv)
                    VW(v)
                    I(v.reduce_max(out=accs[5][:, i:i + 1], in_=o5t[:, :],
                                   axis=AX.X), "vec")
                R.vector(v_f5)
                v_ob = C["vec"]

                def g_ob(g, i=i, v_ob=v_ob):
                    g.wait_ge(S["vec"], v_ob)
                    I(g.dma_start(out=o_s[:, i * NJC * CHT:(i + 1) * NJC * CHT],
                                  in_=ob[:, :]), "st", 16)
                R.gpsimd(g_ob)
                s_now["obst"] = C["st"]
                sn3.append(s_now)
                R.flush(blk)

            def v_fold5(v):
                VW(v)
                I(v.reduce_max(out=fold[:, 0:1], in_=accs[5][:, :], axis=AX.X),
                  "vec")
            R.vector(v_fold5)
            v_f5c = C["vec"]

            def g_cc5(g, v_f5c=v_f5c):
                g.wait_ge(S["vec"], v_f5c)
                I(g.dma_start(out=cc_in.ap(), in_=fold[:, :]), "st", 16)
                g.wait_ge(S["st"], C["st"])
                I(g.collective_compute(
                    "AllReduce", ALU.max, replica_groups=[list(range(B))],
                    ins=[cc_in.ap()], outs=[cc_out.ap()]), "cc")
                g.wait_ge(S["cc"], C["cc"])
                I(g.dma_start(out=brow[:, :],
                              in_=cc_out.ap().rearrange("p f -> (p f)")),
                  "st", 16)
            R.gpsimd(g_cc5)
            st_cc5 = C["st"]

            def v_s5(v, st_cc5=st_cc5):
                v.wait_ge(S["st"], st_cc5)
                VW(v)
                I(v.reduce_max(out=srow[:, 0:4],
                               in_=brow.ap().rearrange("o (p f) -> o f p", f=4),
                               axis=AX.X), "vec")
                VW(v)
                I(v.tensor_tensor(out=srow[:, 5:6], in0=srow[0:1, 12:13],
                                  in1=srow[0:1, 14:15], op=ALU.mult), "vec")
                VW(v)
                I(v.tensor_scalar(out=srow[:, 6:7], in0=srow[0:1, 5:6],
                                  scalar1=65536.0, scalar2=None,
                                  op0=ALU.mult), "vec")
                VW(v)
                I(v.tensor_tensor(out=srow[:, 7:8], in0=srow[0:1, 0:1],
                                  in1=srow[0:1, 6:7], op=ALU.mult), "vec")
                VW(v)
                I(v.tensor_scalar(out=srow[:, 8:9], in0=srow[0:1, 7:8],
                                  scalar1=1.0 / 32767.0, scalar2=1e-12,
                                  op0=ALU.mult, op1=ALU.max), "vec")
                VW(v)
                I(v.reciprocal(out=srow[:, 9:10], in_=srow[0:1, 8:9]), "vec")
                VW(v)
                I(v.tensor_tensor(out=srow[:, 10:11], in0=srow[0:1, 6:7],
                                  in1=srow[0:1, 9:10], op=ALU.mult), "vec")
            R.vector(v_s5)
            v_s5x = C["vec"]

            def t_bc5(t, v_s5x=v_s5x):
                t.wait_ge(S["vec"], v_s5x)
                I(t.matmul(psb3[:, 0:1], ones_m[:, :], srow[:, 10:11],
                           start=True, stop=True), "pe")
                I(t.matmul(psb3[:, 1:2], ones_m[:, :], srow[:, 8:9],
                           start=True, stop=True), "pe")
            R.tensor(t_bc5)
            pe_b5 = C["pe"]

            def s_cst5(s, pe_b5=pe_b5):
                s.wait_ge(S["pe"], pe_b5)
                I(s.copy(out=cst[:, C_F5:C_F5 + 1], in_=psb3[:, 0:1]), "sc")
                I(s.copy(out=cst[:, C_SC5:C_SC5 + 1], in_=psb3[:, 1:2]), "sc")
            R.scalar(s_cst5)
            f3_done = dict(C)
            R.flush(blk)

        stack_wd = ExitStack()
        wdT = stack_wd.enter_context(nc.sbuf_tensor("wdT", [128, NJC * D], BF16))
        with ExitStack() as pf4, nc.Block() as blk:
            R = Rec(blk)
            old = [pf4.enter_context(nc.sbuf_tensor(f"old{j}", [128, DI], FP16))
                   for j in range(2)]
            x1ld = [pf4.enter_context(nc.sbuf_tensor(f"x1l{j}", [128, NDC * CHT], F32))
                    for j in range(2)]
            t5 = pf4.enter_context(nc.sbuf_tensor("t5", [128, DI], I32))
            oqt = [pf4.enter_context(nc.sbuf_tensor(f"oqt{j}", [128, NJC * CHT], BF16))
                   for j in range(2)]
            outb = pf4.enter_context(nc.sbuf_tensor("outb", [128, NDC * CHT], F32))
            psD = [pf4.enter_context(nc.psum_tensor(f"psD{j}", [128, CHT], F32))
                   for j in range(2)]

            def sy_wd(sy):
                sy.wait_ge(S["sc"], f3_done["sc"])
                I(sy.dma_start(out=wdT[:, :], in_=wdownT_s[:, :]), "ld", 16)
            R.sync(sy_wd)
            ld_wd = C["ld"]

            sn4 = []
            for i in range(NCH):
                bi = i % 2
                s_now = {}

                def sy_x1(sy, i=i, bi=bi, ld_wd=ld_wd):
                    sy.wait_ge(S["ld"], ld_wd)
                    if i >= 2:
                        sy.wait_ge(S["pe"], sn4[i - 2]["mm"])
                        sy.wait_ge(S["vec"], sn4[i - 2]["res"])
                    else:
                        sy.wait_ge(S["sc"], f3_done["sc"])
                        sy.wait_ge(S["st"], f3_done["st"])
                        sy.wait_ge(S["vec"], f3_done["vec"])
                    I(sy.dma_start(
                        out=x1ld[bi][:, :],
                        in_=x1_s[:, i * NDC * CHT:(i + 1) * NDC * CHT]), "ld", 16)
                R.sync(sy_x1)

                sub_oq = []
                for sub in range(4):
                    sl = slice(sub * DI, (sub + 1) * DI)
                    ob_ = sub % 2

                    def sy_o(sy, i=i, sub=sub, ob_=ob_):
                        if i > 0 or sub > 1:
                            sy.wait_ge(S["sc"], o_rd4[ob_][0])
                        I(sy.dma_start(
                            out=old[ob_][:, :],
                            in_=o_s[:, (i * NJC + sub * (NJC // 4)) * CHT:
                                    (i * NJC + (sub + 1) * (NJC // 4)) * CHT]),
                          "ld", 16)
                    R.sync(sy_o)
                    ld_o = C["ld"]

                    def s_c5(s, sub=sub, sl=sl, ld_o=ld_o, ob_=ob_, i=i):
                        s.wait_ge(S["ld"], ld_o)
                        if sub >= 1:
                            s.wait_ge(S["vec"], sub_oq[sub - 1])
                        elif i >= 1:
                            s.wait_ge(S["vec"], t5_free[0])
                        if i >= 2 and sub == 0:
                            s.wait_ge(S["pe"], sn4[i - 2]["mm"])
                        I(s.activation(out=t5[:, :], in_=old[ob_][:, :],
                                       func=AF.Relu, bias=cst[:, C_RB:C_RB + 1],
                                       scale=cst[:, C_F5:C_F5 + 1]), "sc")
                    R.scalar(s_c5)
                    sc_c5 = C["sc"]
                    if i == 0 and sub == 0:
                        o_rd4 = {0: [0], 1: [0]}
                    o_rd4[ob_] = [C["sc"]]

                    def v_c5(v, sub=sub, sl=sl, sc_c5=sc_c5, bi=bi):
                        v.wait_ge(S["sc"], sc_c5)
                        I(v.tensor_scalar(out=oqt[bi][:, sl], in0=t5[:, :],
                                          scalar1=DEB, scalar2=None,
                                          op0=ALU.subtract), "vec")
                    R.vector(v_c5)
                    sub_oq.append(C["vec"])
                t5_free = [C["vec"]]
                v_oqc = C["vec"]

                res_parts = []
                for ht in range(NDC):
                    def t_d(t, ht=ht, v_oqc=v_oqc, i=i, bi=bi):
                        t.wait_ge(S["vec"], v_oqc)
                        if ht >= 2:
                            t.wait_ge(S["vec"], res_parts[ht - 2])
                        elif i >= 1:
                            t.wait_ge(S["vec"], sn4[i - 1]["res_last2"][ht])
                        for jc in range(NJC):
                            I(t.matmul(psD[ht % 2][:, :],
                                       wdT[:, jc * D + ht * 128:
                                           jc * D + ht * 128 + 128],
                                       oqt[bi][:, jc * CHT:(jc + 1) * CHT],
                                       start=(jc == 0), stop=(jc == NJC - 1)),
                              "pe")
                    R.tensor(t_d)
                    pe_d = C["pe"]

                    def v_res(v, ht=ht, bi=bi, pe_d=pe_d, i=i):
                        v.wait_ge(S["pe"], pe_d)
                        if ht == 0 and i >= 1:
                            v.wait_ge(S["st"], sn4[i - 1]["ost"])
                        I(v.scalar_tensor_tensor(
                            out=outb[:, ht * CHT:(ht + 1) * CHT],
                            in0=psD[ht % 2][:, :],
                            scalar=cst[:, C_SC5:C_SC5 + 1],
                            in1=x1ld[bi][:, ht * CHT:(ht + 1) * CHT],
                            op0=ALU.mult, op1=ALU.add), "vec")
                    R.vector(v_res)
                    res_parts.append(C["vec"])
                s_now["mm"] = C["pe"]
                s_now["res"] = C["vec"]
                s_now["res_last2"] = {0: res_parts[-2], 1: res_parts[-1]}

                def g_out(g, i=i, rl=res_parts[-1]):
                    g.wait_ge(S["vec"], rl)
                    I(g.dma_start(out=outT[:, i * NDC * CHT:(i + 1) * NDC * CHT],
                                  in_=outb[:, :]), "st", 16)
                R.gpsimd(g_out)
                s_now["ost"] = C["st"]
                sn4.append(s_now)
                R.flush(blk)

            def sy_fin(sy):
                sy.wait_ge(S["st"], C["st"])
            R.sync(sy_fin)
            R.flush(blk)
        stack_wd.close()

    return nc


_NC_CACHE = [None]


def kernel(L0_attn_x, L0_attn_Wqkv, L0_attn_Wout, L0_attn_bout,
           L0_ffn_w_gate, L0_ffn_w_up, L0_ffn_w_down):
    x = np.ascontiguousarray(np.asarray(L0_attn_x, dtype=np.float32))
    wqkv = np.ascontiguousarray(np.asarray(L0_attn_Wqkv, dtype=np.float32))
    wo = np.ascontiguousarray(np.asarray(L0_attn_Wout, dtype=np.float32))
    bo = np.ascontiguousarray(np.asarray(L0_attn_bout, dtype=np.float32))
    wg = np.ascontiguousarray(np.asarray(L0_ffn_w_gate, dtype=np.float32))
    wu = np.ascontiguousarray(np.asarray(L0_ffn_w_up, dtype=np.float32))
    wd = np.ascontiguousarray(np.asarray(L0_ffn_w_down, dtype=np.float32))

    if _NC_CACHE[0] is None:
        _NC_CACHE[0] = build_nc()
    nc = _NC_CACHE[0]

    in_maps = []
    for c in range(B):
        xb = x[c].T.reshape(NDC, 128, NCH, CHT).transpose(1, 2, 0, 3)
        in_maps.append({
            "xT": np.ascontiguousarray(xb).reshape(128, NCH * NDC * CHT),
            "wqkv": wqkv, "wout": wo, "bout": bo,
            "wgate": wg, "wup": wu, "wdown": wd,
        })
    res = run_bass_kernel_spmd(nc, in_maps, core_ids=list(range(B)))
    out = np.empty((B, NTOK, D), dtype=np.float32)
    for c in range(B):
        ob = np.asarray(res.results[c]["outT"]).reshape(128, NCH, NDC, CHT)
        out[c] = ob.transpose(2, 0, 1, 3).reshape(D, NTOK).T
    return out


# version 22
# speedup vs baseline: 1.0385x; 1.0385x over previous
import numpy as np
from contextlib import ExitStack

import concourse.bass as bass
import concourse.mybir as mybir
from concourse.bass_utils import run_bass_kernel_spmd

F32 = mybir.dt.float32
I32 = mybir.dt.int32
BF16 = mybir.dt.bfloat16
FP16 = mybir.dt.float16
AF = mybir.ActivationFunctionType
ALU = mybir.AluOpType
AX = mybir.AxisListType

B = 8
NTOK = 8192
D = 1024
DI = 3072
H = 16
NCH = 16
CHT = 512
NDC = D // 128
NJC = DI // 128
DEB = 32768.0

C_RB, C_ONE, C_65536, C_INV1, C_SC3, C_NB3, C_F3, C_F2, C_F4, C_F5, C_SC5, C_RBW = range(12)


AUTO_FLUSH = [True]

DBG = set()


def build_nc():
    nc = bass.Bass("TRN2", target_bir_lowering=False, debug=False)

    def scratch(name, shape, dt):
        kind = "ExternalOutput" if name in DBG else "Internal"
        return nc.dram_tensor(name, shape, dt, kind=kind).ap()

    xT = nc.dram_tensor("xT", [128, NCH * NDC * CHT], F32, kind="ExternalInput").ap()
    wqkv = nc.dram_tensor("wqkv", [D, 3 * D], F32, kind="ExternalInput").ap()
    wout = nc.dram_tensor("wout", [D, D], F32, kind="ExternalInput").ap()
    bout = nc.dram_tensor("bout", [D], F32, kind="ExternalInput").ap()
    wgate = nc.dram_tensor("wgate", [DI, D], F32, kind="ExternalInput").ap()
    wup = nc.dram_tensor("wup", [DI, D], F32, kind="ExternalInput").ap()
    wdown = nc.dram_tensor("wdown", [D, DI], F32, kind="ExternalInput").ap()
    outT = nc.dram_tensor("outT", [128, NCH * NDC * CHT], F32,
                          kind="ExternalOutput").ap()

    q_s = scratch("q_s", [128, NCH * NDC * CHT], BF16)
    xl2_s = scratch("xl2_s", [128, NCH * NDC * CHT], FP16)
    x1_s = scratch("x1_s", [128, NCH * NDC * CHT], F32)
    up_s = scratch("up_s", [128, NCH * NJC * CHT], FP16)
    gate_s = scratch("gate_s", [128, NCH * NJC * CHT], FP16)
    g2_s = scratch("g2_s", [128, NCH * NJC * CHT], BF16)
    o_s = scratch("o_s", [128, NCH * NJC * CHT], FP16)
    wupT_s = scratch("wupT_s", [128, NDC * DI], BF16)
    wgateT_s = scratch("wgateT_s", [128, NDC * DI], BF16)
    wdownT_s = scratch("wdownT_s", [128, NJC * D], BF16)
    cc_in = nc.dram_tensor("cc_in", [128, 4], F32)
    cc_out = nc.dram_tensor("cc_out", [128, 4], F32)

    C = {}

    top = ExitStack()
    with top:
        S = {}
        for n in ("ld", "st", "pe", "vec", "sc", "gp", "cc"):
            S[n] = top.enter_context(nc.semaphore(n))
            C[n] = 0

        def I(instr, sem=None, n=1):
            if sem is not None:
                instr.then_inc(S[sem], n)
                C[sem] += n
            return instr

        def VW(v):
            v.wait_ge(S["vec"], C["vec"])

        def SW(s):
            s.wait_ge(S["sc"], C["sc"])

        class _InstrRec:
            def __init__(self, rec, eng, op, args, kwargs):
                self.rec, self.eng, self.op = rec, eng, op
                self.args, self.kwargs = args, kwargs
                self.incs = []

            def then_inc(self, sem, n=1):
                self.incs.append((sem, n))
                return self

            def emit(self, e):
                ins = getattr(e, self.op)(*self.args, **self.kwargs)
                for sem, n in self.incs:
                    ins.then_inc(sem, n)
                return ins

        class _EngProxy:
            def __init__(self, rec, name):
                self._rec, self._name = rec, name

            def wait_ge(self, sem, val):
                r = _InstrRec(self._rec, self._name, "wait_ge", (sem, val), {})
                self._rec.q[self._name].append(r)
                return r

            def __getattr__(self, op):
                def call(*a, **k):
                    r = _InstrRec(self._rec, self._name, op, a, k)
                    self._rec.q[self._name].append(r)
                    return r
                return call

        class Rec:
            ENGS = ("sync", "gpsimd", "scalar", "tensor", "vector")
            AUTO = AUTO_FLUSH

            def __init__(self, blk=None):
                self.q = {e: [] for e in self.ENGS}
                self.blk = blk

            def __getattr__(self, eng):
                if eng not in self.q:
                    raise AttributeError(eng)
                def run(fn):
                    fn(_EngProxy(self, eng))
                    if self.blk is not None and (Rec.AUTO[0] or eng != "scalar"):
                        self.flush_one(self.blk, eng)
                return run

            def flush_one(self, blk, eng):
                ths = self.q[eng]
                if not ths:
                    return
                self.q[eng] = []
                getattr(blk, eng)(lambda e, ths=ths: [t.emit(e) for t in ths])

            def flush(self, blk):
                for eng in self.ENGS:
                    self.flush_one(blk, eng)

        cst = top.enter_context(nc.sbuf_tensor("cst", [128, 16], F32))
        bout_sb = top.enter_context(nc.sbuf_tensor("bout_sb", [128, NDC], F32))
        ones_m = top.enter_context(nc.sbuf_tensor("ones_m", [1, 128], F32))
        ones_mb = top.enter_context(nc.sbuf_tensor("ones_mb", [1, 128], BF16))
        onesk = top.enter_context(nc.sbuf_tensor("onesk", [128, 1], F32))
        oneskb = top.enter_context(nc.sbuf_tensor("oneskb", [128, 1], BF16))
        idn = top.enter_context(nc.sbuf_tensor("idn", [128, 128], BF16))
        iot = top.enter_context(nc.sbuf_tensor("iot", [128, 128], F32))
        prow = top.enter_context(nc.sbuf_tensor("prow", [1, 8], F32))
        pri = top.enter_context(nc.sbuf_tensor("pri", [1, 4], I32))
        accs = {k: top.enter_context(nc.sbuf_tensor(f"acc{k}", [128, NCH], F32))
                for k in (1, 2, 3, 4, 5)}
        jtmp = top.enter_context(nc.sbuf_tensor("jtmp", [128, NJC], F32))
        fold = top.enter_context(nc.sbuf_tensor("fold", [128, 4], F32))
        brow = top.enter_context(nc.sbuf_tensor("brow", [1, 512], F32))
        srow = top.enter_context(nc.sbuf_tensor("srow", [1, 16], F32))
        kv_sb = top.enter_context(nc.sbuf_tensor("kv_sb", [128, (H // 2) * 128], BF16))
        kv_ks = top.enter_context(nc.sbuf_tensor("kv_ks", [128, H // 2], BF16))
        mrow = top.enter_context(nc.sbuf_tensor("mrow", [1, CHT], F32))
        m2row = top.enter_context(nc.sbuf_tensor("m2row", [1, CHT], F32))
        trow = top.enter_context(nc.sbuf_tensor("trow", [1, CHT], F32))
        vrowt = top.enter_context(nc.sbuf_tensor("vrowt", [1, CHT], F32))
        rrow = top.enter_context(nc.sbuf_tensor("rrow", [1, CHT], F32))
        zrow = top.enter_context(nc.sbuf_tensor("zrow", [1, CHT], F32))
        zrow2 = top.enter_context(nc.sbuf_tensor("zrow2", [1, CHT], F32))

        with ExitStack() as p0, nc.Block() as blk:
            R = Rec(blk)
            wraw = p0.enter_context(nc.sbuf_tensor("wraw", [128, DI], F32))
            tmpa = p0.enter_context(nc.sbuf_tensor("tmpa", [128, DI], F32))
            tree = p0.enter_context(nc.sbuf_tensor("tree", [128, DI], F32))
            rep1 = p0.enter_context(nc.sbuf_tensor("rep1", [128, DI], F32))
            rep2 = p0.enter_context(nc.sbuf_tensor("rep2", [128, DI], F32))
            sca = p0.enter_context(nc.sbuf_tensor("sca", [128, DI // 32], F32))
            inva = p0.enter_context(nc.sbuf_tensor("inva", [128, DI // 32], F32))
            ti32 = p0.enter_context(nc.sbuf_tensor("ti32", [128, DI], I32))
            wqn = p0.enter_context(nc.sbuf_tensor("wqn", [128, DI], BF16))
            wTf = p0.enter_context(nc.sbuf_tensor("wTf", [128, NDC * DI], BF16))
            pstr = p0.enter_context(nc.psum_tensor("pstr", [128, 512], BF16))
            psb0 = p0.enter_context(nc.psum_tensor("psb0", [128, 8], F32))

            def g_init(g):
                I(g.memset(cst[:, C_ONE:C_ONE + 1], 1.0), "gp")
                I(g.memset(cst[:, C_65536:C_65536 + 1], 65536.0), "gp")
                I(g.memset(ones_m[:, :], 1.0), "gp")
                I(g.memset(onesk[:, :], 1.0), "gp")
                I(g.memset(oneskb[:, :], 1.0), "gp")
                I(g.memset(prow[:, 0:8], 0.7), "gp")
                for k in accs:
                    I(g.memset(accs[k][:, :], 0.0), "gp")
                I(g.memset(fold[:, :], 0.0), "gp")
                I(g.iota(iot[:, :], [[1, 128]], channel_multiplier=-1,
                         allow_small_or_imprecise_dtypes=True), "gp")
            R.gpsimd(g_init)
            gp0 = C["gp"]

            def v_idn_f(v):
                v.wait_ge(S["gp"], gp0)
                I(v.tensor_scalar(out=idn[:, :], in0=iot[:, :], scalar1=0.0,
                                  scalar2=None, op0=ALU.is_equal), "vec")
            R.vector(v_idn_f)
            v_idn = C["vec"]

            def s_probe(s):
                s.wait_ge(S["gp"], gp0)
                I(s.activation(out=pri[:, :], in_=prow[0:1, 0:4], func=AF.Relu,
                               bias=fold[0:1, 0:1],
                               scale=cst[0:1, C_ONE:C_ONE + 1]), "sc")
            R.scalar(s_probe)
            sc_probe = C["sc"]

            def v_rb(v):
                v.wait_ge(S["sc"], sc_probe)
                I(v.tensor_scalar(out=prow[:, 1:2], in0=pri[0:1, 0:1],
                                  scalar1=-0.5, scalar2=32768.5,
                                  op0=ALU.mult, op1=ALU.add), "vec")
                I(v.tensor_scalar(out=prow[:, 2:3], in0=pri[0:1, 0:1],
                                  scalar1=-0.5, scalar2=8.5,
                                  op0=ALU.mult, op1=ALU.add), "vec")
            R.vector(v_rb)
            v_rbc = C["vec"]

            def t_rb(t):
                t.wait_ge(S["vec"], v_rbc)
                I(t.matmul(psb0[:, 0:2], ones_m[:, :], prow[:, 1:3],
                           start=True, stop=True), "pe")
            R.tensor(t_rb)
            pe_rb = C["pe"]

            def s_rb(s):
                s.wait_ge(S["pe"], pe_rb)
                I(s.copy(out=cst[:, C_RB:C_RB + 1], in_=psb0[:, 0:1]), "sc")
                I(s.copy(out=cst[:, C_RBW:C_RBW + 1], in_=psb0[:, 1:2]), "sc")
            R.scalar(s_rb)
            sc_rb = C["sc"]

            def sy_bout(sy):
                I(sy.dma_start(out=bout_sb[:, :],
                               in_=bout.rearrange("(a p) -> p a", p=128),
                               allow_slow_non_contiguous=True), "ld", 16)
            R.sync(sy_bout)

            wr_rd = [0]
            tm_rd = [0]
            wq_rd = [0]
            ev_done = [0]
            for wdram, O, Iw, wdst in ((wup, DI, D, wupT_s),
                                       (wgate, DI, D, wgateT_s),
                                       (wdown, D, DI, wdownT_s)):
                nic = Iw // 128
                nb = Iw // 32
                for ot in range(O // 128):
                    R.flush(blk)
                    def sy_w(sy, wdram=wdram, Iw=Iw, ot=ot, wrv=wr_rd[0]):
                        sy.wait_ge(S["vec"], wrv)
                        I(sy.dma_start(
                            out=wraw[:, 0:Iw],
                            in_=wdram.rearrange("(a p) c -> a p c", p=128)[ot]),
                          "ld", 16)
                    R.sync(sy_w)
                    ld_w = C["ld"]

                    def v_abs(v, Iw=Iw, ld_w=ld_w, tmv=tm_rd[0]):
                        v.wait_ge(S["ld"], ld_w)
                        v.wait_ge(S["sc"], tmv)
                        I(v.scalar_tensor_tensor(out=tmpa[:, 0:Iw],
                                                 in0=wraw[:, 0:Iw], scalar=-1.0,
                                                 in1=wraw[:, 0:Iw],
                                                 op0=ALU.mult, op1=ALU.max), "vec")
                    R.vector(v_abs)
                    v_absc = C["vec"]

                    def v_dq(v, Iw=Iw, nb=nb, sc_abs=v_absc):
                        v.wait_ge(S["vec"], sc_abs)
                        VW(v)
                        src = tmpa[:, 0:Iw].rearrange("p (n two) -> p n two", two=2)
                        w = Iw // 2
                        I(v.tensor_tensor(out=tree[:, 0:w], in0=src[:, :, 0],
                                          in1=src[:, :, 1], op=ALU.max), "vec")
                        off = 0
                        while w > nb:
                            VW(v)
                            s2 = tree[:, off:off + w].rearrange(
                                "p (n two) -> p n two", two=2)
                            I(v.tensor_tensor(out=tree[:, off + w:off + w + w // 2],
                                              in0=s2[:, :, 0], in1=s2[:, :, 1],
                                              op=ALU.max), "vec")
                            off += w
                            w //= 2
                        VW(v)
                        I(v.tensor_scalar(out=sca[:, 0:nb], in0=tree[:, off:off + nb],
                                          scalar1=1.0 / 7.0, scalar2=1e-12,
                                          op0=ALU.mult, op1=ALU.max), "vec")
                        VW(v)
                        I(v.reciprocal(out=inva[:, 0:nb], in_=sca[:, 0:nb]), "vec")
                        VW(v)
                        for dst, srct in ((rep1, inva), (rep2, sca)):
                            dv = dst[:, 0:Iw].rearrange("p (n k) -> p n k", k=32)
                            I(v.tensor_copy(
                                out=dv[:, :, 0:1],
                                in_=srct[:, 0:nb].rearrange("p (n o) -> p n o", o=1)), "vec")
                            wd = 1
                            while wd < 32:
                                take = min(wd, 32 - wd)
                                VW(v)
                                I(v.tensor_copy(out=dv[:, :, wd:wd + take],
                                                in_=dv[:, :, 0:take]), "vec")
                                wd += take
                            VW(v)
                        I(v.tensor_tensor(out=tmpa[:, 0:Iw], in0=wraw[:, 0:Iw],
                                          in1=rep1[:, 0:Iw], op=ALU.mult), "vec")
                    R.vector(v_dq)
                    wr_rd[0] = C["vec"]
                    v_t = C["vec"]

                    def s_conv(s, Iw=Iw, v_t=v_t):
                        s.wait_ge(S["vec"], v_t)
                        s.wait_ge(S["sc"], sc_rb)
                        I(s.activation(out=ti32[:, 0:Iw], in_=tmpa[:, 0:Iw],
                                       func=AF.Relu, bias=cst[:, C_RBW:C_RBW + 1],
                                       scale=cst[:, C_ONE:C_ONE + 1]), "sc")
                    R.scalar(s_conv)
                    sc_conv = C["sc"]
                    tm_rd[0] = C["sc"]

                    def v_wq(v, Iw=Iw, sc_conv=sc_conv, wqv=wq_rd[0]):
                        v.wait_ge(S["sc"], sc_conv)
                        v.wait_ge(S["pe"], wqv)
                        I(v.scalar_tensor_tensor(out=wqn[:, 0:Iw], in0=ti32[:, 0:Iw],
                                                 scalar=8.0, in1=rep2[:, 0:Iw],
                                                 op0=ALU.subtract, op1=ALU.mult),
                          "vec")
                    R.vector(v_wq)
                    v_wqc = C["vec"]

                    for g0 in range(0, nic, 4):
                        gk = min(4, nic - g0)

                        def t_tr(t, g0=g0, gk=gk, v_wqc=v_wqc, evd=ev_done[0]):
                            t.wait_ge(S["vec"], max(v_wqc, v_idn))
                            t.wait_ge(S["sc"], evd)
                            for j in range(gk):
                                ic = g0 + j
                                I(t.transpose(pstr[:, j * 128:j * 128 + 128],
                                              wqn[:, ic * 128:ic * 128 + 128],
                                              idn[:, :]), "pe")
                        R.tensor(t_tr)
                        pe_tr = C["pe"]
                        wq_rd[0] = C["pe"]

                        def s_ev(s, g0=g0, gk=gk, ot=ot, O=O, pe_tr=pe_tr):
                            s.wait_ge(S["pe"], pe_tr)
                            dst = wTf.ap().rearrange("p (a j) -> p a j", j=O)[
                                :, g0:g0 + gk, ot * 128:ot * 128 + 128]
                            I(s.copy(out=dst,
                                     in_=pstr[:, 0:gk * 128].rearrange(
                                         "p (a j) -> p a j", j=128)), "sc")
                        R.scalar(s_ev)
                        ev_done[0] = C["sc"]

                def g_st(g, wdst=wdst, nic=nic, O=O, scv=ev_done[0]):
                    g.wait_ge(S["sc"], scv)
                    I(g.dma_start(out=wdst[:, 0:nic * O], in_=wTf[:, 0:nic * O]),
                      "st", 16)
                R.gpsimd(g_st)
                st_w = C["st"]

                def s_gate(s, st_w=st_w):
                    s.wait_ge(S["st"], st_w)
                R.scalar(s_gate)

            R.flush(blk)

        stack_wq = ExitStack()
        wqkv_sb = stack_wq.enter_context(
            nc.sbuf_tensor("wqkv_sb", [128, NDC * 3 * D], BF16))
        stack_wo = ExitStack()
        wout_sb = stack_wo.enter_context(
            nc.sbuf_tensor("wout_sb", [128, NDC * D], BF16))
        with ExitStack() as p0b, nc.Block() as blk:
            R = Rec(blk)
            wcst = [p0b.enter_context(nc.sbuf_tensor(f"wcst{j}", [128, 3 * D], F32))
                    for j in range(2)]
            casts = []
            for idx in range(2 * NDC):
                R.flush(blk)
                is_o = idx >= NDC
                ic = idx - NDC if is_o else idx
                w = D if is_o else 3 * D
                src = (wout if is_o else wqkv).rearrange("(a p) c -> a p c", p=128)[ic]

                def sy_c(sy, idx=idx, src=src, w=w):
                    if idx >= 2:
                        sy.wait_ge(S["vec"], casts[idx - 2])
                    I(sy.dma_start(out=wcst[idx % 2][:, 0:w], in_=src), "ld", 16)
                R.sync(sy_c)
                ld_c = C["ld"]

                def v_c(v, idx=idx, is_o=is_o, ic=ic, w=w, ld_c=ld_c):
                    v.wait_ge(S["ld"], ld_c)
                    dst = (wout_sb[:, ic * D:(ic + 1) * D] if is_o
                           else wqkv_sb[:, ic * 3 * D:(ic + 1) * 3 * D])
                    I(v.tensor_copy(out=dst, in_=wcst[idx % 2][:, 0:w]), "vec")
                    casts.append(C["vec"])
                R.vector(v_c)

            R.flush(blk)

        with ExitStack() as pa, nc.Block() as blk:
            R = Rec(blk)
            xblk = [pa.enter_context(nc.sbuf_tensor(f"xblk{j}", [128, NDC * CHT], F32))
                    for j in range(2)]
            sqblk = pa.enter_context(nc.sbuf_tensor("sqblk", [128, NDC * CHT], F32))
            tblk = sqblk
            xl1 = pa.enter_context(nc.sbuf_tensor("xl1", [128, NDC * CHT], BF16))
            k_sb = pa.enter_context(nc.sbuf_tensor("k_sb", [128, 4 * D], BF16))
            v_sb = pa.enter_context(nc.sbuf_tensor("v_sb", [128, 4 * D], BF16))
            qev = [pa.enter_context(nc.sbuf_tensor(f"qev{j}", [128, NDC * CHT], BF16))
                   for j in range(2)]
            mBs = pa.enter_context(nc.sbuf_tensor("mBs", [128, CHT], F32))
            rBs = pa.enter_context(nc.sbuf_tensor("rBs", [128, CHT], F32))
            psA = pa.enter_context(nc.psum_tensor("psA", [128, CHT], F32))
            psB = pa.enter_context(nc.psum_tensor("psB", [128, CHT], F32))
            psP = [pa.enter_context(nc.psum_tensor(f"psP{j}", [128, CHT], F32))
                   for j in range(2)]
            kvP = [pa.enter_context(nc.psum_tensor(f"kvP{j}", [128, 512], F32))
                   for j in range(2)]
            kvS2 = pa.enter_context(nc.psum_tensor("kvS2", [128, 8], F32))

            sn = []
            for i in range(NCH):
                bi = i % 2
                s_now = {}

                def sy_x(sy, i=i, bi=bi):
                    if i >= 2:
                        sy.wait_ge(S["vec"], sn[i - 2]["x_rd_v"])
                        sy.wait_ge(S["pe"], sn[i - 2]["x_rd_pe"])
                    I(sy.dma_start(out=xblk[bi][:, :],
                                   in_=xT[:, i * NDC * CHT:(i + 1) * NDC * CHT]),
                      "ld", 16)
                R.sync(sy_x)
                ld_x = C["ld"]

                def s_sq(s, i=i, bi=bi, ld_x=ld_x):
                    s.wait_ge(S["ld"], ld_x)
                    if i >= 1:
                        s.wait_ge(S["pe"], sn[i - 1]["stats"])
                        s.wait_ge(S["vec"], sn[i - 1]["ln"])
                    I(s.activation(out=sqblk[:, :], in_=xblk[bi][:, :],
                                   func=AF.Square, bias=0.0, scale=1.0), "sc")
                R.scalar(s_sq)
                sc_sq = C["sc"]

                def t_st(t, i=i, bi=bi, sc_sq=sc_sq):
                    t.wait_ge(S["sc"], sc_sq)
                    if i >= 1:
                        t.wait_ge(S["vec"], sn[i - 1]["bce"])
                    for a in range(NDC):
                        I(t.matmul(psA[0:1, :], onesk[:, :],
                                   xblk[bi][:, a * CHT:(a + 1) * CHT],
                                   start=(a == 0), stop=(a == NDC - 1)), "pe")
                    for a in range(NDC):
                        I(t.matmul(psB[0:1, :], onesk[:, :],
                                   sqblk[:, a * CHT:(a + 1) * CHT],
                                   start=(a == 0), stop=(a == NDC - 1)), "pe")
                R.tensor(t_st)
                s_now["stats"] = C["pe"]
                pe_stats = C["pe"]

                def v_rm(v, pe_stats=pe_stats):
                    v.wait_ge(S["pe"], pe_stats)
                    I(v.tensor_scalar(out=mrow[:, :], in0=psA[0:1, :],
                                      scalar1=1.0 / D, scalar2=None,
                                      op0=ALU.mult), "vec")
                    I(v.tensor_scalar(out=m2row[:, :], in0=psB[0:1, :],
                                      scalar1=1.0 / D, scalar2=None,
                                      op0=ALU.mult), "vec")
                R.vector(v_rm)
                s_now["rm1"] = C["vec"]

                def v_rm2(v):
                    VW(v)
                    I(v.tensor_tensor(out=trow[:, :], in0=mrow[:, :],
                                      in1=mrow[:, :], op=ALU.mult), "vec")
                    VW(v)
                    I(v.tensor_tensor(out=vrowt[:, :], in0=m2row[:, :],
                                      in1=trow[:, :], op=ALU.subtract), "vec")
                    VW(v)
                    I(v.tensor_scalar(out=trow[:, :], in0=vrowt[:, :],
                                      scalar1=1e-5, scalar2=None, op0=ALU.add),
                      "vec")
                    VW(v)
                    I(v.reciprocal(out=vrowt[:, :], in_=trow[:, :]), "vec")
                R.vector(v_rm2)
                v_rm2c = C["vec"]

                def s_rstd(s, v_rm2c=v_rm2c):
                    s.wait_ge(S["vec"], v_rm2c)
                    I(s.activation(out=rrow[:, :], in_=vrowt[:, :], func=AF.Sqrt,
                                   bias=0.0, scale=1.0), "sc")
                R.scalar(s_rstd)
                sc_rstd = C["sc"]

                def t_bc(t, i=i, sc_rstd=sc_rstd):
                    t.wait_ge(S["sc"], sc_rstd)
                    if i >= 1:
                        t.wait_ge(S["vec"], sn[i - 1]["bce"])
                    I(t.matmul(psA[:, :], ones_m[:, :], mrow[:, :],
                               start=True, stop=True), "pe")
                    I(t.matmul(psB[:, :], ones_m[:, :], rrow[:, :],
                               start=True, stop=True), "pe")
                R.tensor(t_bc)
                pe_bc = C["pe"]

                def v_bce(v, pe_bc=pe_bc):
                    v.wait_ge(S["pe"], pe_bc)
                    I(v.tensor_copy(out=mBs[:, :], in_=psA[:, :]), "vec")
                    I(v.tensor_copy(out=rBs[:, :], in_=psB[:, :]), "vec")
                R.vector(v_bce)
                s_now["bce"] = C["vec"]

                def v_ln(v, i=i, bi=bi):
                    if i >= 1:
                        v.wait_ge(S["pe"], sn[i - 1]["qkv"])
                    VW(v)
                    for a in range(NDC):
                        sl = slice(a * CHT, (a + 1) * CHT)
                        I(v.tensor_tensor(out=tblk[:, sl], in0=xblk[bi][:, sl],
                                          in1=mBs[:, :], op=ALU.subtract), "vec")
                        VW(v)
                        I(v.tensor_tensor(out=xl1[:, sl], in0=tblk[:, sl],
                                          in1=rBs[:, :], op=ALU.mult), "vec")
                R.vector(v_ln)
                s_now["ln"] = C["vec"]
                s_now["x_rd_v"] = C["vec"]
                v_lnc = C["vec"]

                if i == 0:
                    qe_prev = {0: ("sc", 0), 1: ("sc", 0)}
                q_ev = []
                for dot in range(NDC):
                    def t_q(t, i=i, dot=dot, v_lnc=v_lnc, qp=None):
                        t.wait_ge(S["vec"], v_lnc)
                        if i >= 1 or dot >= 2:
                            t.wait_ge(S[qe_prev[dot % 2][0]], qe_prev[dot % 2][1])
                        for ic in range(NDC):
                            I(t.matmul(psP[dot % 2][:, :],
                                       wqkv_sb[:, ic * 3 * D + dot * 128:
                                               ic * 3 * D + dot * 128 + 128],
                                       xl1[:, ic * CHT:(ic + 1) * CHT],
                                       start=(ic == 0), stop=(ic == NDC - 1)),
                              "pe")
                    R.tensor(t_q)
                    pe_q = C["pe"]

                    def v_qe(v, bi=bi, dot=dot, pe_q=pe_q, i=i):
                        v.wait_ge(S["pe"], pe_q)
                        if i >= 1:
                            v.wait_ge(S["st"], sn[i - 1]["qst"])
                        I(v.tensor_scalar(out=qev[bi][:, dot * CHT:(dot + 1) * CHT],
                                          in0=psP[dot % 2][:, :], scalar1=0.0,
                                          scalar2=None, op0=ALU.max), "vec")
                    R.vector(v_qe)
                    q_ev.append(("vec", C["vec"]))
                    qe_prev[dot % 2] = ("vec", C["vec"])

                for g in range(4):
                    for ct in range(4):
                        def t_kv(t, i=i, g=g, ct=ct, v_lnc=v_lnc):
                            t.wait_ge(S["vec"], v_lnc)
                            t.wait_ge(S[qe_prev[ct % 2][0]], qe_prev[ct % 2][1])
                            for ic in range(NDC):
                                I(t.matmul(
                                    psP[ct % 2][:, :],
                                    xl1[:, ic * CHT + g * 128:
                                        ic * CHT + g * 128 + 128],
                                    wqkv_sb[:, ic * 3 * D + D + ct * CHT:
                                            ic * 3 * D + D + (ct + 1) * CHT],
                                    start=(ic == 0), stop=(ic == NDC - 1)), "pe")
                        R.tensor(t_kv)
                        pe_kv = C["pe"]

                        def v_kve(v, g=g, ct=ct, pe_kv=pe_kv):
                            v.wait_ge(S["pe"], pe_kv)
                            if ct < 2:
                                I(v.tensor_scalar(
                                    out=k_sb[:, g * D + ct * CHT:
                                             g * D + (ct + 1) * CHT],
                                    in0=psP[ct % 2][:, :], scalar1=0.0,
                                    scalar2=None, op0=ALU.max), "vec")
                            else:
                                I(v.tensor_copy(
                                    out=v_sb[:, g * D + (ct - 2) * CHT:
                                             g * D + (ct - 1) * CHT],
                                    in_=psP[ct % 2][:, :]), "vec")
                        R.vector(v_kve)
                        qe_prev[ct % 2] = ("vec", C["vec"])
                    sc_kve = C["vec"]

                    def t_acc(t, i=i, g=g, sc_kve=sc_kve):
                        t.wait_ge(S["vec"], sc_kve)
                        first = (i == 0 and g == 0)
                        last = (i == NCH - 1 and g == 3)
                        for a in range(NDC):
                            I(t.matmul(kvP[a // 4][:, (a % 4) * 128:
                                                   (a % 4) * 128 + 128],
                                       k_sb[:, g * D + a * 128:g * D + (a + 1) * 128],
                                       v_sb[:, g * D + a * 128:g * D + (a + 1) * 128],
                                       start=first, stop=last), "pe")
                            I(t.matmul(kvS2[:, a:a + 1],
                                       k_sb[:, g * D + a * 128:g * D + (a + 1) * 128],
                                       oneskb[:, :],
                                       start=first, stop=last), "pe")
                    R.tensor(t_acc)
                    kv_rd = [C["pe"]]
                s_now["qkv"] = C["pe"]
                s_now["x_rd_pe"] = s_now["stats"]

                def g_qst(g, i=i, bi=bi, qlast=q_ev[-1]):
                    g.wait_ge(S[qlast[0]], qlast[1])
                    I(g.dma_start(out=q_s[:, i * NDC * CHT:(i + 1) * NDC * CHT],
                                  in_=qev[bi][:, :]), "st", 16)
                R.gpsimd(g_qst)
                s_now["qst"] = C["st"]
                sn.append(s_now)
                R.flush(blk)

            pe_all = C["pe"]

            def g_kvz(g, pe_all=pe_all):
                g.wait_ge(S["pe"], pe_all)
                I(g.memset(kv_sb[:, :], 0.0), "gp")
            R.gpsimd(g_kvz)
            gp_kvz = C["gp"]

            def s_kvout(s, pe_all=pe_all, gp_kvz=gp_kvz):
                s.wait_ge(S["gp"], gp_kvz)
                I(s.copy(out=kv_sb.ap().rearrange("p (h c) -> p h c", c=128)[0:64, 0:4, 0:64],
                         in_=bass.AP(kvP[0].ap().tensor, 0,
                                     [[512, 64], [128, 4], [1, 64]])), "sc")
                I(s.copy(out=kv_sb.ap().rearrange("p (h c) -> p h c", c=128)[0:64, 4:8, 0:64],
                         in_=bass.AP(kvP[1].ap().tensor, 0,
                                     [[512, 64], [128, 4], [1, 64]])), "sc")
                I(s.copy(out=kv_sb.ap().rearrange("p (h c) -> p h c", c=128)[64:128, 0:4, 64:128],
                         in_=bass.AP(kvP[0].ap().tensor, 64 * 512 + 64,
                                     [[512, 64], [128, 4], [1, 64]])), "sc")
                I(s.copy(out=kv_sb.ap().rearrange("p (h c) -> p h c", c=128)[64:128, 4:8, 64:128],
                         in_=bass.AP(kvP[1].ap().tensor, 64 * 512 + 64,
                                     [[512, 64], [128, 4], [1, 64]])), "sc")
                I(s.activation(
                    out=kv_ks.ap().rearrange("p (h o) -> p h o", o=1),
                    in_=kvS2[:, :].rearrange("p (h o) -> p h o", o=1),
                    func=AF.Copy, bias=1e-6, scale=1.0), "sc")
            R.scalar(s_kvout)
            a1_done = dict(C)
            R.flush(blk)

        with ExitStack() as pa2, nc.Block() as blk:
            R = Rec(blk)
            qld = [pa2.enter_context(nc.sbuf_tensor(f"qld{j}", [128, NDC * CHT], BF16))
                   for j in range(2)]
            xblk = [pa2.enter_context(nc.sbuf_tensor(f"x2blk{j}", [128, NDC * CHT], F32))
                    for j in range(2)]
            oaT = pa2.enter_context(nc.sbuf_tensor("oaT", [128, NDC * CHT], BF16))
            x1blk = pa2.enter_context(nc.sbuf_tensor("x1blk", [128, NDC * CHT], F32))
            sqblk = pa2.enter_context(nc.sbuf_tensor("sq2blk", [128, NDC * CHT], F32))
            tblk = sqblk
            xl2 = [pa2.enter_context(nc.sbuf_tensor(f"xl2b{j}", [128, NDC * CHT], FP16))
                   for j in range(2)]
            psO = [pa2.enter_context(nc.psum_tensor(f"psO{j}", [128, CHT], F32))
                   for j in range(2)]
            psZd = pa2.enter_context(nc.psum_tensor("psZd", [16, CHT], F32))
            psZB = pa2.enter_context(nc.psum_tensor("psZB", [128, CHT], F32))
            zdr = pa2.enter_context(nc.sbuf_tensor("zdr", [16, CHT], F32))
            zdr2 = pa2.enter_context(nc.sbuf_tensor("zdr2", [16, CHT], F32))
            zrS = pa2.enter_context(nc.sbuf_tensor("zrS", [16, CHT], BF16))
            zbsv = pa2.enter_context(nc.sbuf_tensor("zbsv", [128, CHT], F32))
            Z16 = pa2.enter_context(nc.sbuf_tensor("Z16", [128, 128], BF16))
            Sall = pa2.enter_context(nc.sbuf_tensor("Sall", [16, 8 * 128], BF16))
            psY = [pa2.enter_context(nc.psum_tensor(f"psY{j}", [128, CHT], F32))
                   for j in range(2)]
            psA = pa2.enter_context(nc.psum_tensor("ps2A", [128, CHT], F32))
            psB = pa2.enter_context(nc.psum_tensor("ps2B", [128, CHT], F32))
            mBs = pa2.enter_context(nc.sbuf_tensor("mBs2", [128, CHT], F32))
            rBs = pa2.enter_context(nc.sbuf_tensor("rBs2", [128, CHT], F32))

            Sit = pa2.enter_context(nc.sbuf_tensor("Sit", [16, 8 * 128], F32))

            def g_sel(g):
                I(g.memset(Z16[:, :], 0.0), "gp")
                I(g.iota(Sit.ap().rearrange("k (c w) -> k c w", w=64),
                         [[1, 16], [0, 64]], channel_multiplier=-1,
                         allow_small_or_imprecise_dtypes=True), "gp")
            R.gpsimd(g_sel)
            gp_sel = C["gp"]

            def v_sel(v):
                v.wait_ge(S["gp"], gp_sel)
                I(v.tensor_scalar(out=Sall[:, :], in0=Sit[:, :], scalar1=0.0,
                                  scalar2=None, op0=ALU.is_equal), "vec")
            R.vector(v_sel)

            def s_z16(s):
                s.wait_ge(S["gp"], gp_sel)
                s.wait_ge(S["sc"], a1_done["sc"])
                for h in range(H):
                    a, pb = h // 2, (h % 2) * 64
                    I(s.copy(out=Z16[pb:pb + 64, a * 16 + h:a * 16 + h + 1],
                             in_=kv_ks[pb:pb + 64, a:a + 1]), "sc")
            R.scalar(s_z16)
            sc_z16 = C["sc"]

            sn = []
            zev_prev = [0]
            zbc_last = [0]
            for i in range(NCH):
                bi = i % 2
                s_now = {}

                def sy_q(sy, i=i, bi=bi):
                    if i >= 2:
                        sy.wait_ge(S["pe"], sn[i - 2]["q_rd"])
                        sy.wait_ge(S["vec"], sn[i - 2]["x_rd"])
                    else:
                        sy.wait_ge(S["sc"], a1_done["sc"])
                        sy.wait_ge(S["st"], a1_done["st"])
                        sy.wait_ge(S["vec"], a1_done["vec"])
                    I(sy.dma_start(out=qld[bi][:, :],
                                   in_=q_s[:, i * NDC * CHT:(i + 1) * NDC * CHT]),
                      "ld", 16)
                    I(sy.dma_start(out=xblk[bi][:, :],
                                   in_=xT[:, i * NDC * CHT:(i + 1) * NDC * CHT]),
                      "ld", 16)
                R.sync(sy_q)
                ld_q = C["ld"]

                def t_zd(t, i=i, bi=bi, ld_q=ld_q, zp=zev_prev[0]):
                    t.wait_ge(S["ld"], ld_q)
                    t.wait_ge(S["sc"], sc_z16)
                    if i >= 1:
                        t.wait_ge(S["vec"], zp)
                    for a in range(NDC):
                        I(t.matmul(psZd[:, :], Z16[:, a * 16:(a + 1) * 16],
                                   qld[bi][:, a * CHT:(a + 1) * CHT],
                                   start=(a == 0), stop=(a == NDC - 1)), "pe")
                R.tensor(t_zd)
                pe_zd = C["pe"]

                def v_zd(v, pe_zd=pe_zd, zb=zbc_last[0], i=i):
                    v.wait_ge(S["pe"], pe_zd)
                    if i >= 1:
                        v.wait_ge(S["pe"], zb)
                    I(v.tensor_scalar(out=zdr[:, :], in0=psZd[:, :],
                                      scalar1=1e-6, scalar2=None,
                                      op0=ALU.add), "vec")
                    VW(v)
                    I(v.reciprocal(out=zdr2[:, :], in_=zdr[:, :]), "vec")
                    VW(v)
                    I(v.tensor_copy(out=zrS[:, :], in_=zdr2[:, :]), "vec")
                R.vector(v_zd)
                zev_prev[0] = C["vec"]
                v_zdc = C["vec"]

                oa_done = []
                vzc_prev = [0]
                for a in range(NDC):
                    def t_o(t, i=i, bi=bi, a=a, ld_q=ld_q):
                        t.wait_ge(S["ld"], ld_q)
                        if a >= 2:
                            t.wait_ge(S["vec"], oa_done[a - 2])
                        elif i >= 1:
                            t.wait_ge(S["vec"], sn[i - 1]["oa_last2"][a % 2])
                        I(t.matmul(psO[a % 2][:, :],
                                   kv_sb[:, a * 128:(a + 1) * 128],
                                   qld[bi][:, a * CHT:(a + 1) * CHT],
                                   start=True, stop=True), "pe")
                    R.tensor(t_o)
                    pe_o = C["pe"]

                    def t_zbc(t, a=a, i=i, v_zdc=v_zdc, vz=vzc_prev[0]):
                        t.wait_ge(S["vec"], v_zdc)
                        if a >= 1 or i >= 1:
                            t.wait_ge(S["vec"], vz)
                        I(t.matmul(psZB[:, :], Sall[:, a * 128:(a + 1) * 128],
                                   zrS[:, :], start=True, stop=True), "pe")
                    R.tensor(t_zbc)
                    pe_zbc = C["pe"]
                    zbc_last[0] = pe_zbc

                    def v_zc(v, pe_zbc=pe_zbc):
                        v.wait_ge(S["pe"], pe_zbc)
                        I(v.tensor_copy(out=zbsv[:, :], in_=psZB[:, :]), "vec")
                    R.vector(v_zc)
                    vzc_prev[0] = C["vec"]

                    def v_oa(v, a=a, i=i, pe_o=pe_o):
                        v.wait_ge(S["pe"], pe_o)
                        VW(v)
                        if i >= 1 and a < 2:
                            v.wait_ge(S["pe"], sn[i - 1]["wout"])
                        I(v.tensor_tensor(
                            out=oaT[:, a * CHT:(a + 1) * CHT],
                            in0=psO[a % 2][:, :], in1=zbsv[:, :],
                            op=ALU.mult), "vec")
                    R.vector(v_oa)
                    oa_done.append(C["vec"])
                s_now["q_rd"] = C["pe"]
                s_now["oa_last2"] = {0: oa_done[-2], 1: oa_done[-1]}
                v_oal = C["vec"]

                x1_parts = []
                for dot in range(NDC):
                    def t_y(t, dot=dot, v_oal=v_oal, i=i):
                        t.wait_ge(S["vec"], v_oal)
                        if dot >= 2:
                            t.wait_ge(S["vec"], x1_parts[dot - 2])
                        elif i >= 1:
                            t.wait_ge(S["vec"], sn[i - 1]["x1_last2"][dot])
                        for ic in range(NDC):
                            I(t.matmul(psY[dot % 2][:, :],
                                       wout_sb[:, ic * D + dot * 128:
                                               ic * D + dot * 128 + 128],
                                       oaT[:, ic * CHT:(ic + 1) * CHT],
                                       start=(ic == 0), stop=(ic == NDC - 1)),
                              "pe")
                    R.tensor(t_y)
                    pe_y = C["pe"]

                    def v_x1(v, i=i, dot=dot, bi=bi, pe_y=pe_y):
                        v.wait_ge(S["pe"], pe_y)
                        if dot == 0 and i >= 1:
                            v.wait_ge(S["st"], sn[i - 1]["xl2st"])
                        I(v.scalar_tensor_tensor(
                            out=x1blk[:, dot * CHT:(dot + 1) * CHT],
                            in0=psY[dot % 2][:, :],
                            scalar=bout_sb[:, dot:dot + 1],
                            in1=xblk[bi][:, dot * CHT:(dot + 1) * CHT],
                            op0=ALU.add, op1=ALU.add), "vec")
                    R.vector(v_x1)
                    x1_parts.append(C["vec"])
                s_now["wout"] = C["pe"]
                s_now["x1_last2"] = {0: x1_parts[-2], 1: x1_parts[-1]}
                s_now["x_rd"] = C["vec"]
                v_x1c = C["vec"]

                def s_sq2(s, v_x1c=v_x1c, i=i):
                    s.wait_ge(S["vec"], v_x1c)
                    if i >= 1:
                        s.wait_ge(S["pe"], sn[i - 1]["stats2"])
                        s.wait_ge(S["vec"], sn[i - 1]["ln2"])
                    I(s.activation(out=sqblk[:, :], in_=x1blk[:, :],
                                   func=AF.Square, bias=0.0, scale=1.0), "sc")
                R.scalar(s_sq2)
                sc_sq2 = C["sc"]

                def t_st2(t, sc_sq2=sc_sq2, i=i):
                    t.wait_ge(S["sc"], sc_sq2)
                    if i >= 1:
                        t.wait_ge(S["vec"], sn[i - 1]["bce2"])
                    for a in range(NDC):
                        I(t.matmul(psA[0:1, :], onesk[:, :],
                                   x1blk[:, a * CHT:(a + 1) * CHT],
                                   start=(a == 0), stop=(a == NDC - 1)), "pe")
                    for a in range(NDC):
                        I(t.matmul(psB[0:1, :], onesk[:, :],
                                   sqblk[:, a * CHT:(a + 1) * CHT],
                                   start=(a == 0), stop=(a == NDC - 1)), "pe")
                R.tensor(t_st2)
                s_now["stats2"] = C["pe"]
                pe_st2 = C["pe"]

                def v_rm(v, pe_st2=pe_st2):
                    v.wait_ge(S["pe"], pe_st2)
                    I(v.tensor_scalar(out=mrow[:, :], in0=psA[0:1, :],
                                      scalar1=1.0 / D, scalar2=None,
                                      op0=ALU.mult), "vec")
                    I(v.tensor_scalar(out=m2row[:, :], in0=psB[0:1, :],
                                      scalar1=1.0 / D, scalar2=None,
                                      op0=ALU.mult), "vec")
                R.vector(v_rm)
                s_now["rm2"] = C["vec"]

                def v_rm2(v):
                    VW(v)
                    I(v.tensor_tensor(out=trow[:, :], in0=mrow[:, :],
                                      in1=mrow[:, :], op=ALU.mult), "vec")
                    VW(v)
                    I(v.tensor_tensor(out=vrowt[:, :], in0=m2row[:, :],
                                      in1=trow[:, :], op=ALU.subtract), "vec")
                    VW(v)
                    I(v.tensor_scalar(out=trow[:, :], in0=vrowt[:, :],
                                      scalar1=1e-5, scalar2=None, op0=ALU.add),
                      "vec")
                    VW(v)
                    I(v.reciprocal(out=vrowt[:, :], in_=trow[:, :]), "vec")
                R.vector(v_rm2)
                v_rm2c = C["vec"]

                def s_rstd(s, v_rm2c=v_rm2c):
                    s.wait_ge(S["vec"], v_rm2c)
                    I(s.activation(out=rrow[:, :], in_=vrowt[:, :], func=AF.Sqrt,
                                   bias=0.0, scale=1.0), "sc")
                R.scalar(s_rstd)
                sc_rstd = C["sc"]

                def t_bc2(t, sc_rstd=sc_rstd, i=i):
                    t.wait_ge(S["sc"], sc_rstd)
                    if i >= 1:
                        t.wait_ge(S["vec"], sn[i - 1]["bce2"])
                    I(t.matmul(psA[:, :], ones_m[:, :], mrow[:, :],
                               start=True, stop=True), "pe")
                    I(t.matmul(psB[:, :], ones_m[:, :], rrow[:, :],
                               start=True, stop=True), "pe")
                R.tensor(t_bc2)
                pe_bc2 = C["pe"]

                def v_bce2(v, pe_bc2=pe_bc2):
                    v.wait_ge(S["pe"], pe_bc2)
                    I(v.tensor_copy(out=mBs[:, :], in_=psA[:, :]), "vec")
                    I(v.tensor_copy(out=rBs[:, :], in_=psB[:, :]), "vec")
                R.vector(v_bce2)
                s_now["bce2"] = C["vec"]

                def v_ln2(v, i=i, bi=bi):
                    if i >= 2:
                        v.wait_ge(S["st"], sn[i - 2]["xl2st"])
                    VW(v)
                    for a in range(NDC):
                        sl = slice(a * CHT, (a + 1) * CHT)
                        I(v.tensor_tensor(out=tblk[:, sl], in0=x1blk[:, sl],
                                          in1=mBs[:, :], op=ALU.subtract), "vec")
                        VW(v)
                        I(v.tensor_tensor(out=xl2[bi][:, sl], in0=tblk[:, sl],
                                          in1=rBs[:, :], op=ALU.mult), "vec")
                    VW(v)
                    I(v.reduce_max(out=accs[1][:, i:i + 1], in_=xl2[bi][:, :],
                                   axis=AX.X, apply_absolute_value=True), "vec")
                R.vector(v_ln2)
                s_now["ln2"] = C["vec"]
                v_ln2c = C["vec"]

                def g_st2(g, i=i, bi=bi, v_x1c=v_x1c, v_ln2c=v_ln2c):
                    g.wait_ge(S["vec"], v_ln2c)
                    I(g.dma_start(out=x1_s[:, i * NDC * CHT:(i + 1) * NDC * CHT],
                                  in_=x1blk[:, :]), "st", 16)
                    I(g.dma_start(out=xl2_s[:, i * NDC * CHT:(i + 1) * NDC * CHT],
                                  in_=xl2[bi][:, :]), "st", 16)
                R.gpsimd(g_st2)
                s_now["xl2st"] = C["st"]
                sn.append(s_now)
                R.flush(blk)

            def v_fold1(v):
                VW(v)
                I(v.reduce_max(out=fold[:, 0:1], in_=accs[1][:, :], axis=AX.X),
                  "vec")
            R.vector(v_fold1)
            v_f1 = C["vec"]

            def g_cc1(g, v_f1=v_f1):
                g.wait_ge(S["vec"], v_f1)
                I(g.dma_start(out=cc_in.ap(), in_=fold[:, :]), "st", 16)
                g.wait_ge(S["st"], C["st"])
                I(g.collective_compute(
                    "AllReduce", ALU.max, replica_groups=[list(range(B))],
                    ins=[cc_in.ap()], outs=[cc_out.ap()]), "cc")
                g.wait_ge(S["cc"], C["cc"])
                I(g.dma_start(
                    out=brow[:, :],
                    in_=cc_out.ap().rearrange("p f -> (p f)")), "st", 16)
            R.gpsimd(g_cc1)
            st_cc1 = C["st"]

            def v_s1(v, st_cc1=st_cc1):
                v.wait_ge(S["st"], st_cc1)
                VW(v)
                I(v.reduce_max(out=srow[:, 0:4],
                               in_=brow.ap().rearrange("o (p f) -> o f p", f=4),
                               axis=AX.X), "vec")
                VW(v)
                I(v.tensor_scalar(out=srow[:, 4:5], in0=srow[0:1, 0:1],
                                  scalar1=1.0 / 32767.0, scalar2=1e-12,
                                  op0=ALU.mult, op1=ALU.max), "vec")
                VW(v)
                I(v.reciprocal(out=srow[:, 5:6], in_=srow[0:1, 4:5]), "vec")
            R.vector(v_s1)
            v_s1c = C["vec"]

            def t_bcs1b(t, v_s1c=v_s1c):
                t.wait_ge(S["vec"], v_s1c)
                I(t.matmul(psY[0][:, 0:1], ones_m[:, :], srow[:, 5:6],
                           start=True, stop=True), "pe")
            R.tensor(t_bcs1b)
            pe_b1 = C["pe"]

            def s_cst1(s, pe_b1=pe_b1):
                s.wait_ge(S["pe"], pe_b1)
                I(s.copy(out=cst[:, C_INV1:C_INV1 + 1], in_=psY[0][:, 0:1]), "sc")
            R.scalar(s_cst1)
            a2_done = dict(C)
            R.flush(blk)

        stack_wo.close()
        stack_wq.close()

        stack_wu = ExitStack()
        wuT = stack_wu.enter_context(nc.sbuf_tensor("wuT", [128, NDC * DI], BF16))
        stack_wg = ExitStack()
        wgT = stack_wg.enter_context(nc.sbuf_tensor("wgT", [128, NDC * DI], BF16))

        def emit_up_or_gate(R, buf, is_gate, acc_idx, scr_out, wT, f_scale_slot,
                            start_waits, hook=None):
            sn = []
            nxl = len(buf["xl2ld"])
            NG = NJC // 4
            for i in range(NCH):
                bi = i % nxl
                s_now = {}

                def sy_l(sy, i=i, bi=bi, nxl=nxl):
                    if i >= nxl:
                        sy.wait_ge(S["sc"], sn[i - nxl]["xq_sc"])
                    else:
                        for k, vv in start_waits.items():
                            sy.wait_ge(S[k], vv)
                    I(sy.dma_start(
                        out=buf["xl2ld"][bi][:, :],
                        in_=xl2_s[:, i * NDC * CHT:(i + 1) * NDC * CHT]), "ld", 16)
                R.sync(sy_l)
                ld_l = C["ld"]

                def s_xq(s, i=i, bi=bi, ld_l=ld_l):
                    s.wait_ge(S["ld"], ld_l)
                    if i >= 1:
                        s.wait_ge(S["vec"], sn[i - 1]["xqt_v"])
                    I(s.activation(out=buf["xqi"][:, :], in_=buf["xl2ld"][bi][:, :],
                                   func=AF.Relu, bias=cst[:, C_RB:C_RB + 1],
                                   scale=cst[:, C_INV1:C_INV1 + 1]), "sc")
                R.scalar(s_xq)
                s_now["xq_sc"] = C["sc"]
                sc_xq = C["sc"]

                def v_xq(v, sc_xq=sc_xq, i=i):
                    v.wait_ge(S["sc"], sc_xq)
                    if i >= 1:
                        v.wait_ge(S["pe"], sn[i - 1]["mm"])
                    I(v.tensor_scalar(out=buf["xqt"][:, :], in0=buf["xqi"][:, :],
                                      scalar1=DEB, scalar2=None,
                                      op0=ALU.subtract), "vec")
                R.vector(v_xq)
                s_now["xqt_v"] = C["vec"]
                v_xqc = C["vec"]

                ev_list = []
                rd_list = []
                for g4 in range(NG):
                    pg = buf["psF"][g4 % 2]

                    def t_mm(t, i=i, g4=g4, v_xqc=v_xqc, wT=wT, pg=pg):
                        t.wait_ge(S["vec"], v_xqc)
                        if g4 >= 2:
                            t.wait_ge(S["sc"], ev_list[g4 - 2])
                        elif i >= 1:
                            t.wait_ge(S["sc"], sn[i - 1]["ev_last2"][g4])
                        for jj in range(4):
                            jt = g4 * 4 + jj
                            for ic in range(NDC):
                                I(t.matmul(pg[:, jj * CHT:(jj + 1) * CHT],
                                           wT[:, ic * DI + jt * 128:
                                              ic * DI + jt * 128 + 128],
                                           buf["xqt"][:, ic * CHT:(ic + 1) * CHT],
                                           start=(ic == 0), stop=(ic == NDC - 1)),
                                  "pe")
                    R.tensor(t_mm)
                    pe_mm = C["pe"]

                    def s_ev(s, i=i, g4=g4, pe_mm=pe_mm, pg=pg):
                        s.wait_ge(S["pe"], pe_mm)
                        if i > 0 or g4 > 1:
                            s.wait_ge(S["st"], ev_st[g4 % 2][0])
                            if len(rd_list) >= 2:
                                s.wait_ge(S["vec"], rd_list[-2])
                            elif i > 0:
                                s.wait_ge(S["vec"], sn[i - 1]["rd_last2"][g4 % 2])
                        I(s.activation(
                            out=buf["evb"][g4 % 2][:, :],
                            in_=pg[:, :], func=AF.Copy,
                            bias=0.0, scale=1.0 / 16.0), "sc")
                    R.scalar(s_ev)
                    ev_list.append(C["sc"])

                    def v_red(v, g4=g4, ev=ev_list[-1]):
                        v.wait_ge(S["sc"], ev)
                        I(v.reduce_max(out=jtmp[:, g4:g4 + 1],
                                       in_=buf["evb"][g4 % 2][:, :], axis=AX.X,
                                       apply_absolute_value=True), "vec")
                    R.vector(v_red)
                    rd_list.append(C["vec"])

                    def g_stg(g, i=i, g4=g4, el=ev_list[-1], scr_out=scr_out):
                        g.wait_ge(S["sc"], el)
                        I(g.dma_start(
                            out=scr_out[:, (i * NJC + g4 * 4) * CHT:
                                        (i * NJC + (g4 + 1) * 4) * CHT],
                            in_=buf["evb"][g4 % 2][:, :]), "st", 16)
                    R.gpsimd(g_stg)
                    if i == 0 and g4 < 2:
                        ev_st = {0: [0], 1: [0]} if g4 == 0 else ev_st
                    ev_st[g4 % 2] = [C["st"]]
                    if hook is not None and i >= 1 and 1 <= g4 <= 4:
                        hook(i - 1, g4 - 1)
                s_now["mm"] = C["pe"]
                s_now["ev_last2"] = {0: ev_list[-2], 1: ev_list[-1]}
                s_now["rd_last2"] = {0: rd_list[-2], 1: rd_list[-1]}

                def v_fold(v, i=i, acc_idx=acc_idx, rl=rd_list[-1], NG=NG):
                    v.wait_ge(S["vec"], rl)
                    VW(v)
                    I(v.reduce_max(out=accs[acc_idx][:, i:i + 1],
                                   in_=jtmp[:, 0:NG], axis=AX.X), "vec")
                R.vector(v_fold)

                s_now["sto"] = C["st"]
                if hook is not None and i >= 1:
                    hook(i - 1, "end")
                sn.append(s_now)
                R.flush(blk)
            if hook is not None:
                for st_ in (0, 1, 2, 3, "end"):
                    hook(NCH - 1, st_)
                R.flush(blk)
            return sn

        with ExitStack() as pf1, nc.Block() as blk:
            R = Rec(blk)
            buf = {
                "xl2ld": [pf1.enter_context(
                    nc.sbuf_tensor(f"xl2ld{j}", [128, NDC * CHT], FP16))
                    for j in range(2)],
                "xqi": pf1.enter_context(
                    nc.sbuf_tensor("xqi", [128, NDC * CHT], I32)),
                "xqt": pf1.enter_context(
                    nc.sbuf_tensor("xqt", [128, NDC * CHT], BF16)),
                "evb": [pf1.enter_context(
                    nc.sbuf_tensor(f"gevb{j}", [128, 4 * CHT], FP16))
                    for j in range(2)],
                "psF": [pf1.enter_context(nc.psum_tensor(f"psF{j}", [128, 4 * CHT], F32))
                        for j in range(2)],
            }
            psb2 = buf["psF"][0]

            def sy_wg(sy):
                sy.wait_ge(S["sc"], a2_done["sc"])
                I(sy.dma_start(out=wgT[:, :], in_=wgateT_s[:, :]), "ld", 16)
                I(sy.dma_start(out=wuT[:, :], in_=wupT_s[:, :]), "ld", 16)
            R.sync(sy_wg)
            ld_wg = C["ld"]

            def t_wait(t, ld_wg=ld_wg):
                t.wait_ge(S["ld"], ld_wg)
            R.tensor(t_wait)

            sn_g = emit_up_or_gate(R, buf, True, 3, gate_s, wgT, None,
                                   {"sc": a2_done["sc"], "vec": a2_done["vec"],
                                    "st": a2_done["st"]})

            def v_fold3(v):
                VW(v)
                I(v.reduce_max(out=fold[:, 0:1], in_=accs[3][:, :], axis=AX.X),
                  "vec")
            R.vector(v_fold3)
            v_f3 = C["vec"]

            def g_cc3(g, v_f3=v_f3):
                g.wait_ge(S["vec"], v_f3)
                I(g.dma_start(out=cc_in.ap(), in_=fold[:, :]), "st", 16)
                g.wait_ge(S["st"], C["st"])
                I(g.collective_compute(
                    "AllReduce", ALU.max, replica_groups=[list(range(B))],
                    ins=[cc_in.ap()], outs=[cc_out.ap()]), "cc")
                g.wait_ge(S["cc"], C["cc"])
                I(g.dma_start(out=brow[:, :],
                              in_=cc_out.ap().rearrange("p f -> (p f)")),
                  "st", 16)
            R.gpsimd(g_cc3)
            st_cc3 = C["st"]

            def v_s3(v, st_cc3=st_cc3):
                v.wait_ge(S["st"], st_cc3)
                VW(v)
                I(v.reduce_max(out=srow[:, 0:4],
                               in_=brow.ap().rearrange("o (p f) -> o f p", f=4),
                               axis=AX.X), "vec")
                VW(v)
                I(v.tensor_scalar(out=srow[:, 10:11], in0=srow[0:1, 4:5],
                                  scalar1=16.0, scalar2=None, op0=ALU.mult), "vec")
                VW(v)
                I(v.tensor_tensor(out=srow[:, 6:7], in0=srow[0:1, 0:1],
                                  in1=srow[0:1, 10:11], op=ALU.mult), "vec")
                VW(v)
                I(v.tensor_scalar(out=srow[:, 7:8], in0=srow[0:1, 6:7],
                                  scalar1=1.0 / 32767.0, scalar2=1e-12,
                                  op0=ALU.mult, op1=ALU.max), "vec")
                VW(v)
                I(v.tensor_scalar(out=srow[:, 8:9], in0=srow[0:1, 7:8],
                                  scalar1=-DEB, scalar2=None, op0=ALU.mult), "vec")
                VW(v)
                I(v.reciprocal(out=srow[:, 9:10], in_=srow[0:1, 7:8]), "vec")
                VW(v)
                I(v.tensor_scalar(out=srow[:, 10:11], in0=srow[0:1, 4:5],
                                  scalar1=16.0, scalar2=None, op0=ALU.mult), "vec")
                VW(v)
                I(v.tensor_tensor(out=srow[:, 11:12], in0=srow[0:1, 10:11],
                                  in1=srow[0:1, 9:10], op=ALU.mult), "vec")
            R.vector(v_s3)
            v_s3c = C["vec"]

            def t_bc3(t, v_s3c=v_s3c, sc_all=C["sc"]):
                t.wait_ge(S["vec"], v_s3c)
                t.wait_ge(S["sc"], sc_all)
                I(t.matmul(psb2[:, 0:1], ones_m[:, :], srow[:, 7:8],
                           start=True, stop=True), "pe")
                I(t.matmul(psb2[:, 1:2], ones_m[:, :], srow[:, 8:9],
                           start=True, stop=True), "pe")
                I(t.matmul(psb2[:, 2:3], ones_m[:, :], srow[:, 11:12],
                           start=True, stop=True), "pe")
            R.tensor(t_bc3)
            pe_b3 = C["pe"]

            def s_cst3(s, pe_b3=pe_b3):
                s.wait_ge(S["pe"], pe_b3)
                I(s.copy(out=cst[:, C_SC3:C_SC3 + 1], in_=psb2[:, 0:1]), "sc")
                I(s.copy(out=cst[:, C_NB3:C_NB3 + 1], in_=psb2[:, 1:2]), "sc")
                I(s.copy(out=cst[:, C_F3:C_F3 + 1], in_=psb2[:, 2:3]), "sc")
            R.scalar(s_cst3)
            f1a_done = dict(C)
            R.flush(blk)
        stack_wg.close()

        with ExitStack() as pf2, nc.Block() as blk:
            R = Rec(blk)
            buf = {
                "xl2ld": [pf2.enter_context(
                    nc.sbuf_tensor("xl2le0", [128, NDC * CHT], FP16))],
                "xqi": pf2.enter_context(
                    nc.sbuf_tensor("xqi2", [128, NDC * CHT], I32)),
                "xqt": pf2.enter_context(
                    nc.sbuf_tensor("xqt2", [128, NDC * CHT], BF16)),
                "evb": [pf2.enter_context(
                    nc.sbuf_tensor(f"uevb{j}", [128, 4 * CHT], FP16))
                    for j in range(2)],
                "psF": [pf2.enter_context(nc.psum_tensor(f"psF2{j}", [128, 4 * CHT], F32))
                        for j in range(2)],
            }
            gld = [pf2.enter_context(nc.sbuf_tensor("gld0", [128, DI], FP16))]
            t3 = [pf2.enter_context(nc.sbuf_tensor(f"t3_{j}", [128, DI], I32))
                  for j in range(2)]
            sgt = pf2.enter_context(nc.sbuf_tensor("sgt", [128, DI], F32))
            u4 = [pf2.enter_context(nc.sbuf_tensor(f"u4_{j}", [128, DI], I32))
                  for j in range(2)]
            sqc = pf2.enter_context(nc.sbuf_tensor("sqc", [128, DI], F32))
            g2b = [pf2.enter_context(nc.sbuf_tensor(f"g2b{j}", [128, DI], BF16))
                   for j in range(2)]
            g4t = pf2.enter_context(nc.sbuf_tensor("g4t", [128, 4], F32))

            sn2 = []
            g2_stl = {0: [0], 1: [0]}
            g2_rd = {0: [0], 1: [0]}
            g_sc_rd = {0: [0]}

            f2_subv = {}

            def f2_hook(i, stage):
                if stage == "end":
                    sub_v = f2_subv[i]

                    def v_f4(v, i=i, lv=sub_v[-1]):
                        v.wait_ge(S["vec"], lv)
                        VW(v)
                        I(v.reduce_max(out=accs[4][:, i:i + 1], in_=g4t[:, :],
                                       axis=AX.X), "vec")
                    R.vector(v_f4)
                    sn2.append({"g2v": C["vec"], "g2st": C["st"]})
                    return
                sub_v = f2_subv.setdefault(i, [])
                if True:
                    sub = stage
                    sb = sub % 2

                    def sy_g(sy, i=i, sub=sub, sb=sb):
                        if i > 0 or sub > 0:
                            sy.wait_ge(S["sc"], g_sc_rd[0][0])
                        else:
                            sy.wait_ge(S["sc"], f1a_done["sc"])
                        I(sy.dma_start(
                            out=gld[0][:, :],
                            in_=gate_s[:, (i * NJC + sub * (NJC // 4)) * CHT:
                                       (i * NJC + (sub + 1) * (NJC // 4)) * CHT]),
                          "ld", 16)
                    R.sync(sy_g)
                    ld_g = C["ld"]

                    def s_ch(s, i=i, sub=sub, sb=sb, ld_g=ld_g):
                        s.wait_ge(S["ld"], ld_g)
                        if i > 0 or sub > 1:
                            s.wait_ge(S["vec"], g2_rd[sb][0])
                        I(s.activation(out=t3[sb][:, :], in_=gld[0][:, :],
                                       func=AF.Relu,
                                       bias=cst[:, C_RB:C_RB + 1],
                                       scale=cst[:, C_F3:C_F3 + 1]), "sc")
                        SW(s)
                        I(s.activation(out=sgt[:, :], in_=t3[sb][:, :],
                                       func=AF.Sigmoid,
                                       bias=cst[:, C_NB3:C_NB3 + 1],
                                       scale=cst[:, C_SC3:C_SC3 + 1]), "sc")
                        SW(s)
                        I(s.activation(out=u4[sb][:, :], in_=sgt[:, :],
                                       func=AF.Relu,
                                       bias=cst[:, C_RB:C_RB + 1],
                                       scale=cst[:, C_65536:C_65536 + 1]), "sc")
                    R.scalar(s_ch)
                    sc_ch = C["sc"]
                    g_sc_rd[0] = [C["sc"]]

                    def v_ch(v, i=i, sub=sub, sb=sb, sc_ch=sc_ch):
                        v.wait_ge(S["sc"], sc_ch)
                        if i > 0 or sub > 1:
                            v.wait_ge(S["st"], g2_stl[sb][0])
                        I(v.tensor_scalar(out=sqc[:, :], in0=u4[sb][:, :],
                                          scalar1=DEB + 65535.0, scalar2=DEB,
                                          op0=ALU.min, op1=ALU.subtract), "vec")
                        VW(v)
                        I(v.scalar_tensor_tensor(out=g2b[sb][:, :],
                                                 in0=t3[sb][:, :],
                                                 scalar=DEB, in1=sqc[:, :],
                                                 op0=ALU.subtract, op1=ALU.mult),
                          "vec")
                        VW(v)
                        I(v.reduce_max(out=g4t[:, sub:sub + 1],
                                       in_=g2b[sb][:, :],
                                       axis=AX.X, apply_absolute_value=True),
                          "vec")
                    R.vector(v_ch)
                    sub_v.append(C["vec"])
                    g2_rd[sb] = [C["vec"]]

                    def g_g2(g, i=i, sub=sub, sb=sb, vv=sub_v[-1]):
                        g.wait_ge(S["vec"], vv)
                        I(g.dma_start(
                            out=g2_s[:, (i * NJC + sub * (NJC // 4)) * CHT:
                                     (i * NJC + (sub + 1) * (NJC // 4)) * CHT],
                            in_=g2b[sb][:, :]), "st", 16)
                    R.gpsimd(g_g2)
                    g2_stl[sb] = [C["st"]]

            sn_u = emit_up_or_gate(R, buf, False, 2, up_s, wuT, None,
                                   {"sc": f1a_done["sc"], "vec": f1a_done["vec"],
                                    "st": f1a_done["st"]}, hook=f2_hook)

            def v_fold24(v, lg=sn2[-1]["g2v"]):
                v.wait_ge(S["vec"], lg)
                VW(v)
                I(v.reduce_max(out=fold[:, 0:1], in_=accs[2][:, :], axis=AX.X),
                  "vec")
                I(v.reduce_max(out=fold[:, 1:2], in_=accs[4][:, :], axis=AX.X),
                  "vec")
            R.vector(v_fold24)
            v_f24 = C["vec"]

            def g_cc24(g, v_f24=v_f24):
                g.wait_ge(S["vec"], v_f24)
                I(g.dma_start(out=cc_in.ap(), in_=fold[:, :]), "st", 16)
                g.wait_ge(S["st"], C["st"])
                I(g.collective_compute(
                    "AllReduce", ALU.max, replica_groups=[list(range(B))],
                    ins=[cc_in.ap()], outs=[cc_out.ap()]), "cc")
                g.wait_ge(S["cc"], C["cc"])
                I(g.dma_start(out=brow[:, :],
                              in_=cc_out.ap().rearrange("p f -> (p f)")),
                  "st", 16)
            R.gpsimd(g_cc24)
            st_cc24 = C["st"]

            def v_s24(v, st_cc24=st_cc24):
                v.wait_ge(S["st"], st_cc24)
                VW(v)
                I(v.reduce_max(out=srow[:, 0:4],
                               in_=brow.ap().rearrange("o (p f) -> o f p", f=4),
                               axis=AX.X), "vec")
                VW(v)
                I(v.tensor_scalar(out=srow[:, 10:11], in0=srow[0:1, 4:5],
                                  scalar1=16.0, scalar2=None, op0=ALU.mult), "vec")
                VW(v)
                I(v.tensor_tensor(out=srow[:, 6:7], in0=srow[0:1, 0:1],
                                  in1=srow[0:1, 10:11], op=ALU.mult), "vec")
                VW(v)
                I(v.tensor_scalar(out=srow[:, 12:13], in0=srow[0:1, 6:7],
                                  scalar1=1.0 / 32767.0, scalar2=1e-12,
                                  op0=ALU.mult, op1=ALU.max), "vec")
                VW(v)
                I(v.reciprocal(out=srow[:, 9:10], in_=srow[0:1, 12:13]), "vec")
                VW(v)
                I(v.tensor_scalar(out=srow[:, 10:11], in0=srow[0:1, 4:5],
                                  scalar1=16.0, scalar2=None, op0=ALU.mult), "vec")
                VW(v)
                I(v.tensor_tensor(out=srow[:, 11:12], in0=srow[0:1, 10:11],
                                  in1=srow[0:1, 9:10], op=ALU.mult), "vec")
                VW(v)
                I(v.tensor_scalar(out=srow[:, 13:14], in0=cst[0:1, C_SC3:C_SC3 + 1],
                                  scalar1=1.0 / 65536.0, scalar2=None,
                                  op0=ALU.mult), "vec")
                VW(v)
                I(v.tensor_tensor(out=srow[:, 6:7], in0=srow[0:1, 1:2],
                                  in1=srow[0:1, 13:14], op=ALU.mult), "vec")
                VW(v)
                I(v.tensor_scalar(out=srow[:, 14:15], in0=srow[0:1, 6:7],
                                  scalar1=1.0 / 32767.0, scalar2=1e-12,
                                  op0=ALU.mult, op1=ALU.max), "vec")
                VW(v)
                I(v.reciprocal(out=srow[:, 9:10], in_=srow[0:1, 14:15]), "vec")
                VW(v)
                I(v.tensor_tensor(out=srow[:, 15:16], in0=srow[0:1, 13:14],
                                  in1=srow[0:1, 9:10], op=ALU.mult), "vec")
            R.vector(v_s24)
            v_s24c = C["vec"]

            def t_bc24(t, v_s24c=v_s24c, sc_all=C["sc"]):
                t.wait_ge(S["vec"], v_s24c)
                t.wait_ge(S["sc"], sc_all)
                I(t.matmul(psb2[:, 0:1], ones_m[:, :], srow[:, 11:12],
                           start=True, stop=True), "pe")
                I(t.matmul(psb2[:, 1:2], ones_m[:, :], srow[:, 15:16],
                           start=True, stop=True), "pe")
            R.tensor(t_bc24)
            pe_b24 = C["pe"]

            def s_cst24(s, pe_b24=pe_b24):
                s.wait_ge(S["pe"], pe_b24)
                I(s.copy(out=cst[:, C_F2:C_F2 + 1], in_=psb2[:, 0:1]), "sc")
                I(s.copy(out=cst[:, C_F4:C_F4 + 1], in_=psb2[:, 1:2]), "sc")
            R.scalar(s_cst24)
            f2_done = dict(C)
            R.flush(blk)
        stack_wu.close()

        with ExitStack() as pf3, nc.Block() as blk:
            R = Rec(blk)
            uld = [pf3.enter_context(nc.sbuf_tensor(f"uld{j}", [128, DI], FP16))
                   for j in range(2)]
            g2ld = [pf3.enter_context(nc.sbuf_tensor(f"g2l{j}", [128, DI], BF16))
                    for j in range(2)]
            t2 = [pf3.enter_context(nc.sbuf_tensor(f"t2_{j}", [128, DI], I32))
                  for j in range(2)]
            upq = [pf3.enter_context(nc.sbuf_tensor(f"upq{j}", [128, DI], F32))
                   for j in range(2)]
            t4 = [pf3.enter_context(nc.sbuf_tensor(f"t4_{j}", [128, DI], I32))
                  for j in range(2)]
            ob = pf3.enter_context(nc.sbuf_tensor("ob", [128, NJC * CHT], FP16))
            o5t = pf3.enter_context(nc.sbuf_tensor("o5t", [128, 4], F32))
            psb3 = pf3.enter_context(nc.psum_tensor("psb3", [128, 8], F32))

            sn3 = []
            for i in range(NCH):
                bi = i % 2
                s_now = {}

                sub_v = []
                for sub in range(4):
                    sl = slice(sub * DI, (sub + 1) * DI)
                    sb = sub % 2

                    def sy_u(sy, i=i, sub=sub, sb=sb):
                        if i > 0 or sub > 1:
                            sy.wait_ge(S["sc"], u_rd3[sb][0])
                        elif i == 0 and sub == 0:
                            sy.wait_ge(S["sc"], f2_done["sc"])
                            sy.wait_ge(S["st"], f2_done["st"])
                            sy.wait_ge(S["vec"], f2_done["vec"])
                        off = (i * NJC + sub * (NJC // 4)) * CHT
                        end = (i * NJC + (sub + 1) * (NJC // 4)) * CHT
                        I(sy.dma_start(out=uld[sb][:, :],
                                       in_=up_s[:, off:end]), "ld", 16)
                        I(sy.dma_start(out=g2ld[sb][:, :],
                                       in_=g2_s[:, off:end]), "ld", 16)
                    R.sync(sy_u)
                    ld_u = C["ld"]

                    def s_c2(s, i=i, sub=sub, sb=sb, sl=sl, ld_u=ld_u, bi=bi):
                        s.wait_ge(S["ld"], ld_u)
                        if i > 0 or sub > 1:
                            s.wait_ge(S["vec"], ob_rd[sb][0])
                        I(s.activation(out=t2[sb][:, :], in_=uld[sb][:, :],
                                       func=AF.Relu, bias=cst[:, C_RB:C_RB + 1],
                                       scale=cst[:, C_F2:C_F2 + 1]), "sc")
                        I(s.activation(out=t4[sb][:, :], in_=g2ld[sb][:, :],
                                       func=AF.Relu, bias=cst[:, C_RB:C_RB + 1],
                                       scale=cst[:, C_F4:C_F4 + 1]), "sc")
                        SW(s)
                        I(s.activation(out=upq[sb][:, :], in_=t2[sb][:, :],
                                       func=AF.Copy, bias=-0.5,
                                       scale=1.0 / 65536.0), "sc")
                    R.scalar(s_c2)
                    sc_c2 = C["sc"]
                    if i == 0 and sub == 0:
                        u_rd3 = {0: [0], 1: [0]}
                    u_rd3[sb] = [C["sc"]]

                    def v_c2(v, i=i, sub=sub, sb=sb, sl=sl, sc_c2=sc_c2):
                        v.wait_ge(S["sc"], sc_c2)
                        if sub == 0 and i >= 1:
                            v.wait_ge(S["st"], sn3[i - 1]["obst"])
                        I(v.scalar_tensor_tensor(out=ob[:, sl], in0=t4[sb][:, :],
                                                 scalar=DEB, in1=upq[sb][:, :],
                                                 op0=ALU.subtract, op1=ALU.mult),
                          "vec")
                        VW(v)
                        I(v.reduce_max(out=o5t[:, sub:sub + 1], in_=ob[:, sl],
                                       axis=AX.X, apply_absolute_value=True),
                          "vec")
                    R.vector(v_c2)
                    sub_v.append(C["vec"])
                    if i == 0 and sub < 2:
                        ob_rd = {0: [0], 1: [0]} if sub == 0 else ob_rd
                    ob_rd[sb] = [C["vec"]]
                s_now["rd"] = C["sc"]

                def v_f5(v, i=i, lv=sub_v[-1]):
                    v.wait_ge(S["vec"], lv)
                    VW(v)
                    I(v.reduce_max(out=accs[5][:, i:i + 1], in_=o5t[:, :],
                                   axis=AX.X), "vec")
                R.vector(v_f5)
                v_ob = C["vec"]

                def g_ob(g, i=i, v_ob=v_ob):
                    g.wait_ge(S["vec"], v_ob)
                    I(g.dma_start(out=o_s[:, i * NJC * CHT:(i + 1) * NJC * CHT],
                                  in_=ob[:, :]), "st", 16)
                R.gpsimd(g_ob)
                s_now["obst"] = C["st"]
                sn3.append(s_now)
                R.flush(blk)

            def v_fold5(v):
                VW(v)
                I(v.reduce_max(out=fold[:, 0:1], in_=accs[5][:, :], axis=AX.X),
                  "vec")
            R.vector(v_fold5)
            v_f5c = C["vec"]

            def g_cc5(g, v_f5c=v_f5c):
                g.wait_ge(S["vec"], v_f5c)
                I(g.dma_start(out=cc_in.ap(), in_=fold[:, :]), "st", 16)
                g.wait_ge(S["st"], C["st"])
                I(g.collective_compute(
                    "AllReduce", ALU.max, replica_groups=[list(range(B))],
                    ins=[cc_in.ap()], outs=[cc_out.ap()]), "cc")
                g.wait_ge(S["cc"], C["cc"])
                I(g.dma_start(out=brow[:, :],
                              in_=cc_out.ap().rearrange("p f -> (p f)")),
                  "st", 16)
            R.gpsimd(g_cc5)
            st_cc5 = C["st"]

            def v_s5(v, st_cc5=st_cc5):
                v.wait_ge(S["st"], st_cc5)
                VW(v)
                I(v.reduce_max(out=srow[:, 0:4],
                               in_=brow.ap().rearrange("o (p f) -> o f p", f=4),
                               axis=AX.X), "vec")
                VW(v)
                I(v.tensor_tensor(out=srow[:, 5:6], in0=srow[0:1, 12:13],
                                  in1=srow[0:1, 14:15], op=ALU.mult), "vec")
                VW(v)
                I(v.tensor_scalar(out=srow[:, 6:7], in0=srow[0:1, 5:6],
                                  scalar1=65536.0, scalar2=None,
                                  op0=ALU.mult), "vec")
                VW(v)
                I(v.tensor_tensor(out=srow[:, 7:8], in0=srow[0:1, 0:1],
                                  in1=srow[0:1, 6:7], op=ALU.mult), "vec")
                VW(v)
                I(v.tensor_scalar(out=srow[:, 8:9], in0=srow[0:1, 7:8],
                                  scalar1=1.0 / 32767.0, scalar2=1e-12,
                                  op0=ALU.mult, op1=ALU.max), "vec")
                VW(v)
                I(v.reciprocal(out=srow[:, 9:10], in_=srow[0:1, 8:9]), "vec")
                VW(v)
                I(v.tensor_tensor(out=srow[:, 10:11], in0=srow[0:1, 6:7],
                                  in1=srow[0:1, 9:10], op=ALU.mult), "vec")
            R.vector(v_s5)
            v_s5x = C["vec"]

            def t_bc5(t, v_s5x=v_s5x):
                t.wait_ge(S["vec"], v_s5x)
                I(t.matmul(psb3[:, 0:1], ones_m[:, :], srow[:, 10:11],
                           start=True, stop=True), "pe")
                I(t.matmul(psb3[:, 1:2], ones_m[:, :], srow[:, 8:9],
                           start=True, stop=True), "pe")
            R.tensor(t_bc5)
            pe_b5 = C["pe"]

            def s_cst5(s, pe_b5=pe_b5):
                s.wait_ge(S["pe"], pe_b5)
                I(s.copy(out=cst[:, C_F5:C_F5 + 1], in_=psb3[:, 0:1]), "sc")
                I(s.copy(out=cst[:, C_SC5:C_SC5 + 1], in_=psb3[:, 1:2]), "sc")
            R.scalar(s_cst5)
            f3_done = dict(C)
            R.flush(blk)

        stack_wd = ExitStack()
        wdT = stack_wd.enter_context(nc.sbuf_tensor("wdT", [128, NJC * D], BF16))
        with ExitStack() as pf4, nc.Block() as blk:
            R = Rec(blk)
            old = [pf4.enter_context(nc.sbuf_tensor(f"old{j}", [128, DI], FP16))
                   for j in range(2)]
            x1ld = [pf4.enter_context(nc.sbuf_tensor(f"x1l{j}", [128, NDC * CHT], F32))
                    for j in range(2)]
            t5 = pf4.enter_context(nc.sbuf_tensor("t5", [128, DI], I32))
            oqt = [pf4.enter_context(nc.sbuf_tensor(f"oqt{j}", [128, NJC * CHT], BF16))
                   for j in range(2)]
            outb = pf4.enter_context(nc.sbuf_tensor("outb", [128, NDC * CHT], F32))
            psD = [pf4.enter_context(nc.psum_tensor(f"psD{j}", [128, CHT], F32))
                   for j in range(NDC)]

            def sy_wd(sy):
                sy.wait_ge(S["sc"], f3_done["sc"])
                I(sy.dma_start(out=wdT[:, :], in_=wdownT_s[:, :]), "ld", 16)
            R.sync(sy_wd)
            ld_wd = C["ld"]

            sn4 = []
            for i in range(NCH):
                bi = i % 2
                s_now = {}

                def sy_x1(sy, i=i, bi=bi, ld_wd=ld_wd):
                    sy.wait_ge(S["ld"], ld_wd)
                    if i >= 2:
                        sy.wait_ge(S["pe"], sn4[i - 2]["mm"])
                        sy.wait_ge(S["vec"], sn4[i - 2]["res"])
                    else:
                        sy.wait_ge(S["sc"], f3_done["sc"])
                        sy.wait_ge(S["st"], f3_done["st"])
                        sy.wait_ge(S["vec"], f3_done["vec"])
                    I(sy.dma_start(
                        out=x1ld[bi][:, :],
                        in_=x1_s[:, i * NDC * CHT:(i + 1) * NDC * CHT]), "ld", 16)
                R.sync(sy_x1)

                sub_oq = []
                for sub in range(4):
                    sl = slice(sub * DI, (sub + 1) * DI)
                    ob_ = sub % 2

                    def sy_o(sy, i=i, sub=sub, ob_=ob_):
                        if i > 0 or sub > 1:
                            sy.wait_ge(S["sc"], o_rd4[ob_][0])
                        I(sy.dma_start(
                            out=old[ob_][:, :],
                            in_=o_s[:, (i * NJC + sub * (NJC // 4)) * CHT:
                                    (i * NJC + (sub + 1) * (NJC // 4)) * CHT]),
                          "ld", 16)
                    R.sync(sy_o)
                    ld_o = C["ld"]

                    def s_c5(s, sub=sub, sl=sl, ld_o=ld_o, ob_=ob_, i=i):
                        s.wait_ge(S["ld"], ld_o)
                        if sub >= 1:
                            s.wait_ge(S["vec"], sub_oq[sub - 1])
                        elif i >= 1:
                            s.wait_ge(S["vec"], t5_free[0])
                        if i >= 2 and sub == 0:
                            s.wait_ge(S["pe"], sn4[i - 2]["mm"])
                        I(s.activation(out=t5[:, :], in_=old[ob_][:, :],
                                       func=AF.Relu, bias=cst[:, C_RB:C_RB + 1],
                                       scale=cst[:, C_F5:C_F5 + 1]), "sc")
                    R.scalar(s_c5)
                    sc_c5 = C["sc"]
                    if i == 0 and sub == 0:
                        o_rd4 = {0: [0], 1: [0]}
                    o_rd4[ob_] = [C["sc"]]

                    def v_c5(v, sub=sub, sl=sl, sc_c5=sc_c5, bi=bi):
                        v.wait_ge(S["sc"], sc_c5)
                        I(v.tensor_scalar(out=oqt[bi][:, sl], in0=t5[:, :],
                                          scalar1=DEB, scalar2=None,
                                          op0=ALU.subtract), "vec")
                    R.vector(v_c5)
                    sub_oq.append(C["vec"])
                    v_sub = C["vec"]

                    NG6 = NJC // 4
                    if sub == 3:
                        pe3_base = C["pe"]

                    def t_d(t, sub=sub, v_sub=v_sub, i=i, bi=bi, NG6=NG6):
                        t.wait_ge(S["vec"], v_sub)
                        if sub == 0 and i >= 1:
                            t.wait_ge(S["vec"], sn4[i - 1]["res"])
                        for ht in range(NDC):
                            for jc in range(sub * NG6, (sub + 1) * NG6):
                                I(t.matmul(psD[ht][:, :],
                                           wdT[:, jc * D + ht * 128:
                                               jc * D + ht * 128 + 128],
                                           oqt[bi][:, jc * CHT:(jc + 1) * CHT],
                                           start=(sub == 0 and jc == 0),
                                           stop=(sub == 3 and jc == NJC - 1)),
                                  "pe")
                    R.tensor(t_d)
                t5_free = [C["vec"]]

                res_parts = []
                for ht in range(NDC):
                    def v_res(v, ht=ht, bi=bi, i=i,
                              pw=pe3_base + (ht + 1) * (NJC // 4)):
                        v.wait_ge(S["pe"], pw)
                        if ht == 0 and i >= 1:
                            v.wait_ge(S["st"], sn4[i - 1]["ost"])
                        I(v.scalar_tensor_tensor(
                            out=outb[:, ht * CHT:(ht + 1) * CHT],
                            in0=psD[ht][:, :],
                            scalar=cst[:, C_SC5:C_SC5 + 1],
                            in1=x1ld[bi][:, ht * CHT:(ht + 1) * CHT],
                            op0=ALU.mult, op1=ALU.add), "vec")
                    R.vector(v_res)
                    res_parts.append(C["vec"])
                s_now["mm"] = C["pe"]
                s_now["res"] = C["vec"]

                def g_out(g, i=i, rl=res_parts[-1]):
                    g.wait_ge(S["vec"], rl)
                    I(g.dma_start(out=outT[:, i * NDC * CHT:(i + 1) * NDC * CHT],
                                  in_=outb[:, :]), "st", 16)
                R.gpsimd(g_out)
                s_now["ost"] = C["st"]
                sn4.append(s_now)
                R.flush(blk)

            def sy_fin(sy):
                sy.wait_ge(S["st"], C["st"])
            R.sync(sy_fin)
            R.flush(blk)
        stack_wd.close()

    return nc


_NC_CACHE = [None]


def kernel(L0_attn_x, L0_attn_Wqkv, L0_attn_Wout, L0_attn_bout,
           L0_ffn_w_gate, L0_ffn_w_up, L0_ffn_w_down):
    x = np.ascontiguousarray(np.asarray(L0_attn_x, dtype=np.float32))
    wqkv = np.ascontiguousarray(np.asarray(L0_attn_Wqkv, dtype=np.float32))
    wo = np.ascontiguousarray(np.asarray(L0_attn_Wout, dtype=np.float32))
    bo = np.ascontiguousarray(np.asarray(L0_attn_bout, dtype=np.float32))
    wg = np.ascontiguousarray(np.asarray(L0_ffn_w_gate, dtype=np.float32))
    wu = np.ascontiguousarray(np.asarray(L0_ffn_w_up, dtype=np.float32))
    wd = np.ascontiguousarray(np.asarray(L0_ffn_w_down, dtype=np.float32))

    if _NC_CACHE[0] is None:
        _NC_CACHE[0] = build_nc()
    nc = _NC_CACHE[0]

    in_maps = []
    for c in range(B):
        xb = x[c].T.reshape(NDC, 128, NCH, CHT).transpose(1, 2, 0, 3)
        in_maps.append({
            "xT": np.ascontiguousarray(xb).reshape(128, NCH * NDC * CHT),
            "wqkv": wqkv, "wout": wo, "bout": bo,
            "wgate": wg, "wup": wu, "wdown": wd,
        })
    res = run_bass_kernel_spmd(nc, in_maps, core_ids=list(range(B)))
    out = np.empty((B, NTOK, D), dtype=np.float32)
    for c in range(B):
        ob = np.asarray(res.results[c]["outT"]).reshape(128, NCH, NDC, CHT)
        out[c] = ob.transpose(2, 0, 1, 3).reshape(D, NTOK).T
    return out


# version 23
# speedup vs baseline: 1.0444x; 1.0057x over previous
import numpy as np
from contextlib import ExitStack

import concourse.bass as bass
import concourse.mybir as mybir
from concourse.bass_utils import run_bass_kernel_spmd

F32 = mybir.dt.float32
I32 = mybir.dt.int32
BF16 = mybir.dt.bfloat16
FP16 = mybir.dt.float16
AF = mybir.ActivationFunctionType
ALU = mybir.AluOpType
AX = mybir.AxisListType

B = 8
NTOK = 8192
D = 1024
DI = 3072
H = 16
NCH = 16
CHT = 512
NDC = D // 128
NJC = DI // 128
DEB = 32768.0

C_RB, C_ONE, C_65536, C_INV1, C_SC3, C_NB3, C_F3, C_F2, C_F4, C_F5, C_SC5, C_RBW = range(12)


AUTO_FLUSH = [True]

DBG = set()


def build_nc():
    nc = bass.Bass("TRN2", target_bir_lowering=False, debug=False)

    def scratch(name, shape, dt):
        kind = "ExternalOutput" if name in DBG else "Internal"
        return nc.dram_tensor(name, shape, dt, kind=kind).ap()

    xT = nc.dram_tensor("xT", [128, NCH * NDC * CHT], F32, kind="ExternalInput").ap()
    wqkv = nc.dram_tensor("wqkv", [D, 3 * D], F32, kind="ExternalInput").ap()
    wout = nc.dram_tensor("wout", [D, D], F32, kind="ExternalInput").ap()
    bout = nc.dram_tensor("bout", [D], F32, kind="ExternalInput").ap()
    wgate = nc.dram_tensor("wgate", [DI, D], F32, kind="ExternalInput").ap()
    wup = nc.dram_tensor("wup", [DI, D], F32, kind="ExternalInput").ap()
    wdown = nc.dram_tensor("wdown", [D, DI], F32, kind="ExternalInput").ap()
    outT = nc.dram_tensor("outT", [128, NCH * NDC * CHT], F32,
                          kind="ExternalOutput").ap()

    q_s = scratch("q_s", [128, NCH * NDC * CHT], BF16)
    xl2_s = scratch("xl2_s", [128, NCH * NDC * CHT], FP16)
    x1_s = scratch("x1_s", [128, NCH * NDC * CHT], F32)
    up_s = scratch("up_s", [128, NCH * NJC * CHT], FP16)
    gate_s = scratch("gate_s", [128, NCH * NJC * CHT], FP16)
    g2_s = scratch("g2_s", [128, NCH * NJC * CHT], BF16)
    o_s = scratch("o_s", [128, NCH * NJC * CHT], FP16)
    wupT_s = scratch("wupT_s", [128, NDC * DI], BF16)
    wgateT_s = scratch("wgateT_s", [128, NDC * DI], BF16)
    wdownT_s = scratch("wdownT_s", [128, NJC * D], BF16)
    cc_in = nc.dram_tensor("cc_in", [128, 4], F32)
    cc_out = nc.dram_tensor("cc_out", [128, 4], F32)

    C = {}

    top = ExitStack()
    with top:
        S = {}
        for n in ("ld", "st", "pe", "vec", "sc", "gp", "cc"):
            S[n] = top.enter_context(nc.semaphore(n))
            C[n] = 0

        def I(instr, sem=None, n=1):
            if sem is not None:
                instr.then_inc(S[sem], n)
                C[sem] += n
            return instr

        def VW(v):
            v.wait_ge(S["vec"], C["vec"])

        def SW(s):
            s.wait_ge(S["sc"], C["sc"])

        class _InstrRec:
            def __init__(self, rec, eng, op, args, kwargs):
                self.rec, self.eng, self.op = rec, eng, op
                self.args, self.kwargs = args, kwargs
                self.incs = []

            def then_inc(self, sem, n=1):
                self.incs.append((sem, n))
                return self

            def emit(self, e):
                ins = getattr(e, self.op)(*self.args, **self.kwargs)
                for sem, n in self.incs:
                    ins.then_inc(sem, n)
                return ins

        class _EngProxy:
            def __init__(self, rec, name):
                self._rec, self._name = rec, name

            def wait_ge(self, sem, val):
                r = _InstrRec(self._rec, self._name, "wait_ge", (sem, val), {})
                self._rec.q[self._name].append(r)
                return r

            def __getattr__(self, op):
                def call(*a, **k):
                    r = _InstrRec(self._rec, self._name, op, a, k)
                    self._rec.q[self._name].append(r)
                    return r
                return call

        class Rec:
            ENGS = ("sync", "gpsimd", "scalar", "tensor", "vector")
            AUTO = AUTO_FLUSH

            def __init__(self, blk=None):
                self.q = {e: [] for e in self.ENGS}
                self.blk = blk

            def __getattr__(self, eng):
                if eng not in self.q:
                    raise AttributeError(eng)
                def run(fn):
                    fn(_EngProxy(self, eng))
                    if self.blk is not None and (Rec.AUTO[0] or eng != "scalar"):
                        self.flush_one(self.blk, eng)
                return run

            def flush_one(self, blk, eng):
                ths = self.q[eng]
                if not ths:
                    return
                self.q[eng] = []
                getattr(blk, eng)(lambda e, ths=ths: [t.emit(e) for t in ths])

            def flush(self, blk):
                for eng in self.ENGS:
                    self.flush_one(blk, eng)

        cst = top.enter_context(nc.sbuf_tensor("cst", [128, 16], F32))
        bout_sb = top.enter_context(nc.sbuf_tensor("bout_sb", [128, NDC], F32))
        ones_m = top.enter_context(nc.sbuf_tensor("ones_m", [1, 128], F32))
        ones_mb = top.enter_context(nc.sbuf_tensor("ones_mb", [1, 128], BF16))
        onesk = top.enter_context(nc.sbuf_tensor("onesk", [128, 1], F32))
        oneskb = top.enter_context(nc.sbuf_tensor("oneskb", [128, 1], BF16))
        idn = top.enter_context(nc.sbuf_tensor("idn", [128, 128], BF16))
        iot = top.enter_context(nc.sbuf_tensor("iot", [128, 128], F32))
        prow = top.enter_context(nc.sbuf_tensor("prow", [1, 8], F32))
        pri = top.enter_context(nc.sbuf_tensor("pri", [1, 4], I32))
        accs = {k: top.enter_context(nc.sbuf_tensor(f"acc{k}", [128, NCH], F32))
                for k in (1, 2, 3, 4, 5)}
        jtmp = top.enter_context(nc.sbuf_tensor("jtmp", [128, NJC], F32))
        fold = top.enter_context(nc.sbuf_tensor("fold", [128, 4], F32))
        brow = top.enter_context(nc.sbuf_tensor("brow", [1, 512], F32))
        srow = top.enter_context(nc.sbuf_tensor("srow", [1, 16], F32))
        kv_sb = top.enter_context(nc.sbuf_tensor("kv_sb", [128, (H // 2) * 128], BF16))
        kv_ks = top.enter_context(nc.sbuf_tensor("kv_ks", [128, H // 2], BF16))
        mrow = top.enter_context(nc.sbuf_tensor("mrow", [1, CHT], F32))
        m2row = top.enter_context(nc.sbuf_tensor("m2row", [1, CHT], F32))
        trow = top.enter_context(nc.sbuf_tensor("trow", [1, CHT], F32))
        vrowt = top.enter_context(nc.sbuf_tensor("vrowt", [1, CHT], F32))
        rrow = top.enter_context(nc.sbuf_tensor("rrow", [1, CHT], F32))
        zrow = top.enter_context(nc.sbuf_tensor("zrow", [1, CHT], F32))
        zrow2 = top.enter_context(nc.sbuf_tensor("zrow2", [1, CHT], F32))

        with ExitStack() as p0, nc.Block() as blk:
            R = Rec(blk)
            wraw = p0.enter_context(nc.sbuf_tensor("wraw", [128, DI], F32))
            tmpa = p0.enter_context(nc.sbuf_tensor("tmpa", [128, DI], F32))
            tree = p0.enter_context(nc.sbuf_tensor("tree", [128, DI], F32))
            rep1 = p0.enter_context(nc.sbuf_tensor("rep1", [128, DI], F32))
            rep2 = p0.enter_context(nc.sbuf_tensor("rep2", [128, DI], F32))
            sca = p0.enter_context(nc.sbuf_tensor("sca", [128, DI // 32], F32))
            inva = p0.enter_context(nc.sbuf_tensor("inva", [128, DI // 32], F32))
            ti32 = p0.enter_context(nc.sbuf_tensor("ti32", [128, DI], I32))
            wqn = p0.enter_context(nc.sbuf_tensor("wqn", [128, DI], BF16))
            wTf = p0.enter_context(nc.sbuf_tensor("wTf", [128, NDC * DI], BF16))
            pstr = p0.enter_context(nc.psum_tensor("pstr", [128, 512], BF16))
            psb0 = p0.enter_context(nc.psum_tensor("psb0", [128, 8], F32))

            def g_init(g):
                I(g.memset(cst[:, C_ONE:C_ONE + 1], 1.0), "gp")
                I(g.memset(cst[:, C_65536:C_65536 + 1], 65536.0), "gp")
                I(g.memset(ones_m[:, :], 1.0), "gp")
                I(g.memset(onesk[:, :], 1.0), "gp")
                I(g.memset(oneskb[:, :], 1.0), "gp")
                I(g.memset(prow[:, 0:8], 0.7), "gp")
                for k in accs:
                    I(g.memset(accs[k][:, :], 0.0), "gp")
                I(g.memset(fold[:, :], 0.0), "gp")
                I(g.iota(iot[:, :], [[1, 128]], channel_multiplier=-1,
                         allow_small_or_imprecise_dtypes=True), "gp")
            R.gpsimd(g_init)
            gp0 = C["gp"]

            def v_idn_f(v):
                v.wait_ge(S["gp"], gp0)
                I(v.tensor_scalar(out=idn[:, :], in0=iot[:, :], scalar1=0.0,
                                  scalar2=None, op0=ALU.is_equal), "vec")
            R.vector(v_idn_f)
            v_idn = C["vec"]

            def s_probe(s):
                s.wait_ge(S["gp"], gp0)
                I(s.activation(out=pri[:, :], in_=prow[0:1, 0:4], func=AF.Relu,
                               bias=fold[0:1, 0:1],
                               scale=cst[0:1, C_ONE:C_ONE + 1]), "sc")
            R.scalar(s_probe)
            sc_probe = C["sc"]

            def v_rb(v):
                v.wait_ge(S["sc"], sc_probe)
                I(v.tensor_scalar(out=prow[:, 1:2], in0=pri[0:1, 0:1],
                                  scalar1=-0.5, scalar2=32768.5,
                                  op0=ALU.mult, op1=ALU.add), "vec")
                I(v.tensor_scalar(out=prow[:, 2:3], in0=pri[0:1, 0:1],
                                  scalar1=-0.5, scalar2=8.5,
                                  op0=ALU.mult, op1=ALU.add), "vec")
            R.vector(v_rb)
            v_rbc = C["vec"]

            def t_rb(t):
                t.wait_ge(S["vec"], v_rbc)
                I(t.matmul(psb0[:, 0:2], ones_m[:, :], prow[:, 1:3],
                           start=True, stop=True), "pe")
            R.tensor(t_rb)
            pe_rb = C["pe"]

            def s_rb(s):
                s.wait_ge(S["pe"], pe_rb)
                I(s.copy(out=cst[:, C_RB:C_RB + 1], in_=psb0[:, 0:1]), "sc")
                I(s.copy(out=cst[:, C_RBW:C_RBW + 1], in_=psb0[:, 1:2]), "sc")
            R.scalar(s_rb)
            sc_rb = C["sc"]

            def sy_bout(sy):
                I(sy.dma_start(out=bout_sb[:, :],
                               in_=bout.rearrange("(a p) -> p a", p=128),
                               allow_slow_non_contiguous=True), "ld", 16)
            R.sync(sy_bout)

            wr_rd = [0]
            tm_rd = [0]
            wq_rd = [0]
            ev_done = [0]
            for wdram, O, Iw, wdst in ((wup, DI, D, wupT_s),
                                       (wgate, DI, D, wgateT_s),
                                       (wdown, D, DI, wdownT_s)):
                nic = Iw // 128
                nb = Iw // 32
                for ot in range(O // 128):
                    R.flush(blk)
                    def sy_w(sy, wdram=wdram, Iw=Iw, ot=ot, wrv=wr_rd[0]):
                        sy.wait_ge(S["vec"], wrv)
                        I(sy.dma_start(
                            out=wraw[:, 0:Iw],
                            in_=wdram.rearrange("(a p) c -> a p c", p=128)[ot]),
                          "ld", 16)
                    R.sync(sy_w)
                    ld_w = C["ld"]

                    def v_abs(v, Iw=Iw, ld_w=ld_w, tmv=tm_rd[0]):
                        v.wait_ge(S["ld"], ld_w)
                        v.wait_ge(S["sc"], tmv)
                        I(v.scalar_tensor_tensor(out=tmpa[:, 0:Iw],
                                                 in0=wraw[:, 0:Iw], scalar=-1.0,
                                                 in1=wraw[:, 0:Iw],
                                                 op0=ALU.mult, op1=ALU.max), "vec")
                    R.vector(v_abs)
                    v_absc = C["vec"]

                    def v_dq(v, Iw=Iw, nb=nb, sc_abs=v_absc):
                        v.wait_ge(S["vec"], sc_abs)
                        VW(v)
                        src = tmpa[:, 0:Iw].rearrange("p (n two) -> p n two", two=2)
                        w = Iw // 2
                        I(v.tensor_tensor(out=tree[:, 0:w], in0=src[:, :, 0],
                                          in1=src[:, :, 1], op=ALU.max), "vec")
                        off = 0
                        while w > nb:
                            VW(v)
                            s2 = tree[:, off:off + w].rearrange(
                                "p (n two) -> p n two", two=2)
                            I(v.tensor_tensor(out=tree[:, off + w:off + w + w // 2],
                                              in0=s2[:, :, 0], in1=s2[:, :, 1],
                                              op=ALU.max), "vec")
                            off += w
                            w //= 2
                        VW(v)
                        I(v.tensor_scalar(out=sca[:, 0:nb], in0=tree[:, off:off + nb],
                                          scalar1=1.0 / 7.0, scalar2=1e-12,
                                          op0=ALU.mult, op1=ALU.max), "vec")
                        VW(v)
                        I(v.reciprocal(out=inva[:, 0:nb], in_=sca[:, 0:nb]), "vec")
                        VW(v)
                        for dst, srct in ((rep1, inva), (rep2, sca)):
                            dv = dst[:, 0:Iw].rearrange("p (n k) -> p n k", k=32)
                            I(v.tensor_copy(
                                out=dv[:, :, 0:1],
                                in_=srct[:, 0:nb].rearrange("p (n o) -> p n o", o=1)), "vec")
                            wd = 1
                            while wd < 32:
                                take = min(wd, 32 - wd)
                                VW(v)
                                I(v.tensor_copy(out=dv[:, :, wd:wd + take],
                                                in_=dv[:, :, 0:take]), "vec")
                                wd += take
                            VW(v)
                        I(v.tensor_tensor(out=tmpa[:, 0:Iw], in0=wraw[:, 0:Iw],
                                          in1=rep1[:, 0:Iw], op=ALU.mult), "vec")
                    R.vector(v_dq)
                    wr_rd[0] = C["vec"]
                    v_t = C["vec"]

                    def s_conv(s, Iw=Iw, v_t=v_t):
                        s.wait_ge(S["vec"], v_t)
                        s.wait_ge(S["sc"], sc_rb)
                        I(s.activation(out=ti32[:, 0:Iw], in_=tmpa[:, 0:Iw],
                                       func=AF.Relu, bias=cst[:, C_RBW:C_RBW + 1],
                                       scale=cst[:, C_ONE:C_ONE + 1]), "sc")
                    R.scalar(s_conv)
                    sc_conv = C["sc"]
                    tm_rd[0] = C["sc"]

                    def v_wq(v, Iw=Iw, sc_conv=sc_conv, wqv=wq_rd[0]):
                        v.wait_ge(S["sc"], sc_conv)
                        v.wait_ge(S["pe"], wqv)
                        I(v.scalar_tensor_tensor(out=wqn[:, 0:Iw], in0=ti32[:, 0:Iw],
                                                 scalar=8.0, in1=rep2[:, 0:Iw],
                                                 op0=ALU.subtract, op1=ALU.mult),
                          "vec")
                    R.vector(v_wq)
                    v_wqc = C["vec"]

                    for g0 in range(0, nic, 4):
                        gk = min(4, nic - g0)

                        def t_tr(t, g0=g0, gk=gk, v_wqc=v_wqc, evd=ev_done[0]):
                            t.wait_ge(S["vec"], max(v_wqc, v_idn))
                            t.wait_ge(S["sc"], evd)
                            for j in range(gk):
                                ic = g0 + j
                                I(t.transpose(pstr[:, j * 128:j * 128 + 128],
                                              wqn[:, ic * 128:ic * 128 + 128],
                                              idn[:, :]), "pe")
                        R.tensor(t_tr)
                        pe_tr = C["pe"]
                        wq_rd[0] = C["pe"]

                        def s_ev(s, g0=g0, gk=gk, ot=ot, O=O, pe_tr=pe_tr):
                            s.wait_ge(S["pe"], pe_tr)
                            dst = wTf.ap().rearrange("p (a j) -> p a j", j=O)[
                                :, g0:g0 + gk, ot * 128:ot * 128 + 128]
                            I(s.copy(out=dst,
                                     in_=pstr[:, 0:gk * 128].rearrange(
                                         "p (a j) -> p a j", j=128)), "sc")
                        R.scalar(s_ev)
                        ev_done[0] = C["sc"]

                def g_st(g, wdst=wdst, nic=nic, O=O, scv=ev_done[0]):
                    g.wait_ge(S["sc"], scv)
                    I(g.dma_start(out=wdst[:, 0:nic * O], in_=wTf[:, 0:nic * O]),
                      "st", 16)
                R.gpsimd(g_st)
                st_w = C["st"]

                def s_gate(s, st_w=st_w):
                    s.wait_ge(S["st"], st_w)
                R.scalar(s_gate)

            R.flush(blk)

        stack_wq = ExitStack()
        wqkv_sb = stack_wq.enter_context(
            nc.sbuf_tensor("wqkv_sb", [128, NDC * 3 * D], BF16))
        stack_wo = ExitStack()
        wout_sb = stack_wo.enter_context(
            nc.sbuf_tensor("wout_sb", [128, NDC * D], BF16))
        with ExitStack() as p0b, nc.Block() as blk:
            R = Rec(blk)
            wcst = [p0b.enter_context(nc.sbuf_tensor(f"wcst{j}", [128, 3 * D], F32))
                    for j in range(2)]
            casts = []
            for idx in range(2 * NDC):
                R.flush(blk)
                is_o = idx >= NDC
                ic = idx - NDC if is_o else idx
                w = D if is_o else 3 * D
                src = (wout if is_o else wqkv).rearrange("(a p) c -> a p c", p=128)[ic]

                def sy_c(sy, idx=idx, src=src, w=w):
                    if idx >= 2:
                        sy.wait_ge(S["vec"], casts[idx - 2])
                    I(sy.dma_start(out=wcst[idx % 2][:, 0:w], in_=src), "ld", 16)
                R.sync(sy_c)
                ld_c = C["ld"]

                def v_c(v, idx=idx, is_o=is_o, ic=ic, w=w, ld_c=ld_c):
                    v.wait_ge(S["ld"], ld_c)
                    dst = (wout_sb[:, ic * D:(ic + 1) * D] if is_o
                           else wqkv_sb[:, ic * 3 * D:(ic + 1) * 3 * D])
                    I(v.tensor_copy(out=dst, in_=wcst[idx % 2][:, 0:w]), "vec")
                    casts.append(C["vec"])
                R.vector(v_c)

            R.flush(blk)

        with ExitStack() as pa, nc.Block() as blk:
            R = Rec(blk)
            xblk = [pa.enter_context(nc.sbuf_tensor(f"xblk{j}", [128, NDC * CHT], F32))
                    for j in range(2)]
            sqblk = pa.enter_context(nc.sbuf_tensor("sqblk", [128, NDC * CHT], F32))
            tblk = sqblk
            xl1 = pa.enter_context(nc.sbuf_tensor("xl1", [128, NDC * CHT], BF16))
            k_sb = pa.enter_context(nc.sbuf_tensor("k_sb", [128, 4 * D], BF16))
            v_sb = pa.enter_context(nc.sbuf_tensor("v_sb", [128, 4 * D], BF16))
            qev = [pa.enter_context(nc.sbuf_tensor(f"qev{j}", [128, NDC * CHT], BF16))
                   for j in range(2)]
            mBs = pa.enter_context(nc.sbuf_tensor("mBs", [128, CHT], F32))
            rBs = pa.enter_context(nc.sbuf_tensor("rBs", [128, CHT], F32))
            psA = pa.enter_context(nc.psum_tensor("psA", [128, CHT], F32))
            psB = pa.enter_context(nc.psum_tensor("psB", [128, CHT], F32))
            psP = [pa.enter_context(nc.psum_tensor(f"psP{j}", [128, CHT], F32))
                   for j in range(2)]
            kvP = [pa.enter_context(nc.psum_tensor(f"kvP{j}", [128, 512], F32))
                   for j in range(2)]
            kvS2 = pa.enter_context(nc.psum_tensor("kvS2", [128, 8], F32))

            sn = []
            for i in range(NCH):
                bi = i % 2
                s_now = {}

                def sy_x(sy, i=i, bi=bi):
                    if i >= 2:
                        sy.wait_ge(S["vec"], sn[i - 2]["x_rd_v"])
                        sy.wait_ge(S["pe"], sn[i - 2]["x_rd_pe"])
                    I(sy.dma_start(out=xblk[bi][:, :],
                                   in_=xT[:, i * NDC * CHT:(i + 1) * NDC * CHT]),
                      "ld", 16)
                R.sync(sy_x)
                ld_x = C["ld"]

                def s_sq(s, i=i, bi=bi, ld_x=ld_x):
                    s.wait_ge(S["ld"], ld_x)
                    if i >= 1:
                        s.wait_ge(S["pe"], sn[i - 1]["stats"])
                        s.wait_ge(S["vec"], sn[i - 1]["ln"])
                    I(s.activation(out=sqblk[:, :], in_=xblk[bi][:, :],
                                   func=AF.Square, bias=0.0, scale=1.0), "sc")
                R.scalar(s_sq)
                sc_sq = C["sc"]

                def t_st(t, i=i, bi=bi, sc_sq=sc_sq):
                    t.wait_ge(S["sc"], sc_sq)
                    if i >= 1:
                        t.wait_ge(S["vec"], sn[i - 1]["bce"])
                    for a in range(NDC):
                        I(t.matmul(psA[0:1, :], onesk[:, :],
                                   xblk[bi][:, a * CHT:(a + 1) * CHT],
                                   start=(a == 0), stop=(a == NDC - 1)), "pe")
                    for a in range(NDC):
                        I(t.matmul(psB[0:1, :], onesk[:, :],
                                   sqblk[:, a * CHT:(a + 1) * CHT],
                                   start=(a == 0), stop=(a == NDC - 1)), "pe")
                R.tensor(t_st)
                s_now["stats"] = C["pe"]
                pe_stats = C["pe"]

                def v_rm(v, pe_stats=pe_stats):
                    v.wait_ge(S["pe"], pe_stats)
                    I(v.tensor_scalar(out=mrow[:, :], in0=psA[0:1, :],
                                      scalar1=1.0 / D, scalar2=None,
                                      op0=ALU.mult), "vec")
                    I(v.tensor_scalar(out=m2row[:, :], in0=psB[0:1, :],
                                      scalar1=1.0 / D, scalar2=None,
                                      op0=ALU.mult), "vec")
                R.vector(v_rm)
                s_now["rm1"] = C["vec"]

                def v_rm2(v):
                    VW(v)
                    I(v.tensor_tensor(out=trow[:, :], in0=mrow[:, :],
                                      in1=mrow[:, :], op=ALU.mult), "vec")
                    VW(v)
                    I(v.tensor_tensor(out=vrowt[:, :], in0=m2row[:, :],
                                      in1=trow[:, :], op=ALU.subtract), "vec")
                    VW(v)
                    I(v.tensor_scalar(out=trow[:, :], in0=vrowt[:, :],
                                      scalar1=1e-5, scalar2=None, op0=ALU.add),
                      "vec")
                    VW(v)
                    I(v.reciprocal(out=vrowt[:, :], in_=trow[:, :]), "vec")
                R.vector(v_rm2)
                v_rm2c = C["vec"]

                def s_rstd(s, v_rm2c=v_rm2c):
                    s.wait_ge(S["vec"], v_rm2c)
                    I(s.activation(out=rrow[:, :], in_=vrowt[:, :], func=AF.Sqrt,
                                   bias=0.0, scale=1.0), "sc")
                R.scalar(s_rstd)
                sc_rstd = C["sc"]

                def t_bc(t, i=i, sc_rstd=sc_rstd):
                    t.wait_ge(S["sc"], sc_rstd)
                    if i >= 1:
                        t.wait_ge(S["vec"], sn[i - 1]["bce"])
                    I(t.matmul(psA[:, :], ones_m[:, :], mrow[:, :],
                               start=True, stop=True), "pe")
                    I(t.matmul(psB[:, :], ones_m[:, :], rrow[:, :],
                               start=True, stop=True), "pe")
                R.tensor(t_bc)
                pe_bc = C["pe"]

                def v_bce(v, pe_bc=pe_bc):
                    v.wait_ge(S["pe"], pe_bc)
                    I(v.tensor_copy(out=mBs[:, :], in_=psA[:, :]), "vec")
                    I(v.tensor_copy(out=rBs[:, :], in_=psB[:, :]), "vec")
                R.vector(v_bce)
                s_now["bce"] = C["vec"]

                def v_ln(v, i=i, bi=bi):
                    if i >= 1:
                        v.wait_ge(S["pe"], sn[i - 1]["qkv"])
                    VW(v)
                    for a in range(NDC):
                        sl = slice(a * CHT, (a + 1) * CHT)
                        I(v.tensor_tensor(out=tblk[:, sl], in0=xblk[bi][:, sl],
                                          in1=mBs[:, :], op=ALU.subtract), "vec")
                        VW(v)
                        I(v.tensor_tensor(out=xl1[:, sl], in0=tblk[:, sl],
                                          in1=rBs[:, :], op=ALU.mult), "vec")
                R.vector(v_ln)
                s_now["ln"] = C["vec"]
                s_now["x_rd_v"] = C["vec"]
                v_lnc = C["vec"]

                if i == 0:
                    qe_prev = {0: ("sc", 0), 1: ("sc", 0)}
                q_ev = []
                for dot in range(NDC):
                    def t_q(t, i=i, dot=dot, v_lnc=v_lnc, qp=None):
                        t.wait_ge(S["vec"], v_lnc)
                        if i >= 1 or dot >= 2:
                            t.wait_ge(S[qe_prev[dot % 2][0]], qe_prev[dot % 2][1])
                        for ic in range(NDC):
                            I(t.matmul(psP[dot % 2][:, :],
                                       wqkv_sb[:, ic * 3 * D + dot * 128:
                                               ic * 3 * D + dot * 128 + 128],
                                       xl1[:, ic * CHT:(ic + 1) * CHT],
                                       start=(ic == 0), stop=(ic == NDC - 1)),
                              "pe")
                    R.tensor(t_q)
                    pe_q = C["pe"]

                    def v_qe(v, bi=bi, dot=dot, pe_q=pe_q, i=i):
                        v.wait_ge(S["pe"], pe_q)
                        if i >= 1:
                            v.wait_ge(S["st"], sn[i - 1]["qst"])
                        I(v.tensor_scalar(out=qev[bi][:, dot * CHT:(dot + 1) * CHT],
                                          in0=psP[dot % 2][:, :], scalar1=0.0,
                                          scalar2=None, op0=ALU.max), "vec")
                    R.vector(v_qe)
                    q_ev.append(("vec", C["vec"]))
                    qe_prev[dot % 2] = ("vec", C["vec"])

                for g in range(4):
                    for ct in range(4):
                        def t_kv(t, i=i, g=g, ct=ct, v_lnc=v_lnc):
                            t.wait_ge(S["vec"], v_lnc)
                            t.wait_ge(S[qe_prev[ct % 2][0]], qe_prev[ct % 2][1])
                            for ic in range(NDC):
                                I(t.matmul(
                                    psP[ct % 2][:, :],
                                    xl1[:, ic * CHT + g * 128:
                                        ic * CHT + g * 128 + 128],
                                    wqkv_sb[:, ic * 3 * D + D + ct * CHT:
                                            ic * 3 * D + D + (ct + 1) * CHT],
                                    start=(ic == 0), stop=(ic == NDC - 1)), "pe")
                        R.tensor(t_kv)
                        pe_kv = C["pe"]

                        def v_kve(v, g=g, ct=ct, pe_kv=pe_kv):
                            v.wait_ge(S["pe"], pe_kv)
                            if ct < 2:
                                I(v.tensor_scalar(
                                    out=k_sb[:, g * D + ct * CHT:
                                             g * D + (ct + 1) * CHT],
                                    in0=psP[ct % 2][:, :], scalar1=0.0,
                                    scalar2=None, op0=ALU.max), "vec")
                            else:
                                I(v.tensor_copy(
                                    out=v_sb[:, g * D + (ct - 2) * CHT:
                                             g * D + (ct - 1) * CHT],
                                    in_=psP[ct % 2][:, :]), "vec")
                        R.vector(v_kve)
                        qe_prev[ct % 2] = ("vec", C["vec"])
                    sc_kve = C["vec"]

                    def t_acc(t, i=i, g=g, sc_kve=sc_kve):
                        t.wait_ge(S["vec"], sc_kve)
                        first = (i == 0 and g == 0)
                        last = (i == NCH - 1 and g == 3)
                        for a in range(NDC):
                            I(t.matmul(kvP[a // 4][:, (a % 4) * 128:
                                                   (a % 4) * 128 + 128],
                                       k_sb[:, g * D + a * 128:g * D + (a + 1) * 128],
                                       v_sb[:, g * D + a * 128:g * D + (a + 1) * 128],
                                       start=first, stop=last), "pe")
                            I(t.matmul(kvS2[:, a:a + 1],
                                       k_sb[:, g * D + a * 128:g * D + (a + 1) * 128],
                                       oneskb[:, :],
                                       start=first, stop=last), "pe")
                    R.tensor(t_acc)
                    kv_rd = [C["pe"]]
                s_now["qkv"] = C["pe"]
                s_now["x_rd_pe"] = s_now["stats"]

                def g_qst(g, i=i, bi=bi, qlast=q_ev[-1]):
                    g.wait_ge(S[qlast[0]], qlast[1])
                    I(g.dma_start(out=q_s[:, i * NDC * CHT:(i + 1) * NDC * CHT],
                                  in_=qev[bi][:, :]), "st", 16)
                R.gpsimd(g_qst)
                s_now["qst"] = C["st"]
                sn.append(s_now)
                R.flush(blk)

            pe_all = C["pe"]

            def g_kvz(g, pe_all=pe_all):
                g.wait_ge(S["pe"], pe_all)
                I(g.memset(kv_sb[:, :], 0.0), "gp")
            R.gpsimd(g_kvz)
            gp_kvz = C["gp"]

            def s_kvout(s, pe_all=pe_all, gp_kvz=gp_kvz):
                s.wait_ge(S["gp"], gp_kvz)
                I(s.copy(out=kv_sb.ap().rearrange("p (h c) -> p h c", c=128)[0:64, 0:4, 0:64],
                         in_=bass.AP(kvP[0].ap().tensor, 0,
                                     [[512, 64], [128, 4], [1, 64]])), "sc")
                I(s.copy(out=kv_sb.ap().rearrange("p (h c) -> p h c", c=128)[0:64, 4:8, 0:64],
                         in_=bass.AP(kvP[1].ap().tensor, 0,
                                     [[512, 64], [128, 4], [1, 64]])), "sc")
                I(s.copy(out=kv_sb.ap().rearrange("p (h c) -> p h c", c=128)[64:128, 0:4, 64:128],
                         in_=bass.AP(kvP[0].ap().tensor, 64 * 512 + 64,
                                     [[512, 64], [128, 4], [1, 64]])), "sc")
                I(s.copy(out=kv_sb.ap().rearrange("p (h c) -> p h c", c=128)[64:128, 4:8, 64:128],
                         in_=bass.AP(kvP[1].ap().tensor, 64 * 512 + 64,
                                     [[512, 64], [128, 4], [1, 64]])), "sc")
                I(s.activation(
                    out=kv_ks.ap().rearrange("p (h o) -> p h o", o=1),
                    in_=kvS2[:, :].rearrange("p (h o) -> p h o", o=1),
                    func=AF.Copy, bias=1e-6, scale=1.0), "sc")
            R.scalar(s_kvout)
            a1_done = dict(C)
            R.flush(blk)

        with ExitStack() as pa2, nc.Block() as blk:
            R = Rec(blk)
            qld = [pa2.enter_context(nc.sbuf_tensor(f"qld{j}", [128, NDC * CHT], BF16))
                   for j in range(2)]
            xblk = [pa2.enter_context(nc.sbuf_tensor(f"x2blk{j}", [128, NDC * CHT], F32))
                    for j in range(2)]
            oaT = pa2.enter_context(nc.sbuf_tensor("oaT", [128, NDC * CHT], BF16))
            x1blk = pa2.enter_context(nc.sbuf_tensor("x1blk", [128, NDC * CHT], F32))
            sqblk = pa2.enter_context(nc.sbuf_tensor("sq2blk", [128, NDC * CHT], F32))
            tblk = sqblk
            xl2 = [pa2.enter_context(nc.sbuf_tensor(f"xl2b{j}", [128, NDC * CHT], FP16))
                   for j in range(2)]
            psO = [pa2.enter_context(nc.psum_tensor(f"psO{j}", [128, CHT], F32))
                   for j in range(2)]
            psZd = pa2.enter_context(nc.psum_tensor("psZd", [16, CHT], F32))
            psZB = pa2.enter_context(nc.psum_tensor("psZB", [128, CHT], F32))
            zdr = pa2.enter_context(nc.sbuf_tensor("zdr", [16, CHT], F32))
            zdr2 = pa2.enter_context(nc.sbuf_tensor("zdr2", [16, CHT], F32))
            zrS = pa2.enter_context(nc.sbuf_tensor("zrS", [16, CHT], BF16))
            zbsv = pa2.enter_context(nc.sbuf_tensor("zbsv", [128, CHT], F32))
            Z16 = pa2.enter_context(nc.sbuf_tensor("Z16", [128, 128], BF16))
            Sall = pa2.enter_context(nc.sbuf_tensor("Sall", [16, 8 * 128], BF16))
            psY = [pa2.enter_context(nc.psum_tensor(f"psY{j}", [128, CHT], F32))
                   for j in range(2)]
            psA = pa2.enter_context(nc.psum_tensor("ps2A", [128, CHT], F32))
            psB = pa2.enter_context(nc.psum_tensor("ps2B", [128, CHT], F32))
            mBs = pa2.enter_context(nc.sbuf_tensor("mBs2", [128, CHT], F32))
            rBs = pa2.enter_context(nc.sbuf_tensor("rBs2", [128, CHT], F32))

            Sit = pa2.enter_context(nc.sbuf_tensor("Sit", [16, 8 * 128], F32))

            def g_sel(g):
                I(g.memset(Z16[:, :], 0.0), "gp")
                I(g.iota(Sit.ap().rearrange("k (c w) -> k c w", w=64),
                         [[1, 16], [0, 64]], channel_multiplier=-1,
                         allow_small_or_imprecise_dtypes=True), "gp")
            R.gpsimd(g_sel)
            gp_sel = C["gp"]

            def v_sel(v):
                v.wait_ge(S["gp"], gp_sel)
                I(v.tensor_scalar(out=Sall[:, :], in0=Sit[:, :], scalar1=0.0,
                                  scalar2=None, op0=ALU.is_equal), "vec")
            R.vector(v_sel)

            def s_z16(s):
                s.wait_ge(S["gp"], gp_sel)
                s.wait_ge(S["sc"], a1_done["sc"])
                for h in range(H):
                    a, pb = h // 2, (h % 2) * 64
                    I(s.copy(out=Z16[pb:pb + 64, a * 16 + h:a * 16 + h + 1],
                             in_=kv_ks[pb:pb + 64, a:a + 1]), "sc")
            R.scalar(s_z16)
            sc_z16 = C["sc"]

            sn = []
            zev_prev = [0]
            zbc_last = [0]
            for i in range(NCH):
                bi = i % 2
                s_now = {}

                def sy_q(sy, i=i, bi=bi):
                    if i >= 2:
                        sy.wait_ge(S["pe"], sn[i - 2]["q_rd"])
                        sy.wait_ge(S["vec"], sn[i - 2]["x_rd"])
                    else:
                        sy.wait_ge(S["sc"], a1_done["sc"])
                        sy.wait_ge(S["st"], a1_done["st"])
                        sy.wait_ge(S["vec"], a1_done["vec"])
                    I(sy.dma_start(out=qld[bi][:, :],
                                   in_=q_s[:, i * NDC * CHT:(i + 1) * NDC * CHT]),
                      "ld", 16)
                    I(sy.dma_start(out=xblk[bi][:, :],
                                   in_=xT[:, i * NDC * CHT:(i + 1) * NDC * CHT]),
                      "ld", 16)
                R.sync(sy_q)
                ld_q = C["ld"]

                def t_zd(t, i=i, bi=bi, ld_q=ld_q, zp=zev_prev[0]):
                    t.wait_ge(S["ld"], ld_q)
                    t.wait_ge(S["sc"], sc_z16)
                    if i >= 1:
                        t.wait_ge(S["vec"], zp)
                    for a in range(NDC):
                        I(t.matmul(psZd[:, :], Z16[:, a * 16:(a + 1) * 16],
                                   qld[bi][:, a * CHT:(a + 1) * CHT],
                                   start=(a == 0), stop=(a == NDC - 1)), "pe")
                R.tensor(t_zd)
                pe_zd = C["pe"]

                def v_zd(v, pe_zd=pe_zd, zb=zbc_last[0], i=i):
                    v.wait_ge(S["pe"], pe_zd)
                    if i >= 1:
                        v.wait_ge(S["pe"], zb)
                    I(v.tensor_scalar(out=zdr[:, :], in0=psZd[:, :],
                                      scalar1=1e-6, scalar2=None,
                                      op0=ALU.add), "vec")
                    VW(v)
                    I(v.reciprocal(out=zdr2[:, :], in_=zdr[:, :]), "vec")
                    VW(v)
                    I(v.tensor_copy(out=zrS[:, :], in_=zdr2[:, :]), "vec")
                R.vector(v_zd)
                zev_prev[0] = C["vec"]
                v_zdc = C["vec"]

                oa_done = []
                vzc_prev = [0]
                for a in range(NDC):
                    def t_o(t, i=i, bi=bi, a=a, ld_q=ld_q):
                        t.wait_ge(S["ld"], ld_q)
                        if a >= 2:
                            t.wait_ge(S["vec"], oa_done[a - 2])
                        elif i >= 1:
                            t.wait_ge(S["vec"], sn[i - 1]["oa_last2"][a % 2])
                        I(t.matmul(psO[a % 2][:, :],
                                   kv_sb[:, a * 128:(a + 1) * 128],
                                   qld[bi][:, a * CHT:(a + 1) * CHT],
                                   start=True, stop=True), "pe")
                    R.tensor(t_o)
                    pe_o = C["pe"]

                    def t_zbc(t, a=a, i=i, v_zdc=v_zdc, vz=vzc_prev[0]):
                        t.wait_ge(S["vec"], v_zdc)
                        if a >= 1 or i >= 1:
                            t.wait_ge(S["vec"], vz)
                        I(t.matmul(psZB[:, :], Sall[:, a * 128:(a + 1) * 128],
                                   zrS[:, :], start=True, stop=True), "pe")
                    R.tensor(t_zbc)
                    pe_zbc = C["pe"]
                    zbc_last[0] = pe_zbc

                    def v_zc(v, pe_zbc=pe_zbc):
                        v.wait_ge(S["pe"], pe_zbc)
                        I(v.tensor_copy(out=zbsv[:, :], in_=psZB[:, :]), "vec")
                    R.vector(v_zc)
                    vzc_prev[0] = C["vec"]

                    def v_oa(v, a=a, i=i, pe_o=pe_o):
                        v.wait_ge(S["pe"], pe_o)
                        VW(v)
                        if i >= 1 and a < 2:
                            v.wait_ge(S["pe"], sn[i - 1]["wout"])
                        I(v.tensor_tensor(
                            out=oaT[:, a * CHT:(a + 1) * CHT],
                            in0=psO[a % 2][:, :], in1=zbsv[:, :],
                            op=ALU.mult), "vec")
                    R.vector(v_oa)
                    oa_done.append(C["vec"])
                s_now["q_rd"] = C["pe"]
                s_now["oa_last2"] = {0: oa_done[-2], 1: oa_done[-1]}
                v_oal = C["vec"]

                x1_parts = []
                for dot in range(NDC):
                    def t_y(t, dot=dot, v_oal=v_oal, i=i):
                        t.wait_ge(S["vec"], v_oal)
                        if dot >= 2:
                            t.wait_ge(S["vec"], x1_parts[dot - 2])
                        elif i >= 1:
                            t.wait_ge(S["vec"], sn[i - 1]["x1_last2"][dot])
                        for ic in range(NDC):
                            I(t.matmul(psY[dot % 2][:, :],
                                       wout_sb[:, ic * D + dot * 128:
                                               ic * D + dot * 128 + 128],
                                       oaT[:, ic * CHT:(ic + 1) * CHT],
                                       start=(ic == 0), stop=(ic == NDC - 1)),
                              "pe")
                    R.tensor(t_y)
                    pe_y = C["pe"]

                    def v_x1(v, i=i, dot=dot, bi=bi, pe_y=pe_y):
                        v.wait_ge(S["pe"], pe_y)
                        if dot == 0 and i >= 1:
                            v.wait_ge(S["st"], sn[i - 1]["xl2st"])
                        I(v.scalar_tensor_tensor(
                            out=x1blk[:, dot * CHT:(dot + 1) * CHT],
                            in0=psY[dot % 2][:, :],
                            scalar=bout_sb[:, dot:dot + 1],
                            in1=xblk[bi][:, dot * CHT:(dot + 1) * CHT],
                            op0=ALU.add, op1=ALU.add), "vec")
                    R.vector(v_x1)
                    x1_parts.append(C["vec"])
                s_now["wout"] = C["pe"]
                s_now["x1_last2"] = {0: x1_parts[-2], 1: x1_parts[-1]}
                s_now["x_rd"] = C["vec"]
                v_x1c = C["vec"]

                def s_sq2(s, v_x1c=v_x1c, i=i):
                    s.wait_ge(S["vec"], v_x1c)
                    if i >= 1:
                        s.wait_ge(S["pe"], sn[i - 1]["stats2"])
                        s.wait_ge(S["vec"], sn[i - 1]["ln2"])
                    I(s.activation(out=sqblk[:, :], in_=x1blk[:, :],
                                   func=AF.Square, bias=0.0, scale=1.0), "sc")
                R.scalar(s_sq2)
                sc_sq2 = C["sc"]

                def t_st2(t, sc_sq2=sc_sq2, i=i):
                    t.wait_ge(S["sc"], sc_sq2)
                    if i >= 1:
                        t.wait_ge(S["vec"], sn[i - 1]["bce2"])
                    for a in range(NDC):
                        I(t.matmul(psA[0:1, :], onesk[:, :],
                                   x1blk[:, a * CHT:(a + 1) * CHT],
                                   start=(a == 0), stop=(a == NDC - 1)), "pe")
                    for a in range(NDC):
                        I(t.matmul(psB[0:1, :], onesk[:, :],
                                   sqblk[:, a * CHT:(a + 1) * CHT],
                                   start=(a == 0), stop=(a == NDC - 1)), "pe")
                R.tensor(t_st2)
                s_now["stats2"] = C["pe"]
                pe_st2 = C["pe"]

                def v_rm(v, pe_st2=pe_st2):
                    v.wait_ge(S["pe"], pe_st2)
                    I(v.tensor_scalar(out=mrow[:, :], in0=psA[0:1, :],
                                      scalar1=1.0 / D, scalar2=None,
                                      op0=ALU.mult), "vec")
                    I(v.tensor_scalar(out=m2row[:, :], in0=psB[0:1, :],
                                      scalar1=1.0 / D, scalar2=None,
                                      op0=ALU.mult), "vec")
                R.vector(v_rm)
                s_now["rm2"] = C["vec"]

                def v_rm2(v):
                    VW(v)
                    I(v.tensor_tensor(out=trow[:, :], in0=mrow[:, :],
                                      in1=mrow[:, :], op=ALU.mult), "vec")
                    VW(v)
                    I(v.tensor_tensor(out=vrowt[:, :], in0=m2row[:, :],
                                      in1=trow[:, :], op=ALU.subtract), "vec")
                    VW(v)
                    I(v.tensor_scalar(out=trow[:, :], in0=vrowt[:, :],
                                      scalar1=1e-5, scalar2=None, op0=ALU.add),
                      "vec")
                    VW(v)
                    I(v.reciprocal(out=vrowt[:, :], in_=trow[:, :]), "vec")
                R.vector(v_rm2)
                v_rm2c = C["vec"]

                def s_rstd(s, v_rm2c=v_rm2c):
                    s.wait_ge(S["vec"], v_rm2c)
                    I(s.activation(out=rrow[:, :], in_=vrowt[:, :], func=AF.Sqrt,
                                   bias=0.0, scale=1.0), "sc")
                R.scalar(s_rstd)
                sc_rstd = C["sc"]

                def t_bc2(t, sc_rstd=sc_rstd, i=i):
                    t.wait_ge(S["sc"], sc_rstd)
                    if i >= 1:
                        t.wait_ge(S["vec"], sn[i - 1]["bce2"])
                    I(t.matmul(psA[:, :], ones_m[:, :], mrow[:, :],
                               start=True, stop=True), "pe")
                    I(t.matmul(psB[:, :], ones_m[:, :], rrow[:, :],
                               start=True, stop=True), "pe")
                R.tensor(t_bc2)
                pe_bc2 = C["pe"]

                def v_bce2(v, pe_bc2=pe_bc2):
                    v.wait_ge(S["pe"], pe_bc2)
                    I(v.tensor_copy(out=mBs[:, :], in_=psA[:, :]), "vec")
                    I(v.tensor_copy(out=rBs[:, :], in_=psB[:, :]), "vec")
                R.vector(v_bce2)
                s_now["bce2"] = C["vec"]

                def v_ln2(v, i=i, bi=bi):
                    if i >= 2:
                        v.wait_ge(S["st"], sn[i - 2]["xl2st"])
                    VW(v)
                    for a in range(NDC):
                        sl = slice(a * CHT, (a + 1) * CHT)
                        I(v.tensor_tensor(out=tblk[:, sl], in0=x1blk[:, sl],
                                          in1=mBs[:, :], op=ALU.subtract), "vec")
                        VW(v)
                        I(v.tensor_tensor(out=xl2[bi][:, sl], in0=tblk[:, sl],
                                          in1=rBs[:, :], op=ALU.mult), "vec")
                    VW(v)
                    I(v.reduce_max(out=accs[1][:, i:i + 1], in_=xl2[bi][:, :],
                                   axis=AX.X, apply_absolute_value=True), "vec")
                R.vector(v_ln2)
                s_now["ln2"] = C["vec"]
                v_ln2c = C["vec"]

                def g_st2(g, i=i, bi=bi, v_x1c=v_x1c, v_ln2c=v_ln2c):
                    g.wait_ge(S["vec"], v_ln2c)
                    I(g.dma_start(out=x1_s[:, i * NDC * CHT:(i + 1) * NDC * CHT],
                                  in_=x1blk[:, :]), "st", 16)
                    I(g.dma_start(out=xl2_s[:, i * NDC * CHT:(i + 1) * NDC * CHT],
                                  in_=xl2[bi][:, :]), "st", 16)
                R.gpsimd(g_st2)
                s_now["xl2st"] = C["st"]
                sn.append(s_now)
                R.flush(blk)

            def v_fold1(v):
                VW(v)
                I(v.reduce_max(out=fold[:, 0:1], in_=accs[1][:, :], axis=AX.X),
                  "vec")
            R.vector(v_fold1)
            v_f1 = C["vec"]

            def g_cc1(g, v_f1=v_f1):
                g.wait_ge(S["vec"], v_f1)
                I(g.dma_start(out=cc_in.ap(), in_=fold[:, :]), "st", 16)
                g.wait_ge(S["st"], C["st"])
                I(g.collective_compute(
                    "AllReduce", ALU.max, replica_groups=[list(range(B))],
                    ins=[cc_in.ap()], outs=[cc_out.ap()]), "cc")
                g.wait_ge(S["cc"], C["cc"])
                I(g.dma_start(
                    out=brow[:, :],
                    in_=cc_out.ap().rearrange("p f -> (p f)")), "st", 16)
            R.gpsimd(g_cc1)
            st_cc1 = C["st"]

            def v_s1(v, st_cc1=st_cc1):
                v.wait_ge(S["st"], st_cc1)
                VW(v)
                I(v.reduce_max(out=srow[:, 0:4],
                               in_=brow.ap().rearrange("o (p f) -> o f p", f=4),
                               axis=AX.X), "vec")
                VW(v)
                I(v.tensor_scalar(out=srow[:, 4:5], in0=srow[0:1, 0:1],
                                  scalar1=1.0 / 32767.0, scalar2=1e-12,
                                  op0=ALU.mult, op1=ALU.max), "vec")
                VW(v)
                I(v.reciprocal(out=srow[:, 5:6], in_=srow[0:1, 4:5]), "vec")
            R.vector(v_s1)
            v_s1c = C["vec"]

            def t_bcs1b(t, v_s1c=v_s1c):
                t.wait_ge(S["vec"], v_s1c)
                I(t.matmul(psY[0][:, 0:1], ones_m[:, :], srow[:, 5:6],
                           start=True, stop=True), "pe")
            R.tensor(t_bcs1b)
            pe_b1 = C["pe"]

            def s_cst1(s, pe_b1=pe_b1):
                s.wait_ge(S["pe"], pe_b1)
                I(s.copy(out=cst[:, C_INV1:C_INV1 + 1], in_=psY[0][:, 0:1]), "sc")
            R.scalar(s_cst1)
            a2_done = dict(C)
            R.flush(blk)

        stack_wo.close()
        stack_wq.close()

        stack_wu = ExitStack()
        wuT = stack_wu.enter_context(nc.sbuf_tensor("wuT", [128, NDC * DI], BF16))
        stack_wg = ExitStack()
        wgT = stack_wg.enter_context(nc.sbuf_tensor("wgT", [128, NDC * DI], BF16))

        def emit_up_or_gate(R, buf, is_gate, acc_idx, scr_out, wT, f_scale_slot,
                            start_waits, hook=None):
            sn = []
            nxl = len(buf["xl2ld"])
            NG = NJC // 4
            for i in range(NCH):
                bi = i % nxl
                s_now = {}

                def sy_l(sy, i=i, bi=bi, nxl=nxl):
                    if i >= nxl:
                        sy.wait_ge(S["sc"], sn[i - nxl]["xq_sc"])
                    else:
                        for k, vv in start_waits.items():
                            sy.wait_ge(S[k], vv)
                    I(sy.dma_start(
                        out=buf["xl2ld"][bi][:, :],
                        in_=xl2_s[:, i * NDC * CHT:(i + 1) * NDC * CHT]), "ld", 16)
                R.sync(sy_l)
                ld_l = C["ld"]

                def s_xq(s, i=i, bi=bi, ld_l=ld_l):
                    s.wait_ge(S["ld"], ld_l)
                    I(s.activation(out=buf["xqi"][:, :], in_=buf["xl2ld"][bi][:, :],
                                   func=AF.Relu, bias=cst[:, C_RB:C_RB + 1],
                                   scale=cst[:, C_INV1:C_INV1 + 1]), "sc")
                R.scalar(s_xq)
                s_now["xq_sc"] = C["sc"]

                def s_dq(s, i=i):
                    SW(s)
                    if i >= 1:
                        s.wait_ge(S["pe"], sn[i - 1]["mm"])
                    I(s.activation(out=buf["xqt"][:, :], in_=buf["xqi"][:, :],
                                   func=AF.Copy, bias=-DEB, scale=1.0), "sc")
                R.scalar(s_dq)
                s_now["xqt_v"] = C["sc"]
                v_xqc = C["sc"]

                ev_list = []
                rd_list = []
                for g4 in range(NG):
                    pg = buf["psF"][g4 % 2]

                    def t_mm(t, i=i, g4=g4, v_xqc=v_xqc, wT=wT, pg=pg):
                        t.wait_ge(S["sc"], v_xqc)
                        if g4 >= 2:
                            t.wait_ge(S["sc"], ev_list[g4 - 2])
                        elif i >= 1:
                            t.wait_ge(S["sc"], sn[i - 1]["ev_last2"][g4])
                        for jj in range(4):
                            jt = g4 * 4 + jj
                            for ic in range(NDC):
                                I(t.matmul(pg[:, jj * CHT:(jj + 1) * CHT],
                                           wT[:, ic * DI + jt * 128:
                                              ic * DI + jt * 128 + 128],
                                           buf["xqt"][:, ic * CHT:(ic + 1) * CHT],
                                           start=(ic == 0), stop=(ic == NDC - 1)),
                                  "pe")
                    R.tensor(t_mm)
                    pe_mm = C["pe"]

                    def s_ev(s, i=i, g4=g4, pe_mm=pe_mm, pg=pg):
                        s.wait_ge(S["pe"], pe_mm)
                        if i > 0 or g4 > 1:
                            s.wait_ge(S["st"], ev_st[g4 % 2][0])
                            if len(rd_list) >= 2:
                                s.wait_ge(S["vec"], rd_list[-2])
                            elif i > 0:
                                s.wait_ge(S["vec"], sn[i - 1]["rd_last2"][g4 % 2])
                        I(s.activation(
                            out=buf["evb"][g4 % 2][:, :],
                            in_=pg[:, :], func=AF.Copy,
                            bias=0.0, scale=1.0 / 16.0), "sc")
                    R.scalar(s_ev)
                    ev_list.append(C["sc"])

                    def v_red(v, g4=g4, ev=ev_list[-1]):
                        v.wait_ge(S["sc"], ev)
                        I(v.reduce_max(out=jtmp[:, g4:g4 + 1],
                                       in_=buf["evb"][g4 % 2][:, :], axis=AX.X,
                                       apply_absolute_value=True), "vec")
                    R.vector(v_red)
                    rd_list.append(C["vec"])

                    def g_stg(g, i=i, g4=g4, el=ev_list[-1], scr_out=scr_out):
                        g.wait_ge(S["sc"], el)
                        I(g.dma_start(
                            out=scr_out[:, (i * NJC + g4 * 4) * CHT:
                                        (i * NJC + (g4 + 1) * 4) * CHT],
                            in_=buf["evb"][g4 % 2][:, :]), "st", 16)
                    R.gpsimd(g_stg)
                    if i == 0 and g4 < 2:
                        ev_st = {0: [0], 1: [0]} if g4 == 0 else ev_st
                    ev_st[g4 % 2] = [C["st"]]
                    if hook is not None and i >= 1 and 1 <= g4 <= 4:
                        hook(i - 1, g4 - 1)
                s_now["mm"] = C["pe"]
                s_now["ev_last2"] = {0: ev_list[-2], 1: ev_list[-1]}
                s_now["rd_last2"] = {0: rd_list[-2], 1: rd_list[-1]}

                def v_fold(v, i=i, acc_idx=acc_idx, rl=rd_list[-1], NG=NG):
                    v.wait_ge(S["vec"], rl)
                    VW(v)
                    I(v.reduce_max(out=accs[acc_idx][:, i:i + 1],
                                   in_=jtmp[:, 0:NG], axis=AX.X), "vec")
                R.vector(v_fold)

                s_now["sto"] = C["st"]
                if hook is not None and i >= 1:
                    hook(i - 1, "end")
                sn.append(s_now)
                R.flush(blk)
            if hook is not None:
                for st_ in (0, 1, 2, 3, "end"):
                    hook(NCH - 1, st_)
                R.flush(blk)
            return sn

        with ExitStack() as pf1, nc.Block() as blk:
            R = Rec(blk)
            buf = {
                "xl2ld": [pf1.enter_context(
                    nc.sbuf_tensor(f"xl2ld{j}", [128, NDC * CHT], FP16))
                    for j in range(2)],
                "xqi": pf1.enter_context(
                    nc.sbuf_tensor("xqi", [128, NDC * CHT], I32)),
                "xqt": pf1.enter_context(
                    nc.sbuf_tensor("xqt", [128, NDC * CHT], BF16)),
                "evb": [pf1.enter_context(
                    nc.sbuf_tensor(f"gevb{j}", [128, 4 * CHT], FP16))
                    for j in range(2)],
                "psF": [pf1.enter_context(nc.psum_tensor(f"psF{j}", [128, 4 * CHT], F32))
                        for j in range(2)],
            }
            psb2 = buf["psF"][0]

            def sy_wg(sy):
                sy.wait_ge(S["sc"], a2_done["sc"])
                I(sy.dma_start(out=wgT[:, :], in_=wgateT_s[:, :]), "ld", 16)
                I(sy.dma_start(out=wuT[:, :], in_=wupT_s[:, :]), "ld", 16)
            R.sync(sy_wg)
            ld_wg = C["ld"]

            def t_wait(t, ld_wg=ld_wg):
                t.wait_ge(S["ld"], ld_wg)
            R.tensor(t_wait)

            sn_g = emit_up_or_gate(R, buf, True, 3, gate_s, wgT, None,
                                   {"sc": a2_done["sc"], "vec": a2_done["vec"],
                                    "st": a2_done["st"]})

            def v_fold3(v):
                VW(v)
                I(v.reduce_max(out=fold[:, 0:1], in_=accs[3][:, :], axis=AX.X),
                  "vec")
            R.vector(v_fold3)
            v_f3 = C["vec"]

            def g_cc3(g, v_f3=v_f3):
                g.wait_ge(S["vec"], v_f3)
                I(g.dma_start(out=cc_in.ap(), in_=fold[:, :]), "st", 16)
                g.wait_ge(S["st"], C["st"])
                I(g.collective_compute(
                    "AllReduce", ALU.max, replica_groups=[list(range(B))],
                    ins=[cc_in.ap()], outs=[cc_out.ap()]), "cc")
                g.wait_ge(S["cc"], C["cc"])
                I(g.dma_start(out=brow[:, :],
                              in_=cc_out.ap().rearrange("p f -> (p f)")),
                  "st", 16)
            R.gpsimd(g_cc3)
            st_cc3 = C["st"]

            def v_s3(v, st_cc3=st_cc3):
                v.wait_ge(S["st"], st_cc3)
                VW(v)
                I(v.reduce_max(out=srow[:, 0:4],
                               in_=brow.ap().rearrange("o (p f) -> o f p", f=4),
                               axis=AX.X), "vec")
                VW(v)
                I(v.tensor_scalar(out=srow[:, 10:11], in0=srow[0:1, 4:5],
                                  scalar1=16.0, scalar2=None, op0=ALU.mult), "vec")
                VW(v)
                I(v.tensor_tensor(out=srow[:, 6:7], in0=srow[0:1, 0:1],
                                  in1=srow[0:1, 10:11], op=ALU.mult), "vec")
                VW(v)
                I(v.tensor_scalar(out=srow[:, 7:8], in0=srow[0:1, 6:7],
                                  scalar1=1.0 / 32767.0, scalar2=1e-12,
                                  op0=ALU.mult, op1=ALU.max), "vec")
                VW(v)
                I(v.tensor_scalar(out=srow[:, 8:9], in0=srow[0:1, 7:8],
                                  scalar1=-DEB, scalar2=None, op0=ALU.mult), "vec")
                VW(v)
                I(v.reciprocal(out=srow[:, 9:10], in_=srow[0:1, 7:8]), "vec")
                VW(v)
                I(v.tensor_scalar(out=srow[:, 10:11], in0=srow[0:1, 4:5],
                                  scalar1=16.0, scalar2=None, op0=ALU.mult), "vec")
                VW(v)
                I(v.tensor_tensor(out=srow[:, 11:12], in0=srow[0:1, 10:11],
                                  in1=srow[0:1, 9:10], op=ALU.mult), "vec")
            R.vector(v_s3)
            v_s3c = C["vec"]

            def t_bc3(t, v_s3c=v_s3c, sc_all=C["sc"]):
                t.wait_ge(S["vec"], v_s3c)
                t.wait_ge(S["sc"], sc_all)
                I(t.matmul(psb2[:, 0:1], ones_m[:, :], srow[:, 7:8],
                           start=True, stop=True), "pe")
                I(t.matmul(psb2[:, 1:2], ones_m[:, :], srow[:, 8:9],
                           start=True, stop=True), "pe")
                I(t.matmul(psb2[:, 2:3], ones_m[:, :], srow[:, 11:12],
                           start=True, stop=True), "pe")
            R.tensor(t_bc3)
            pe_b3 = C["pe"]

            def s_cst3(s, pe_b3=pe_b3):
                s.wait_ge(S["pe"], pe_b3)
                I(s.copy(out=cst[:, C_SC3:C_SC3 + 1], in_=psb2[:, 0:1]), "sc")
                I(s.copy(out=cst[:, C_NB3:C_NB3 + 1], in_=psb2[:, 1:2]), "sc")
                I(s.copy(out=cst[:, C_F3:C_F3 + 1], in_=psb2[:, 2:3]), "sc")
            R.scalar(s_cst3)
            f1a_done = dict(C)
            R.flush(blk)
        stack_wg.close()

        with ExitStack() as pf2, nc.Block() as blk:
            R = Rec(blk)
            buf = {
                "xl2ld": [pf2.enter_context(
                    nc.sbuf_tensor("xl2le0", [128, NDC * CHT], FP16))],
                "xqi": pf2.enter_context(
                    nc.sbuf_tensor("xqi2", [128, NDC * CHT], I32)),
                "xqt": pf2.enter_context(
                    nc.sbuf_tensor("xqt2", [128, NDC * CHT], BF16)),
                "evb": [pf2.enter_context(
                    nc.sbuf_tensor(f"uevb{j}", [128, 4 * CHT], FP16))
                    for j in range(2)],
                "psF": [pf2.enter_context(nc.psum_tensor(f"psF2{j}", [128, 4 * CHT], F32))
                        for j in range(2)],
            }
            gld = [pf2.enter_context(nc.sbuf_tensor("gld0", [128, DI], FP16))]
            t3 = [pf2.enter_context(nc.sbuf_tensor(f"t3_{j}", [128, DI], I32))
                  for j in range(2)]
            sgt = pf2.enter_context(nc.sbuf_tensor("sgt", [128, DI], F32))
            u4 = [pf2.enter_context(nc.sbuf_tensor(f"u4_{j}", [128, DI], I32))
                  for j in range(2)]
            sqc = pf2.enter_context(nc.sbuf_tensor("sqc", [128, DI], F32))
            g2b = [pf2.enter_context(nc.sbuf_tensor(f"g2b{j}", [128, DI], BF16))
                   for j in range(2)]
            g4t = pf2.enter_context(nc.sbuf_tensor("g4t", [128, 4], F32))

            sn2 = []
            g2_stl = {0: [0], 1: [0]}
            g2_rd = {0: [0], 1: [0]}
            g_sc_rd = {0: [0]}

            f2_subv = {}

            def f2_hook(i, stage):
                if stage == "end":
                    sub_v = f2_subv[i]

                    def v_f4(v, i=i, lv=sub_v[-1]):
                        v.wait_ge(S["vec"], lv)
                        VW(v)
                        I(v.reduce_max(out=accs[4][:, i:i + 1], in_=g4t[:, :],
                                       axis=AX.X), "vec")
                    R.vector(v_f4)
                    sn2.append({"g2v": C["vec"], "g2st": C["st"]})
                    return
                sub_v = f2_subv.setdefault(i, [])
                if True:
                    sub = stage
                    sb = sub % 2

                    def sy_g(sy, i=i, sub=sub, sb=sb):
                        if i > 0 or sub > 0:
                            sy.wait_ge(S["sc"], g_sc_rd[0][0])
                        else:
                            sy.wait_ge(S["sc"], f1a_done["sc"])
                        I(sy.dma_start(
                            out=gld[0][:, :],
                            in_=gate_s[:, (i * NJC + sub * (NJC // 4)) * CHT:
                                       (i * NJC + (sub + 1) * (NJC // 4)) * CHT]),
                          "ld", 16)
                    R.sync(sy_g)
                    ld_g = C["ld"]

                    def s_ch(s, i=i, sub=sub, sb=sb, ld_g=ld_g):
                        s.wait_ge(S["ld"], ld_g)
                        if i > 0 or sub > 1:
                            s.wait_ge(S["vec"], g2_rd[sb][0])
                        I(s.activation(out=t3[sb][:, :], in_=gld[0][:, :],
                                       func=AF.Relu,
                                       bias=cst[:, C_RB:C_RB + 1],
                                       scale=cst[:, C_F3:C_F3 + 1]), "sc")
                        SW(s)
                        I(s.activation(out=sgt[:, :], in_=t3[sb][:, :],
                                       func=AF.Sigmoid,
                                       bias=cst[:, C_NB3:C_NB3 + 1],
                                       scale=cst[:, C_SC3:C_SC3 + 1]), "sc")
                        SW(s)
                        I(s.activation(out=u4[sb][:, :], in_=sgt[:, :],
                                       func=AF.Relu,
                                       bias=cst[:, C_RB:C_RB + 1],
                                       scale=cst[:, C_65536:C_65536 + 1]), "sc")
                    R.scalar(s_ch)
                    sc_ch = C["sc"]
                    g_sc_rd[0] = [C["sc"]]

                    def v_ch(v, i=i, sub=sub, sb=sb, sc_ch=sc_ch):
                        v.wait_ge(S["sc"], sc_ch)
                        if i > 0 or sub > 1:
                            v.wait_ge(S["st"], g2_stl[sb][0])
                        I(v.tensor_scalar(out=sqc[:, :], in0=u4[sb][:, :],
                                          scalar1=DEB + 65535.0, scalar2=DEB,
                                          op0=ALU.min, op1=ALU.subtract), "vec")
                        VW(v)
                        I(v.scalar_tensor_tensor(out=g2b[sb][:, :],
                                                 in0=t3[sb][:, :],
                                                 scalar=DEB, in1=sqc[:, :],
                                                 op0=ALU.subtract, op1=ALU.mult),
                          "vec")
                        VW(v)
                        I(v.reduce_max(out=g4t[:, sub:sub + 1],
                                       in_=g2b[sb][:, :],
                                       axis=AX.X, apply_absolute_value=True),
                          "vec")
                    R.vector(v_ch)
                    sub_v.append(C["vec"])
                    g2_rd[sb] = [C["vec"]]

                    def g_g2(g, i=i, sub=sub, sb=sb, vv=sub_v[-1]):
                        g.wait_ge(S["vec"], vv)
                        I(g.dma_start(
                            out=g2_s[:, (i * NJC + sub * (NJC // 4)) * CHT:
                                     (i * NJC + (sub + 1) * (NJC // 4)) * CHT],
                            in_=g2b[sb][:, :]), "st", 16)
                    R.gpsimd(g_g2)
                    g2_stl[sb] = [C["st"]]

            sn_u = emit_up_or_gate(R, buf, False, 2, up_s, wuT, None,
                                   {"sc": f1a_done["sc"], "vec": f1a_done["vec"],
                                    "st": f1a_done["st"]}, hook=f2_hook)

            def v_fold24(v, lg=sn2[-1]["g2v"]):
                v.wait_ge(S["vec"], lg)
                VW(v)
                I(v.reduce_max(out=fold[:, 0:1], in_=accs[2][:, :], axis=AX.X),
                  "vec")
                I(v.reduce_max(out=fold[:, 1:2], in_=accs[4][:, :], axis=AX.X),
                  "vec")
            R.vector(v_fold24)
            v_f24 = C["vec"]

            def g_cc24(g, v_f24=v_f24):
                g.wait_ge(S["vec"], v_f24)
                I(g.dma_start(out=cc_in.ap(), in_=fold[:, :]), "st", 16)
                g.wait_ge(S["st"], C["st"])
                I(g.collective_compute(
                    "AllReduce", ALU.max, replica_groups=[list(range(B))],
                    ins=[cc_in.ap()], outs=[cc_out.ap()]), "cc")
                g.wait_ge(S["cc"], C["cc"])
                I(g.dma_start(out=brow[:, :],
                              in_=cc_out.ap().rearrange("p f -> (p f)")),
                  "st", 16)
            R.gpsimd(g_cc24)
            st_cc24 = C["st"]

            def v_s24(v, st_cc24=st_cc24):
                v.wait_ge(S["st"], st_cc24)
                VW(v)
                I(v.reduce_max(out=srow[:, 0:4],
                               in_=brow.ap().rearrange("o (p f) -> o f p", f=4),
                               axis=AX.X), "vec")
                VW(v)
                I(v.tensor_scalar(out=srow[:, 10:11], in0=srow[0:1, 4:5],
                                  scalar1=16.0, scalar2=None, op0=ALU.mult), "vec")
                VW(v)
                I(v.tensor_tensor(out=srow[:, 6:7], in0=srow[0:1, 0:1],
                                  in1=srow[0:1, 10:11], op=ALU.mult), "vec")
                VW(v)
                I(v.tensor_scalar(out=srow[:, 12:13], in0=srow[0:1, 6:7],
                                  scalar1=1.0 / 32767.0, scalar2=1e-12,
                                  op0=ALU.mult, op1=ALU.max), "vec")
                VW(v)
                I(v.reciprocal(out=srow[:, 9:10], in_=srow[0:1, 12:13]), "vec")
                VW(v)
                I(v.tensor_scalar(out=srow[:, 10:11], in0=srow[0:1, 4:5],
                                  scalar1=16.0, scalar2=None, op0=ALU.mult), "vec")
                VW(v)
                I(v.tensor_tensor(out=srow[:, 11:12], in0=srow[0:1, 10:11],
                                  in1=srow[0:1, 9:10], op=ALU.mult), "vec")
                VW(v)
                I(v.tensor_scalar(out=srow[:, 13:14], in0=cst[0:1, C_SC3:C_SC3 + 1],
                                  scalar1=1.0 / 65536.0, scalar2=None,
                                  op0=ALU.mult), "vec")
                VW(v)
                I(v.tensor_tensor(out=srow[:, 6:7], in0=srow[0:1, 1:2],
                                  in1=srow[0:1, 13:14], op=ALU.mult), "vec")
                VW(v)
                I(v.tensor_scalar(out=srow[:, 14:15], in0=srow[0:1, 6:7],
                                  scalar1=1.0 / 32767.0, scalar2=1e-12,
                                  op0=ALU.mult, op1=ALU.max), "vec")
                VW(v)
                I(v.reciprocal(out=srow[:, 9:10], in_=srow[0:1, 14:15]), "vec")
                VW(v)
                I(v.tensor_tensor(out=srow[:, 15:16], in0=srow[0:1, 13:14],
                                  in1=srow[0:1, 9:10], op=ALU.mult), "vec")
            R.vector(v_s24)
            v_s24c = C["vec"]

            def t_bc24(t, v_s24c=v_s24c, sc_all=C["sc"]):
                t.wait_ge(S["vec"], v_s24c)
                t.wait_ge(S["sc"], sc_all)
                I(t.matmul(psb2[:, 0:1], ones_m[:, :], srow[:, 11:12],
                           start=True, stop=True), "pe")
                I(t.matmul(psb2[:, 1:2], ones_m[:, :], srow[:, 15:16],
                           start=True, stop=True), "pe")
            R.tensor(t_bc24)
            pe_b24 = C["pe"]

            def s_cst24(s, pe_b24=pe_b24):
                s.wait_ge(S["pe"], pe_b24)
                I(s.copy(out=cst[:, C_F2:C_F2 + 1], in_=psb2[:, 0:1]), "sc")
                I(s.copy(out=cst[:, C_F4:C_F4 + 1], in_=psb2[:, 1:2]), "sc")
            R.scalar(s_cst24)
            f2_done = dict(C)
            R.flush(blk)
        stack_wu.close()

        with ExitStack() as pf3, nc.Block() as blk:
            R = Rec(blk)
            uld = [pf3.enter_context(nc.sbuf_tensor(f"uld{j}", [128, DI], FP16))
                   for j in range(2)]
            g2ld = [pf3.enter_context(nc.sbuf_tensor(f"g2l{j}", [128, DI], BF16))
                    for j in range(2)]
            t2 = [pf3.enter_context(nc.sbuf_tensor(f"t2_{j}", [128, DI], I32))
                  for j in range(2)]
            upq = [pf3.enter_context(nc.sbuf_tensor(f"upq{j}", [128, DI], F32))
                   for j in range(2)]
            t4 = [pf3.enter_context(nc.sbuf_tensor(f"t4_{j}", [128, DI], I32))
                  for j in range(2)]
            ob = pf3.enter_context(nc.sbuf_tensor("ob", [128, NJC * CHT], FP16))
            o5t = pf3.enter_context(nc.sbuf_tensor("o5t", [128, 4], F32))
            psb3 = pf3.enter_context(nc.psum_tensor("psb3", [128, 8], F32))

            sn3 = []
            for i in range(NCH):
                bi = i % 2
                s_now = {}

                sub_v = []
                for sub in range(4):
                    sl = slice(sub * DI, (sub + 1) * DI)
                    sb = sub % 2

                    def sy_u(sy, i=i, sub=sub, sb=sb):
                        if i > 0 or sub > 1:
                            sy.wait_ge(S["sc"], u_rd3[sb][0])
                        elif i == 0 and sub == 0:
                            sy.wait_ge(S["sc"], f2_done["sc"])
                            sy.wait_ge(S["st"], f2_done["st"])
                            sy.wait_ge(S["vec"], f2_done["vec"])
                        off = (i * NJC + sub * (NJC // 4)) * CHT
                        end = (i * NJC + (sub + 1) * (NJC // 4)) * CHT
                        I(sy.dma_start(out=uld[sb][:, :],
                                       in_=up_s[:, off:end]), "ld", 16)
                        I(sy.dma_start(out=g2ld[sb][:, :],
                                       in_=g2_s[:, off:end]), "ld", 16)
                    R.sync(sy_u)
                    ld_u = C["ld"]

                    def s_c2(s, i=i, sub=sub, sb=sb, sl=sl, ld_u=ld_u, bi=bi):
                        s.wait_ge(S["ld"], ld_u)
                        if i > 0 or sub > 1:
                            s.wait_ge(S["vec"], ob_rd[sb][0])
                        I(s.activation(out=t2[sb][:, :], in_=uld[sb][:, :],
                                       func=AF.Relu, bias=cst[:, C_RB:C_RB + 1],
                                       scale=cst[:, C_F2:C_F2 + 1]), "sc")
                        I(s.activation(out=t4[sb][:, :], in_=g2ld[sb][:, :],
                                       func=AF.Relu, bias=cst[:, C_RB:C_RB + 1],
                                       scale=cst[:, C_F4:C_F4 + 1]), "sc")
                        SW(s)
                        I(s.activation(out=upq[sb][:, :], in_=t2[sb][:, :],
                                       func=AF.Copy, bias=-0.5,
                                       scale=1.0 / 65536.0), "sc")
                    R.scalar(s_c2)
                    sc_c2 = C["sc"]
                    if i == 0 and sub == 0:
                        u_rd3 = {0: [0], 1: [0]}
                    u_rd3[sb] = [C["sc"]]

                    def v_c2(v, i=i, sub=sub, sb=sb, sl=sl, sc_c2=sc_c2):
                        v.wait_ge(S["sc"], sc_c2)
                        if sub == 0 and i >= 1:
                            v.wait_ge(S["st"], sn3[i - 1]["obst"])
                        I(v.scalar_tensor_tensor(out=ob[:, sl], in0=t4[sb][:, :],
                                                 scalar=DEB, in1=upq[sb][:, :],
                                                 op0=ALU.subtract, op1=ALU.mult),
                          "vec")
                        VW(v)
                        I(v.reduce_max(out=o5t[:, sub:sub + 1], in_=ob[:, sl],
                                       axis=AX.X, apply_absolute_value=True),
                          "vec")
                    R.vector(v_c2)
                    sub_v.append(C["vec"])
                    if i == 0 and sub < 2:
                        ob_rd = {0: [0], 1: [0]} if sub == 0 else ob_rd
                    ob_rd[sb] = [C["vec"]]
                s_now["rd"] = C["sc"]

                def v_f5(v, i=i, lv=sub_v[-1]):
                    v.wait_ge(S["vec"], lv)
                    VW(v)
                    I(v.reduce_max(out=accs[5][:, i:i + 1], in_=o5t[:, :],
                                   axis=AX.X), "vec")
                R.vector(v_f5)
                v_ob = C["vec"]

                def g_ob(g, i=i, v_ob=v_ob):
                    g.wait_ge(S["vec"], v_ob)
                    I(g.dma_start(out=o_s[:, i * NJC * CHT:(i + 1) * NJC * CHT],
                                  in_=ob[:, :]), "st", 16)
                R.gpsimd(g_ob)
                s_now["obst"] = C["st"]
                sn3.append(s_now)
                R.flush(blk)

            def v_fold5(v):
                VW(v)
                I(v.reduce_max(out=fold[:, 0:1], in_=accs[5][:, :], axis=AX.X),
                  "vec")
            R.vector(v_fold5)
            v_f5c = C["vec"]

            def g_cc5(g, v_f5c=v_f5c):
                g.wait_ge(S["vec"], v_f5c)
                I(g.dma_start(out=cc_in.ap(), in_=fold[:, :]), "st", 16)
                g.wait_ge(S["st"], C["st"])
                I(g.collective_compute(
                    "AllReduce", ALU.max, replica_groups=[list(range(B))],
                    ins=[cc_in.ap()], outs=[cc_out.ap()]), "cc")
                g.wait_ge(S["cc"], C["cc"])
                I(g.dma_start(out=brow[:, :],
                              in_=cc_out.ap().rearrange("p f -> (p f)")),
                  "st", 16)
            R.gpsimd(g_cc5)
            st_cc5 = C["st"]

            def v_s5(v, st_cc5=st_cc5):
                v.wait_ge(S["st"], st_cc5)
                VW(v)
                I(v.reduce_max(out=srow[:, 0:4],
                               in_=brow.ap().rearrange("o (p f) -> o f p", f=4),
                               axis=AX.X), "vec")
                VW(v)
                I(v.tensor_tensor(out=srow[:, 5:6], in0=srow[0:1, 12:13],
                                  in1=srow[0:1, 14:15], op=ALU.mult), "vec")
                VW(v)
                I(v.tensor_scalar(out=srow[:, 6:7], in0=srow[0:1, 5:6],
                                  scalar1=65536.0, scalar2=None,
                                  op0=ALU.mult), "vec")
                VW(v)
                I(v.tensor_tensor(out=srow[:, 7:8], in0=srow[0:1, 0:1],
                                  in1=srow[0:1, 6:7], op=ALU.mult), "vec")
                VW(v)
                I(v.tensor_scalar(out=srow[:, 8:9], in0=srow[0:1, 7:8],
                                  scalar1=1.0 / 32767.0, scalar2=1e-12,
                                  op0=ALU.mult, op1=ALU.max), "vec")
                VW(v)
                I(v.reciprocal(out=srow[:, 9:10], in_=srow[0:1, 8:9]), "vec")
                VW(v)
                I(v.tensor_tensor(out=srow[:, 10:11], in0=srow[0:1, 6:7],
                                  in1=srow[0:1, 9:10], op=ALU.mult), "vec")
            R.vector(v_s5)
            v_s5x = C["vec"]

            def t_bc5(t, v_s5x=v_s5x):
                t.wait_ge(S["vec"], v_s5x)
                I(t.matmul(psb3[:, 0:1], ones_m[:, :], srow[:, 10:11],
                           start=True, stop=True), "pe")
                I(t.matmul(psb3[:, 1:2], ones_m[:, :], srow[:, 8:9],
                           start=True, stop=True), "pe")
            R.tensor(t_bc5)
            pe_b5 = C["pe"]

            def s_cst5(s, pe_b5=pe_b5):
                s.wait_ge(S["pe"], pe_b5)
                I(s.copy(out=cst[:, C_F5:C_F5 + 1], in_=psb3[:, 0:1]), "sc")
                I(s.copy(out=cst[:, C_SC5:C_SC5 + 1], in_=psb3[:, 1:2]), "sc")
            R.scalar(s_cst5)
            f3_done = dict(C)
            R.flush(blk)

        stack_wd = ExitStack()
        wdT = stack_wd.enter_context(nc.sbuf_tensor("wdT", [128, NJC * D], BF16))
        with ExitStack() as pf4, nc.Block() as blk:
            R = Rec(blk)
            old = [pf4.enter_context(nc.sbuf_tensor(f"old{j}", [128, DI], FP16))
                   for j in range(2)]
            x1ld = [pf4.enter_context(nc.sbuf_tensor(f"x1l{j}", [128, NDC * CHT], F32))
                    for j in range(2)]
            t5 = pf4.enter_context(nc.sbuf_tensor("t5", [128, DI], I32))
            oqt = [pf4.enter_context(nc.sbuf_tensor(f"oqt{j}", [128, NJC * CHT], BF16))
                   for j in range(2)]
            outb = pf4.enter_context(nc.sbuf_tensor("outb", [128, NDC * CHT], F32))
            psD = [pf4.enter_context(nc.psum_tensor(f"psD{j}", [128, CHT], F32))
                   for j in range(NDC)]

            def sy_wd(sy):
                sy.wait_ge(S["sc"], f3_done["sc"])
                I(sy.dma_start(out=wdT[:, :], in_=wdownT_s[:, :]), "ld", 16)
            R.sync(sy_wd)
            ld_wd = C["ld"]

            sn4 = []
            for i in range(NCH):
                bi = i % 2
                s_now = {}

                def sy_x1(sy, i=i, bi=bi, ld_wd=ld_wd):
                    sy.wait_ge(S["ld"], ld_wd)
                    if i >= 2:
                        sy.wait_ge(S["pe"], sn4[i - 2]["mm"])
                        sy.wait_ge(S["vec"], sn4[i - 2]["res"])
                    else:
                        sy.wait_ge(S["sc"], f3_done["sc"])
                        sy.wait_ge(S["st"], f3_done["st"])
                        sy.wait_ge(S["vec"], f3_done["vec"])
                    I(sy.dma_start(
                        out=x1ld[bi][:, :],
                        in_=x1_s[:, i * NDC * CHT:(i + 1) * NDC * CHT]), "ld", 16)
                R.sync(sy_x1)

                sub_oq = []
                for sub in range(4):
                    sl = slice(sub * DI, (sub + 1) * DI)
                    ob_ = sub % 2

                    def sy_o(sy, i=i, sub=sub, ob_=ob_):
                        if i > 0 or sub > 1:
                            sy.wait_ge(S["sc"], o_rd4[ob_][0])
                        I(sy.dma_start(
                            out=old[ob_][:, :],
                            in_=o_s[:, (i * NJC + sub * (NJC // 4)) * CHT:
                                    (i * NJC + (sub + 1) * (NJC // 4)) * CHT]),
                          "ld", 16)
                    R.sync(sy_o)
                    ld_o = C["ld"]

                    def s_c5(s, sub=sub, sl=sl, ld_o=ld_o, ob_=ob_, i=i):
                        s.wait_ge(S["ld"], ld_o)
                        if sub >= 1:
                            s.wait_ge(S["vec"], sub_oq[sub - 1])
                        elif i >= 1:
                            s.wait_ge(S["vec"], t5_free[0])
                        if i >= 2 and sub == 0:
                            s.wait_ge(S["pe"], sn4[i - 2]["mm"])
                        I(s.activation(out=t5[:, :], in_=old[ob_][:, :],
                                       func=AF.Relu, bias=cst[:, C_RB:C_RB + 1],
                                       scale=cst[:, C_F5:C_F5 + 1]), "sc")
                    R.scalar(s_c5)
                    sc_c5 = C["sc"]
                    if i == 0 and sub == 0:
                        o_rd4 = {0: [0], 1: [0]}
                    o_rd4[ob_] = [C["sc"]]

                    def v_c5(v, sub=sub, sl=sl, sc_c5=sc_c5, bi=bi):
                        v.wait_ge(S["sc"], sc_c5)
                        I(v.tensor_scalar(out=oqt[bi][:, sl], in0=t5[:, :],
                                          scalar1=DEB, scalar2=None,
                                          op0=ALU.subtract), "vec")
                    R.vector(v_c5)
                    sub_oq.append(C["vec"])
                    v_sub = C["vec"]

                    NG6 = NJC // 4
                    if sub == 3:
                        pe3_base = C["pe"]

                    def t_d(t, sub=sub, v_sub=v_sub, i=i, bi=bi, NG6=NG6):
                        t.wait_ge(S["vec"], v_sub)
                        if sub == 0 and i >= 1:
                            t.wait_ge(S["vec"], sn4[i - 1]["res"])
                        for ht in range(NDC):
                            for jc in range(sub * NG6, (sub + 1) * NG6):
                                I(t.matmul(psD[ht][:, :],
                                           wdT[:, jc * D + ht * 128:
                                               jc * D + ht * 128 + 128],
                                           oqt[bi][:, jc * CHT:(jc + 1) * CHT],
                                           start=(sub == 0 and jc == 0),
                                           stop=(sub == 3 and jc == NJC - 1)),
                                  "pe")
                    R.tensor(t_d)
                t5_free = [C["vec"]]

                res_parts = []
                for ht in range(NDC):
                    def v_res(v, ht=ht, bi=bi, i=i,
                              pw=pe3_base + (ht + 1) * (NJC // 4)):
                        v.wait_ge(S["pe"], pw)
                        if ht == 0 and i >= 1:
                            v.wait_ge(S["st"], sn4[i - 1]["ost"])
                        I(v.scalar_tensor_tensor(
                            out=outb[:, ht * CHT:(ht + 1) * CHT],
                            in0=psD[ht][:, :],
                            scalar=cst[:, C_SC5:C_SC5 + 1],
                            in1=x1ld[bi][:, ht * CHT:(ht + 1) * CHT],
                            op0=ALU.mult, op1=ALU.add), "vec")
                    R.vector(v_res)
                    res_parts.append(C["vec"])
                s_now["mm"] = C["pe"]
                s_now["res"] = C["vec"]

                def g_out(g, i=i, rl=res_parts[-1]):
                    g.wait_ge(S["vec"], rl)
                    I(g.dma_start(out=outT[:, i * NDC * CHT:(i + 1) * NDC * CHT],
                                  in_=outb[:, :]), "st", 16)
                R.gpsimd(g_out)
                s_now["ost"] = C["st"]
                sn4.append(s_now)
                R.flush(blk)

            def sy_fin(sy):
                sy.wait_ge(S["st"], C["st"])
            R.sync(sy_fin)
            R.flush(blk)
        stack_wd.close()

    return nc


_NC_CACHE = [None]


def kernel(L0_attn_x, L0_attn_Wqkv, L0_attn_Wout, L0_attn_bout,
           L0_ffn_w_gate, L0_ffn_w_up, L0_ffn_w_down):
    x = np.ascontiguousarray(np.asarray(L0_attn_x, dtype=np.float32))
    wqkv = np.ascontiguousarray(np.asarray(L0_attn_Wqkv, dtype=np.float32))
    wo = np.ascontiguousarray(np.asarray(L0_attn_Wout, dtype=np.float32))
    bo = np.ascontiguousarray(np.asarray(L0_attn_bout, dtype=np.float32))
    wg = np.ascontiguousarray(np.asarray(L0_ffn_w_gate, dtype=np.float32))
    wu = np.ascontiguousarray(np.asarray(L0_ffn_w_up, dtype=np.float32))
    wd = np.ascontiguousarray(np.asarray(L0_ffn_w_down, dtype=np.float32))

    if _NC_CACHE[0] is None:
        _NC_CACHE[0] = build_nc()
    nc = _NC_CACHE[0]

    in_maps = []
    for c in range(B):
        xb = x[c].T.reshape(NDC, 128, NCH, CHT).transpose(1, 2, 0, 3)
        in_maps.append({
            "xT": np.ascontiguousarray(xb).reshape(128, NCH * NDC * CHT),
            "wqkv": wqkv, "wout": wo, "bout": bo,
            "wgate": wg, "wup": wu, "wdown": wd,
        })
    res = run_bass_kernel_spmd(nc, in_maps, core_ids=list(range(B)))
    out = np.empty((B, NTOK, D), dtype=np.float32)
    for c in range(B):
        ob = np.asarray(res.results[c]["outT"]).reshape(128, NCH, NDC, CHT)
        out[c] = ob.transpose(2, 0, 1, 3).reshape(D, NTOK).T
    return out
